# version 02
# baseline (speedup 1.0000x reference)
import numpy as np
import concourse.bass as bass
import concourse.mybir as mybir
from concourse.bass_utils import run_bass_kernel_spmd


import concourse.bass as bass
import concourse.mybir as mybir

F32 = mybir.dt.float32
BF16 = mybir.dt.bfloat16
AX = mybir.AxisListType.X
AL = mybir.AluOpType
AF = mybir.ActivationFunctionType

_SECTION = {"pe": "tensor", "dve": "vector", "act": "scalar",
            "pool": "gpsimd", "sp": "sync", "gp": "gpsimd", "actdma": "scalar"}
_QUEUES = ("sp", "gp", "actdma")
N_DMA_SEMS = 40


class Ev:
    __slots__ = ("key", "tgt", "blk", "dma")

    def __init__(self, key, tgt, blk, dma=False):
        self.key, self.tgt, self.blk, self.dma = key, tgt, blk, dma

    def __repr__(self):
        return f"Ev({self.key},{self.tgt},b{self.blk}{'D' if self.dma else ''})"


class Prog:
    def __init__(self, nc):
        self.nc = nc
        self.sem = {}
        self.cnt = {}
        for e in ("pe", "dve", "act", "pool"):
            self.sem[e] = nc.alloc_semaphore("S_" + e)
            self.cnt[e] = 0
        self._dma_pools = {"sp": [], "gp": [], "actdma": []}
        npool = {"sp": 24, "gp": 12, "actdma": 4}
        ii = 0
        for q, n in npool.items():
            for _ in range(n):
                k = f"d{ii}"
                ii += 1
                self.sem[k] = nc.alloc_semaphore("S_" + k)
                self.cnt[k] = 0
                self._dma_pools[q].append(k)
        self._dma_rr = {"sp": 0, "gp": 0, "actdma": 0}
        self.blk_idx = 0
        self._recs = {}
        self._last_wait = {}
        self._cmap = {}

    def setup_consts(self, values):
        vals = sorted(set(float(v) for v in values))
        if not vals:
            return
        self._ctile = self.nc.alloc_sbuf_tensor(
            "constcols", [128, len(vals)], F32).ap()
        for i, v in enumerate(vals):
            col = self._ctile[:, i:i + 1]
            self.op("pool", lambda e, c=col, vv=v: e.memset(c, vv))
            self._cmap[v] = col
            try:
                self.nc.const_aps.register(F32, v, col)
            except Exception:
                pass

    def ccol(self, v):
        return self._cmap[float(v)]

    def _rec(self, eng_name, fn):
        self._recs.setdefault(_SECTION[eng_name], []).append(fn)

    def _emit_waits(self, eng_name, waits):
        sec = _SECTION[eng_name]
        for ev in (waits or []):
            if ev is None:
                continue
            if ev.blk < self.blk_idx and not ev.dma:
                continue
            key = (sec, ev.key)
            if self._last_wait.get(key, -1) >= ev.tgt:
                continue
            self._last_wait[key] = ev.tgt
            sem, tgt = self.sem[ev.key], ev.tgt
            self._rec(eng_name, lambda e, s=sem, t=tgt: e.wait_ge(s, t))

    def op(self, eng_name, fn, wait=None, inc=True):
        self_serial = eng_name in ("dve", "act", "pool")
        wl = list(wait) if wait else []
        if self_serial and self.cnt[eng_name] > 0:
            wl.append(Ev(eng_name, self.cnt[eng_name], self.blk_idx))
        self._emit_waits(eng_name, wl)
        ev = None
        if inc or self_serial:
            self.cnt[eng_name] += 1
            sem, tgt = self.sem[eng_name], self.cnt[eng_name]
            ev = Ev(eng_name, tgt, self.blk_idx)
            self._rec(eng_name, lambda e, s=sem, f=fn: f(e).then_inc(s, 1))
        else:
            self._rec(eng_name, lambda e, f=fn: f(e))
        return ev

    def raw(self, eng_name, fn):
        self._rec(eng_name, fn)

    def dma_multi(self, q, transfers, wait=None, **kw):
        assert q in _QUEUES
        self._emit_waits(q, wait)
        pool = self._dma_pools[q]
        k = pool[self._dma_rr[q]]
        self._dma_rr[q] = (self._dma_rr[q] + 1) % len(pool)
        sem = self.sem[k]
        base = self.cnt[k]
        if base > 0:
            key = (_SECTION[q], k)
            if self._last_wait.get(key, -1) < base:
                self._last_wait[key] = base
                self._rec(q, lambda e, s=sem, t=base: e.wait_ge(s, t))
        n = len(transfers)
        for t in transfers:
            out, in_ = t[0], t[1]
            tkw = dict(kw)
            if len(t) > 2:
                tkw.update(t[2])
            self._rec(q, lambda e, o=out, i=in_, kww=tkw, s=sem:
                      e.dma_start(out=o, in_=i, **kww).then_inc(s, 16))
        self.cnt[k] = base + 16 * n
        return Ev(k, self.cnt[k], self.blk_idx, dma=True)

    def dma(self, q, out, in_, wait=None, **kw):
        return self.dma_multi(q, [(out, in_)], wait=wait, **kw)

    def dmaT_multi(self, transfers, wait=None):
        self._emit_waits("sp", wait)
        pool = self._dma_pools["sp"]
        k = pool[self._dma_rr["sp"]]
        self._dma_rr["sp"] = (self._dma_rr["sp"] + 1) % len(pool)
        sem = self.sem[k]
        base = self.cnt[k]
        if base > 0:
            key = (_SECTION["sp"], k)
            if self._last_wait.get(key, -1) < base:
                self._last_wait[key] = base
                self._rec("sp", lambda e, s=sem, t=base: e.wait_ge(s, t))
        for (out, in_) in transfers:
            self._rec("sp", lambda e, o=out, i=in_, s=sem:
                      e.dma_start_transpose(o, i).then_inc(s, 16))
        self.cnt[k] = base + 16 * len(transfers)
        return Ev(k, self.cnt[k], self.blk_idx, dma=True)

    def dmaT(self, out, in_, wait=None):
        return self.dmaT_multi([(out, in_)], wait=wait)

    def commit(self, name=None):
        if not self._recs:
            return
        recs, self._recs = self._recs, {}
        with self.nc.Block(name) as b:
            for sec, fns in recs.items():
                def body(eng, fns=fns):
                    for f in fns:
                        f(eng)
                getattr(b, sec)(body)
        self.blk_idx += 1
        self._last_wait = {}

    def pe(self, fn, wait=None, inc=False):
        return self.op("pe", fn, wait=wait, inc=inc)

    def dve(self, fn, wait=None, inc=False):
        return self.op("dve", fn, wait=wait, inc=inc)

    def act(self, fn, wait=None, inc=False):
        return self.op("act", fn, wait=wait, inc=inc)

    def pool(self, fn, wait=None, inc=False):
        return self.op("pool", fn, wait=wait, inc=inc)




E = 1024
T = 2048
NSEQ = 2
CH = 8
NTT = 4
NTC = 16
NEG = -1e30
EPS_LN = 1e-5
SHIFT0 = 12.0
SHIFT1 = 30.0
HID = 2730

CONST_SCALARS = [1.0 / E, -1.0, EPS_LN, 1e-6, -0.5, 1e-24, 8.0, 0.125,
                 -SHIFT0, -SHIFT1, 1e-30, 0.5, 1.0 / 4096.0, 1.0 / 256.0]


def host_consts():
    c = {}
    c["C_EYEF"] = np.eye(128, dtype=np.float32)
    p = np.arange(128)[:, None]
    f = np.arange(512)[None, :]
    m = np.zeros((4, 128, 512), np.float32)
    for j in range(4):
        m[j] = np.where(p <= f - 128 * j, 0.0, NEG)
    c["C_MASK0"] = m
    m1 = np.zeros((8, 128, 512), np.float32)
    for jc in range(8):
        keep = (f <= 128 * jc + p) if jc < 4 else (128 * (jc - 4) + p <= f)
        m1[jc] = np.where(keep, 0.0, NEG)
    c["C_MASK1"] = m1
    d = np.arange(128) % 64
    freq = 10000.0 ** (-2.0 * (d // 2) / 64)
    t = np.arange(T)
    ang = np.maximum(t[None, :] - 1, 0) * freq[:, None]
    Cc = np.cos(ang)
    Ss = np.sin(ang)
    sign = np.where(d % 2 == 0, -1.0, 1.0)[:, None]
    Cc[:, 0] = 1.0
    Ss[:, 0] = 0.0
    c["C_ROPE"] = np.stack([Cc, sign * Ss]).astype(np.float32)
    fi = d % 32
    inv_freq = 1.0 / 10000.0 ** (2.0 * fi / 64)
    base = (2.0 * fi + 0.4 * 64) / (1.4 * 64)
    fr = t[None, :] * inv_freq[:, None]
    scale = base[:, None] ** ((t - T // 2) / 256.0)[None, :]
    cos, sin = np.cos(fr), np.sin(fr)
    sgn = np.where(d < 32, -1.0, 1.0)[:, None]
    c["C_XPOS"] = np.stack([cos * scale, sgn * sin * scale,
                            cos / scale, sgn * sin / scale]).astype(np.float32)
    H = np.array([[1.0]])
    while H.shape[0] < 64:
        H = np.block([[H, H], [H, -H]])
    c["C_HAD"] = H.astype(np.float32)
    P32 = np.zeros((128, 128), np.float32)
    for p in range(128):
        blk, off = p // 64, p % 64
        P32[blk * 64 + (off + 32) % 64, p] = 1.0
    c["C_P32"] = P32
    return c


class K:
    def __init__(self, dbg=()):
        self.dbg = set(dbg)
        nc = bass.Bass("TRN2", target_bir_lowering=False)
        self.nc = nc
        self.P = Prog(nc)
        self.out_evs = []
        self.dbg_names = []
        self._io()
        self._sbuf()

    def _io(self):
        nc = self.nc
        ap = lambda n, s, dt=F32, kind="ExternalInput": nc.dram_tensor(
            n, s, dt, kind=kind).ap()
        i = self.i = {}
        i["x"] = ap("L0_attn_x", [NSEQ, T, E])
        for n, s in [("L0_attn_Wq", [E, E]), ("L0_attn_Wk", [E, E]),
                     ("L0_attn_Wv", [E, E]), ("L0_attn_Wo", [E, E]),
                     ("L0_attn_gamma", [E]), ("L0_attn_beta", [E]),
                     ("L0_ffn_router_w", [E, 8]), ("L0_ffn_router_b", [8]),
                     ("L0_ffn_expert_w", [8, E, E]), ("L0_ffn_expert_b", [8, E]),
                     ("L1_attn_w_qkv", [3 * E, E]), ("L1_attn_w_out", [E, E]),
                     ("L1_attn_sorf_signs", [16, 4, 3, 64]),
                     ("L1_attn_sorf_scales", [16, 4, 64]),
                     ("L1_ffn_wqkv", [2, E, 1536]), ("L1_ffn_q_scale", [2, 64]),
                     ("L1_ffn_k_scale", [2, 64]), ("L1_ffn_wo", [2, 512, E]),
                     ("L1_ffn_ln_g", [2, E]), ("L1_ffn_ln_b", [2, E]),
                     ("L1_ffn_w1", [2, E, 5460]), ("L1_ffn_b1", [2, 5460]),
                     ("L1_ffn_w2", [2, HID, E]), ("L1_ffn_b2", [2, E]),
                     ("C_EYEF", [128, 128]), ("C_MASK0", [4, 128, 512]),
                     ("C_MASK1", [8, 128, 512]), ("C_ROPE", [2, 128, T]),
                     ("C_XPOS", [4, 128, T]), ("C_HAD", [64, 64]),
                     ("C_P32", [128, 128])]:
            i[n] = ap(n, s)
        self.o_out = ap("OUT", [NSEQ, T, E], F32, kind="ExternalOutput")
        sc = lambda n, s, dt=BF16: nc.dram_tensor(n, s, dt, kind="Internal").ap()
        d = self.d = {}
        d["xb16"] = sc("xb16", [NSEQ, T, E])
        for n in ("wq_b", "wk_b", "wv_b", "wo_b", "wout1_b"):
            d[n] = sc(n, [E, E])
        d["wqkv1_b"] = sc("wqkv1_b", [3 * E, E])
        for n, s in [("wqT", [E, E]), ("wkT", [E, E]), ("wvT", [E, E]),
                     ("woT", [E, E]), ("wqkv1T", [E, 3 * E]),
                     ("wout1T", [E, E])]:
            d[n] = sc(n, s)
        d["wqkv_l"] = sc("wqkv_l", [2, E, 1536])
        d["wo_l"] = sc("wo_l", [2, 512, E])
        d["w1_l"] = sc("w1_l", [2, E, 5460])
        d["w2_l"] = sc("w2_l", [2, HID, E])
        d["ew"] = sc("ew", [8, E, E])
        d["disp"] = sc("disp", [8, T])
        d["gatec"] = sc("gatec", [1, T], F32)
        d["g0"] = sc("g0", [CH, 128, T])
        d["s1"] = sc("s1", [CH, 128, T])
        d["qs"] = sc("qs", [CH, 128, T])
        d["ks"] = sc("ks", [CH, 128, T])
        d["vs"] = sc("vs", [16, NTC, 128, 65])
        d["q1"] = sc("q1", [CH, 128, T])
        d["k1"] = sc("k1", [CH, 128, T])
        d["v1"] = sc("v1", [16, NTC, 128, 65])
        d["ybk"] = sc("ybk", [4, 128, T])
        d["x1T"] = sc("x1T", [NSEQ, T, E], F32)

    def dbg_dram(self, name, shape, dt=F32):
        a = self.nc.dram_tensor("DBG_" + name, shape, dt,
                                kind="ExternalOutput").ap()
        self.dbg_names.append(name)
        return a

    def _sbuf(self):
        nc = self.nc
        A = lambda n, s, dt=BF16, **kw: nc.alloc_sbuf_tensor(n, s, dt, **kw).ap()
        self.U0 = A("U0", [128, CH, T])
        self.U4 = A("U4", [128, 16384], is_transpose_dest=True)
        self.U5 = A("U5", [128, 16384], is_transpose_dest=True)
        self.U6 = A("U6", [128, 8192], is_transpose_dest=True)
        self.U7 = A("U7", [128, 4096])
        self.HA = A("HA", [128, T])
        self.HB = A("HB", [128, T])
        self.HC = A("HC", [128, T])
        self.HD = A("HD", [128, 2, 16, 65])
        self.rf = A("rf", [1, 6144], F32)
        self.rb = A("rb", [1, 8192])
        self.eyef = A("eyef", [128, 128], F32)
        self.eyeb = A("eyeb", [128, 128])
        self.ones_r = A("ones_r", [1, 128])
        self.ones_rf = A("ones_rf", [1, 128], F32)
        self.ones_c = A("ones_c", [128, 1])
        self.ones512 = A("ones512", [1, 512])
        self.gam = A("gam", [128, 8], F32)
        self.rwt = A("rwt", [128, 8, 8])
        self.rbias = A("rbias", [8, 1], F32)
        self.ebias = A("ebias", [8, 1024])
        self.MT = A("MT", [128, 16, 256])
        self.lgam = A("lgam", [128, 2, 8], F32)
        self.lbet = A("lbet", [128, 2, 8], F32)
        self.b1c = A("b1c", [128, 2, 44], F32)
        self.b2c = A("b2c", [128, 2, 8], F32)
        self.qksc = A("qksc", [128, 2], F32)
        self.ltT8 = A("ltT8", [128, 8], F32)
        self.oh8 = A("oh8", [128, 8])
        self.t8 = A("t8", [8, 128])
        self.g1x = A("g1x", [1, 128], F32)
        self.mx8 = A("mx8", [128, 1], F32)
        self.nmx8 = A("nmx8", [128, 1], F32)
        self.et8 = A("et8", [128, 8], F32)
        self.sm8 = A("sm8", [128, 1], F32)
        self.gt8 = A("gt8", [128, 1], F32)
        self.mkb = A("mkb", [128, 512])
        self.gb = A("gb", [128, 512], F32)
        self.sb512 = A("sb512", [128, 512], F32)
        self.sb512b = A("sb512b", [128, 512], F32)
        self.nq = A("nq", [128, 16], F32)
        self.nk = A("nk", [128, 16], F32)
        self.colA = A("colA", [128, 1], F32)
        self.colB = A("colB", [128, 1], F32)
        self.colC = A("colC", [128, 1], F32)
        self.p32 = A("p32", [128, 128])
        self.qs2 = A("qs2", [128, 2], F32)
        self.ks2 = A("ks2", [128, 2], F32)
        self.fint = A("fint", [1, 4], F32)
        self.pb = [nc.alloc_psum_tensor(f"pb{j}", [128, 512], F32).ap()
                   for j in range(8)]
        self.u5f = self.U5.bitcast(F32)
        self.u4f = self.U4.bitcast(F32)

    def P_last(self, eng):
        return Ev(eng, self.P.cnt[eng], self.P.blk_idx)

    def prelude(self):
        P = self.P
        i, d = self.i, self.d
        P.setup_consts(CONST_SCALARS)
        for t_ in (self.ones_r, self.ones_rf, self.ones_c, self.ones512):
            P.pool(lambda e, t=t_: e.memset(t, 1.0))
        P.pool(lambda e: e.memset(self.HD[:, :, :, 64:65], 1.0))
        self.ev_xcast = P.dma("gp", d["xb16"], i["x"])
        ce = self.cast_evs = {}
        for src, dst in [("L0_attn_Wq", "wq_b"), ("L0_attn_Wk", "wk_b"),
                         ("L0_attn_Wv", "wv_b"), ("L0_attn_Wo", "wo_b"),
                         ("L1_attn_w_qkv", "wqkv1_b"),
                         ("L1_attn_w_out", "wout1_b"),
                         ("L0_ffn_expert_w", "ew"), ("L1_ffn_wqkv", "wqkv_l"),
                         ("L1_ffn_wo", "wo_l"), ("L1_ffn_w1", "w1_l"),
                         ("L1_ffn_w2", "w2_l")]:
            ce[dst] = P.dma("gp", d[dst], i[src])
        self.ev_eye = P.dma("sp", self.eyef, i["C_EYEF"])
        self.ev_m0 = P.dma(
            "gp", self.U7[:, 0:2048].rearrange("p (j f) -> p j f", f=512),
            i["C_MASK0"].rearrange("j p f -> p j f"))
        self.ev_eyeb = P.dma("gp", self.eyeb, i["C_EYEF"])
        self.ev_p32 = P.dma("gp", self.p32, i["C_P32"])
        slow = {"allow_slow_non_contiguous": True}
        self.ev_gam = P.dma("sp", self.gam,
                            i["L0_attn_gamma"].rearrange("(c p) -> p c", p=128),
                            **slow)
        self.gam16 = self.nc.alloc_sbuf_tensor("gam16", [128, 8], BF16).ap()
        self.bet16 = self.nc.alloc_sbuf_tensor("bet16", [128, 8], BF16).ap()
        self.ev_gam16 = P.dma("gp", self.gam16,
                              i["L0_attn_gamma"].rearrange("(c p) -> p c", p=128),
                              **slow)
        self.ev_bet16 = P.dma("gp", self.bet16,
                              i["L0_attn_beta"].rearrange("(c p) -> p c", p=128),
                              **slow)
        self.ev_rw = P.dma("gp", self.rwt,
                           i["L0_ffn_router_w"].rearrange("(c p) e -> p c e",
                                                          p=128))
        self.ev_rb = P.dma("sp", self.rbias,
                           i["L0_ffn_router_b"].unsqueeze(1))
        self.ev_eb = P.dma("gp", self.ebias, i["L0_ffn_expert_b"])
        for l in range(2):
            self.ev_lgam = P.dma("sp", self.lgam[:, l, :],
                                 i["L1_ffn_ln_g"][l].rearrange(
                                     "(c p) -> p c", p=128), **slow)
            self.ev_lbet = P.dma("sp", self.lbet[:, l, :],
                                 i["L1_ffn_ln_b"][l].rearrange(
                                     "(c p) -> p c", p=128), **slow)
            self.ev_b1a = P.dma("sp", self.b1c[:, l, 0:21],
                                i["L1_ffn_b1"][l][0:2688].rearrange(
                                    "(c p) -> p c", p=128), **slow)
            P.dma("sp", self.b1c[0:42, l, 21:22],
                  i["L1_ffn_b1"][l][2688:2730].rearrange(
                      "(c p) -> p c", p=42), **slow)
            P.dma("sp", self.b1c[:, l, 22:43],
                  i["L1_ffn_b1"][l][2730:5418].rearrange(
                      "(c p) -> p c", p=128), **slow)
            self.ev_b1b = P.dma("sp", self.b1c[0:42, l, 43:44],
                                i["L1_ffn_b1"][l][5418:5460].rearrange(
                                    "(c p) -> p c", p=42), **slow)
            self.ev_b2c = P.dma("sp", self.b2c[:, l, :],
                                i["L1_ffn_b2"][l].rearrange(
                                    "(c p) -> p c", p=128), **slow)
            self.ev_qks = P.dma("sp", self.qs2[0:64, l:l + 1],
                                i["L1_ffn_q_scale"][l].unsqueeze(1), **slow)
            P.dma("sp", self.qs2[64:128, l:l + 1],
                  i["L1_ffn_q_scale"][l].unsqueeze(1), **slow)
            P.dma("sp", self.ks2[0:64, l:l + 1],
                  i["L1_ffn_k_scale"][l].unsqueeze(1), **slow)
            self.ev_kks = P.dma("sp", self.ks2[64:128, l:l + 1],
                                i["L1_ffn_k_scale"][l].unsqueeze(1), **slow)
        P.commit("pre0")
        for (src, dst) in [("wq_b", "wqT"), ("wk_b", "wkT"), ("wv_b", "wvT"),
                           ("wo_b", "woT"), ("wout1_b", "wout1T"),
                           ("wqkv1_b", "wqkv1T")]:
            self._pretranspose(d[src], d[dst], ce[src])
        self._build_MT()
        P.commit("pre1")

    def _pretranspose(self, src, dst, ev):
        P = self.P
        R, C = src.shape[0], src.shape[1]
        prev = [getattr(self, "_pt_prev0", None), getattr(self, "_pt_prev1", None)]
        for c in range(C // 128):
            buf = c % 2
            st = self.U6[:, buf * 4096:buf * 4096 + R]
            w = [ev] + ([prev[buf]] if prev[buf] else [])
            e1 = P.dmaT(st, src[:, c * 128:(c + 1) * 128], wait=w)
            prev[buf] = P.dma("sp", dst[c * 128:(c + 1) * 128, :], st, wait=[e1])
        self._pt_prev0, self._pt_prev1 = prev[0], prev[1]
        self.pret_ev = prev[(C // 128 - 1) % 2]

    def _build_MT(self):
        P = self.P
        i = self.i
        nc = self.nc
        had = nc.alloc_sbuf_tensor("had", [64, 64], BF16).ap()
        e_h = P.dma("gp", had, i["C_HAD"])
        sg = nc.alloc_sbuf_tensor("sg_", [64, 16, 4, 3], F32).ap()
        e_s = P.dma("sp", sg, i["L1_attn_sorf_signs"].rearrange(
            "h r k p -> p h r k"), allow_slow_non_contiguous=True)
        scl = nc.alloc_sbuf_tensor("scl", [64, 16, 4], F32).ap()
        e_c = P.dma("sp", scl, i["L1_attn_sorf_scales"].rearrange(
            "h r p -> p h r"), allow_slow_non_contiguous=True)
        eye64 = nc.alloc_sbuf_tensor("eye64", [64, 64], F32).ap()
        e_e = P.dma("sp", eye64, i["C_EYEF"][0:64, 0:64])
        tmp = nc.alloc_sbuf_tensor("mtmp", [64, 64], BF16).ap()
        pb = self.pb
        prev_m3 = None
        for h in range(16):
            for r in range(4):
                w0 = [e_e, e_c, e_h, e_s] + ([prev_m3] if prev_m3 else [])
                e0 = P.dve(lambda e, h=h, r=r: e.tensor_scalar(
                    tmp, eye64, scl[:, h:h + 1, r], 1.0 / 4096.0,
                    AL.mult, AL.mult), wait=w0)
                m1 = P.pe(lambda e: e.matmul(pb[6][0:64, 0:64], had, tmp,
                                             start=True, stop=True),
                          wait=[e0], inc=True)
                e1 = P.dve(lambda e, h=h, r=r: e.tensor_scalar_mul(
                    tmp, pb[6][0:64, 0:64], sg[:, h:h + 1, r, 2]), wait=[m1])
                m2 = P.pe(lambda e: e.matmul(pb[7][0:64, 0:64], had, tmp,
                                             start=True, stop=True),
                          wait=[e1], inc=True)
                e2 = P.dve(lambda e, h=h, r=r: e.tensor_scalar_mul(
                    tmp, pb[7][0:64, 0:64], sg[:, h:h + 1, r, 1]), wait=[m2])
                m3 = P.pe(lambda e: e.matmul(pb[6][0:64, 0:64], had, tmp,
                                             start=True, stop=True),
                          wait=[e2], inc=True)
                prev_m3 = m3
                P.dve(lambda e, h=h, r=r: e.tensor_scalar_mul(
                    self.MT[0:64, h, r * 64:(r + 1) * 64],
                    pb[6][0:64, 0:64], sg[:, h:h + 1, r, 0]), wait=[m3])
                self.ev_MT = P.dve(lambda e, h=h, r=r: e.tensor_copy(
                    self.MT[64:128, h, r * 64:(r + 1) * 64],
                    self.MT[0:64, h, r * 64:(r + 1) * 64]))

    def ln_stats(self, xF, wait_evs):
        P = self.P
        pbS, pbQ = self.pb[6], self.pb[7]
        sq = self.U5.rearrange("p (c f) -> p c f", f=512)
        r = lambda o: self.rf[0:1, o:o + 2048]
        rb = lambda o: self.rb[0:1, o:o + 2048]
        evS = evQ = None
        prev_q = None
        for tt in range(NTT):
            sqe = []
            for c in range(CH):
                w = list(wait_evs) if tt == 0 else [prev_q]
                sqe.append(P.pool(lambda e, c=c, tt=tt: e.tensor_tensor(
                    sq[:, c, :], xF[:, c, tt * 512:(tt + 1) * 512],
                    xF[:, c, tt * 512:(tt + 1) * 512], AL.mult), wait=w))
            mS = mQ = None
            for c in range(CH):
                st = (c == 0)
                wS = (list(wait_evs) if tt == 0 else []) + \
                    ([evS] if (st and evS) else [])
                mS = P.pe(lambda e, c=c, tt=tt, st=st: e.matmul(
                    pbS[0:1, :], self.ones_c,
                    xF[:, c, tt * 512:(tt + 1) * 512],
                    start=st, stop=(c == CH - 1)), wait=wS or None,
                    inc=(c == CH - 1))
                wQ = [sqe[c]] + ([evQ] if (st and evQ) else [])
                mQ = P.pe(lambda e, c=c, tt=tt, st=st: e.matmul(
                    pbQ[0:1, :], self.ones_c, sq[:, c, :],
                    start=st, stop=(c == CH - 1)), wait=wQ,
                    inc=(c == CH - 1))
            prev_q = mQ
            evS = P.dve(lambda e, tt=tt: e.tensor_copy(
                self.rf[0:1, tt * 512:(tt + 1) * 512], pbS[0:1, :]), wait=[mS])
            evQ = P.dve(lambda e, tt=tt: e.tensor_copy(
                self.rf[0:1, 2048 + tt * 512:2048 + (tt + 1) * 512],
                pbQ[0:1, :]), wait=[mQ])
        P.act(lambda e: e.activation(rb(4096), r(0), AF.Copy,
                                     bias=0.0, scale=1.0 / E), wait=[evS])
        ea = P.act(lambda e: e.activation(r(4096), r(0), AF.Copy,
                                          bias=0.0, scale=1.0 / E))
        e3 = P.pool(lambda e: e.tensor_tensor(r(0), r(4096), r(4096), AL.mult),
                    wait=[ea])
        e4 = P.dve(lambda e: e.scalar_tensor_tensor(
            r(2048), r(2048), 1.0 / E, r(0), AL.mult, AL.subtract),
            wait=[e3, evQ])
        e5 = P.act(lambda e: e.activation(r(0), r(2048), AF.Ln,
                                          bias=EPS_LN, scale=1.0), wait=[e4])
        e6 = P.act(lambda e: e.activation(r(2048), r(0), AF.Exp,
                                          bias=0.0, scale=-0.5), wait=[e5])
        e7 = P.dve(lambda e: e.scalar_tensor_tensor(
            rb(2048), r(4096), -1.0, r(2048), AL.mult, AL.mult), wait=[e6])
        P.dve(lambda e: e.tensor_copy(r(4096), r(2048)), wait=[e7])
        e9 = P.dve(lambda e: e.tensor_copy(rb(0), r(2048)))
        return e9

    def bcast2(self, rowA, rowB, pbA, pbB, wait=None):
        P = self.P
        mA = P.pe(lambda e: e.matmul(pbA, self.ones_r, rowA,
                                     start=True, stop=True), wait=wait,
                  inc=True)
        mB = P.pe(lambda e: e.matmul(pbB, self.ones_r, rowB,
                                     start=True, stop=True), inc=True)
        return mA, mB

    def s_stats_xn(self, s):
        P = self.P
        d = self.d
        xbF = self.U4.rearrange("p (c f) -> p c f", f=T)
        te = P.dmaT_multi(
            [(xbF[:, c, :], d["xb16"][s][:, c * 128:(c + 1) * 128])
             for c in range(CH)], wait=[self.ev_xcast])
        tes = [te]
        ev = self.ln_stats(xbF, [te])
        xn = self.U0
        pbA, pbB = self.pb[4], self.pb[5]
        ee = None
        prevA = prevB = None
        for tt in range(NTT):
            w = [ev] + ([prevA] if prevA else []) + ([prevB] if prevB else [])
            mA, mB = self.bcast2(
                self.rb[0:1, tt * 512:(tt + 1) * 512],
                self.rb[0:1, 2048 + tt * 512:2048 + (tt + 1) * 512],
                pbA, pbB, wait=w)
            eB = P.dve(lambda e: e.tensor_copy(self.sb512, pbB), wait=[mB])
            for c in range(CH):
                t1 = P.dve(lambda e, c=c, tt=tt: e.tensor_tensor(
                    self.sb512b, xbF[:, c, tt * 512:(tt + 1) * 512], pbA,
                    AL.mult), wait=[mA, eB])
                ee = P.dve(lambda e, c=c, tt=tt: e.tensor_tensor(
                    xn[:, c, tt * 512:(tt + 1) * 512], self.sb512b,
                    self.sb512, AL.add))
            prevA, prevB = ee, ee
        if "xn" in self.dbg:
            o = self.dbg_dram(f"xn{s}", [CH, 128, T])
            for c in range(CH):
                self.out_evs.append(P.dma("gp", o[c], xn[:, c, :], wait=[ee]))
        P.commit(f"xn{s}")
        return ee

    def s_moe(self, s, ev_xn):
        P = self.P
        d = self.d
        xn = self.U0
        pb = self.pb
        lt = self.u5f[0:8, 2048:4096]
        ev_lt = None
        for tt in range(NTT):
            m = None
            for ic in range(CH):
                m = P.pe(lambda e, ic=ic, tt=tt: e.matmul(
                    pb[0][0:8, :], self.rwt[:, ic, :],
                    xn[:, ic, tt * 512:(tt + 1) * 512],
                    start=(ic == 0), stop=(ic == CH - 1)),
                    wait=([ev_xn, self.ev_rw, ev_lt] if ic == 0 else None),
                    inc=(ic == CH - 1))
            ev_lt = P.dve(lambda e, tt=tt: e.tensor_scalar_add(
                lt[:, tt * 512:(tt + 1) * 512], pb[0][0:8, :], self.rbias),
                wait=[m, self.ev_rb])
        evd = []
        prev = [None, None, None]
        for tc in range(NTC):
            mt = P.pe(lambda e, tc=tc: e.transpose(
                pb[1][:, 0:8], lt[0:8, tc * 128:(tc + 1) * 128],
                self.eyef[0:8, 0:8]),
                wait=[ev_lt, self.ev_eye] + ([prev[0]] if prev[0] else []),
                inc=True)
            e1 = P.dve(lambda e: e.tensor_copy(self.ltT8, pb[1][:, 0:8]),
                       wait=[mt])
            prev[0] = e1
            P.dve(lambda e: e.reduce_max(self.mx8, self.ltT8, axis=AX))
            e3 = P.dve(lambda e: e.tensor_scalar_mul(self.nmx8, self.mx8, -1.0))
            e4 = P.act(lambda e: e.activation(self.et8, self.ltT8, AF.Exp,
                                              bias=self.nmx8, scale=1.0),
                       wait=[e3])
            P.dve(lambda e: e.reduce_sum(self.sm8, self.et8, axis=AX),
                  wait=[e4])
            e6 = P.dve(lambda e: e.reciprocal(self.gt8, self.sm8))
            e7 = P.dve(lambda e: e.tensor_single_scalar(
                self.oh8, self.ltT8, self.mx8, AL.is_ge))
            mo = P.pe(lambda e: e.transpose(
                pb[2].bitcast(BF16)[0:8, 0:128], self.oh8, self.eyeb),
                wait=[e7, self.ev_eyeb] + ([prev[1]] if prev[1] else []),
                inc=True)
            mg = P.pe(lambda e: e.transpose(
                pb[3][0:1, 0:128], self.gt8, self.eyef),
                wait=[e6] + ([prev[2]] if prev[2] else []), inc=True)
            e8 = P.dve(lambda e: e.tensor_copy(
                self.t8, pb[2].bitcast(BF16)[0:8, 0:128]),
                wait=[mo] + evd[-2:])
            prev[1] = e8
            e9 = P.dve(lambda e: e.tensor_copy(self.g1x, pb[3][0:1, 0:128]),
                       wait=[mg])
            prev[2] = e9
            evd.append(P.dma("sp", d["disp"][:, tc * 128:(tc + 1) * 128],
                             self.t8, wait=[e8]))
            evd.append(P.dma("sp", d["gatec"][0:1, tc * 128:(tc + 1) * 128],
                             self.g1x, wait=[e9]))
        P.commit(f"rt{s}")
        ev_gr = P.dma("sp", self.rf[0:1, 0:2048], d["gatec"], wait=evd)
        disp8 = self.U7[0:8, 2048:4096]
        ev_d8 = P.dma("sp", disp8, d["disp"], wait=evd)
        xm = self.U5.rearrange("p (c f) -> p c f", f=512)
        mkb8 = self.U5[:, 8192:12288].rearrange("p (e f) -> p e f", f=512)
        dtt = self.U5[0:1, 12288:16384].rearrange("o (e f) -> o e f", f=512)
        g0t = self.U5[:, 12800:13312]
        ew = self.U4.rearrange("p (b c f) -> p b c f", b=2, f=1024)
        ev_ew = [None, None]
        ev_free = [None, None]
        last_evac = None
        for tt in range(NTT):
            e_dt = P.dma("sp", dtt,
                         d["disp"][:, tt * 512:(tt + 1) * 512].unsqueeze(0),
                         wait=evd + ([last_evac] if last_evac else []))
            mg = P.pe(lambda e, tt=tt: e.matmul(
                pb[0], self.ones_rf, self.rf[0:1, tt * 512:(tt + 1) * 512],
                start=True, stop=True),
                wait=[ev_gr] + ([last_evac] if last_evac else []), inc=True)
            eg = P.dve(lambda e: e.tensor_copy(self.gb, pb[0]), wait=[mg])
            emk = []
            mkprev = None
            for ee in range(8):
                mk = P.pe(lambda e, ee=ee: e.matmul(
                    pb[1], self.ones_r, dtt[:, ee, :],
                    start=True, stop=True),
                    wait=[e_dt] + ([mkprev] if mkprev else []), inc=True)
                mkprev = P.dve(lambda e, ee=ee: e.tensor_copy(
                    mkb8[:, ee, :], pb[1]), wait=[mk])
                emk.append(mkprev)
            mm_done = None
            for ee in range(8):
                buf = ee % 2
                w = [self.cast_evs["ew"]] + \
                    ([ev_free[buf]] if ev_free[buf] else [])
                ev_ew[buf] = P.dma("gp", ew[:, buf, :, :],
                                   d["ew"][ee].rearrange("(c p) o -> p c o",
                                                         p=128), wait=w)
                exm = []
                for c in range(CH):
                    exm.append(P.pool(lambda e, c=c, tt=tt, ee=ee:
                                      e.tensor_tensor(
                        xm[:, c, :], xn[:, c, tt * 512:(tt + 1) * 512],
                        mkb8[:, ee, :], AL.mult),
                        wait=[emk[ee], ev_xn] + ([mm_done] if mm_done else [])))
                for oc in range(8):
                    for ic in range(CH):
                        st = (ee == 0 and ic == 0)
                        w = [ev_ew[buf], exm[ic]]
                        if st:
                            w.append(emk[7])
                        mm = P.pe(lambda e, oc=oc, ic=ic, st=st, buf=buf:
                                  e.matmul(
                            pb[oc], ew[:, buf, ic, oc * 128:(oc + 1) * 128],
                            xm[:, ic, :], start=st, stop=False),
                            wait=w, inc=(oc == 7 and ic == CH - 1))
                mm_done = mm
                ev_free[buf] = mm
            evE = []
            for oc in range(8):
                evE.append(P.pe(lambda e, oc=oc, tt=tt: e.matmul(
                    pb[oc], self.ebias[0:8, oc * 128:(oc + 1) * 128],
                    disp8[:, tt * 512:(tt + 1) * 512],
                    start=False, stop=True),
                    wait=[ev_d8, self.ev_eb, mm_done], inc=True))
            prev_e2 = getattr(self, "_moe_prev_e2", None)
            for oc in range(8):
                e1 = P.dve(lambda e, oc=oc: e.scalar_tensor_tensor(
                    self.sb512, pb[oc], 1.0, self.gb, AL.mult, AL.mult),
                    wait=[evE[oc], eg] + ([prev_e2] if prev_e2 else []))
                e2 = P.act(lambda e: e.activation(
                    self.mkb, self.sb512, AF.Silu, bias=0.0, scale=1.0),
                    wait=[e1] + ([last_evac] if last_evac else []))
                prev_e2 = e2
                last_evac = P.dma("sp", d["g0"][oc][:, tt * 512:(tt + 1) * 512],
                                  self.mkb, wait=[e2])
            self._moe_prev_e2 = prev_e2
        if "g0" in self.dbg:
            o = self.dbg_dram(f"g0_{s}", [CH, 128, T], BF16)
            self.out_evs.append(P.dma("gp", o, d["g0"], wait=[last_evac]))
        P.commit(f"moe{s}")
        return last_evac

    def s_proj(self, s, xsrc, ev_x, wq_ap, wk_ap, wv_ap, dq, dk, dv, vmode):
        P = self.P
        pb = self.pb
        wv = self.U6.rearrange("p (c f) -> p c f", f=1024)
        ev_wv = P.dma("sp", wv, wv_ap.rearrange("(c p) o -> p c o", p=128))
        vstage = self.HC[:, 0:1040].rearrange("p (h dd) -> p h dd", dd=65)
        em = P.pool(lambda e: e.memset(vstage[:, :, 64:65], 1.0))
        ev_vdma = None
        prev_ev = [None, None]
        for tc in range(NTC):
            evh = []
            for half in range(2):
                b = half
                m = None
                for ic in range(CH):
                    w = ([ev_wv, ev_x] if ic == 0 else None)
                    if ic == 0 and prev_ev[b] is not None:
                        w = w + [prev_ev[b]]
                    m = P.pe(lambda e, ic=ic, tc=tc, half=half, b=b: e.matmul(
                        pb[b], xsrc[:, ic, tc * 128:(tc + 1) * 128],
                        wv[:, ic, half * 512:(half + 1) * 512],
                        start=(ic == 0), stop=(ic == CH - 1)),
                        wait=w, inc=(ic == CH - 1))
                if vmode == "L0":
                    srcap = pb[b].rearrange("p (dd h) -> p dd h", h=16)
                    dstap = vstage[:, :, 32 * half:32 * half + 32].rearrange(
                        "p h dd -> p dd h")
                else:
                    srcap = pb[b].rearrange("p (h dd) -> p h dd", dd=64)
                    dstap = vstage[:, 8 * half:8 * half + 8, 0:64]
                ee = P.dve(lambda e, dst=dstap, sr=srcap: e.tensor_copy(
                    dst, sr), wait=[m, ev_vdma] if ev_vdma else [m])
                prev_ev[b] = ee
                evh.append(ee)
            ev_vdma = P.dma("sp", dv[:, tc, :, :].rearrange("h p dd -> p h dd"),
                            vstage, wait=evh + [em])
        qsl = self.U6[:, 0:4096].rearrange("p (b m c f) -> p b m c f", b=2, m=2, f=128)
        ev_sl = [None, None]
        prev_mm = [None, None]
        ev_qdma = ev_kdma = None
        prev_evq = [None] * NTT
        prev_evk = [None] * NTT
        for oc in range(CH):
            b = oc % 2
            w = [ev_vdma] + ([prev_mm[b]] if prev_mm[b] else [])
            ev_sl[b] = P.dma_multi("sp", [
                (qsl[:, b, 0, :, :], wq_ap(oc).rearrange("(c p) o -> p c o", p=128)),
                (qsl[:, b, 1, :, :], wk_ap(oc).rearrange("(c p) o -> p c o", p=128)),
            ], wait=w)
            evq = []
            evk = []
            mlast = None
            for tt in range(NTT):
                mq = mk = None
                for ic in range(CH):
                    w0 = None
                    if ic == 0:
                        w0 = [ev_sl[b], ev_x]
                        if oc == 0:
                            w0 += [x for x in prev_ev if x]
                        if prev_evq[tt]:
                            w0.append(prev_evq[tt])
                    mq = P.pe(lambda e, ic=ic, tt=tt, b=b: e.matmul(
                        pb[tt], qsl[:, b, 0, ic, :],
                        xsrc[:, ic, tt * 512:(tt + 1) * 512],
                        start=(ic == 0), stop=(ic == CH - 1)),
                        wait=w0, inc=(ic == CH - 1))
                for ic in range(CH):
                    w0 = [prev_evk[tt]] if (ic == 0 and prev_evk[tt]) else None
                    mk = P.pe(lambda e, ic=ic, tt=tt, b=b: e.matmul(
                        pb[4 + tt], qsl[:, b, 1, ic, :],
                        xsrc[:, ic, tt * 512:(tt + 1) * 512],
                        start=(ic == 0), stop=(ic == CH - 1)),
                        wait=w0, inc=(ic == CH - 1))
                wq = [mq] + ([ev_qdma] if (tt == 0 and ev_qdma) else [])
                prev_evq[tt] = P.dve(lambda e, tt=tt: e.tensor_copy(
                    self.HA[:, tt * 512:(tt + 1) * 512], pb[tt]), wait=wq)
                evq.append(prev_evq[tt])
                wk = [mk] + ([ev_kdma] if (tt == 0 and ev_kdma) else [])
                prev_evk[tt] = P.dve(lambda e, tt=tt: e.tensor_copy(
                    self.HB[:, tt * 512:(tt + 1) * 512], pb[4 + tt]), wait=wk)
                evk.append(prev_evk[tt])
                mlast = mk
            prev_mm[b] = mlast
            ev_qdma = P.dma("sp", dq[oc], self.HA, wait=evq)
            ev_kdma = P.dma("sp", dk[oc], self.HB, wait=evk)
        P.commit(f"proj{s}_{vmode}")
        return ev_qdma, ev_kdma, ev_vdma

    def s_attn0(self, s, ev_q, ev_k, ev_v, ev_g0, ev_xn_user):
        P = self.P
        d = self.d
        pb = self.pb
        ek = self.U4.rearrange("p (sc f) -> p sc f", f=512)
        qkvt = self.U4[:, 8192:12288].rearrange("p (c f) -> p c f", f=512)
        g0t = self.U4[:, 0:4096].rearrange("p (c f) -> p c f", f=512)
        xbt = self.U4[:, 4096:8192].rearrange("p (c f) -> p c f", f=512)
        m0 = self.U7[:, 0:2048].rearrange("p (j f) -> p j f", f=512)
        sgf = self.u5f[:, 0:4096].rearrange("p (c f) -> p c f", f=512)
        xTf = self.u5f[:, 4096:8192].rearrange("p (sub f) -> p sub f", f=1024)
        woT = self.U6.rearrange("p (c f) -> p c f", f=1024)
        rbs = lambda o, n: self.rb[0:1, o:o + n]
        rfs = lambda o, n: self.rf[0:1, o:o + n]
        ev_hd1 = P.pool(lambda e: e.memset(self.HD[:, :, :, 64:65], 1.0))
        self._ev_hd1 = ev_hd1
        self.ev_m0 = P.dma(
            "gp", self.U7[:, 0:2048].rearrange("p (j f) -> p j f", f=512),
            self.i["C_MASK0"].rearrange("j p f -> p j f"))
        ev_wo = P.dma("sp", woT, d["woT"].rearrange("(c p) o -> p c o", p=128),
                      wait=[self.pret_ev])
        ev_wog = None
        for half in range(2):
            mw = None
            for ic in range(CH):
                mw = P.pe(lambda e, ic=ic, half=half: e.matmul(
                    pb[6][0:1, :], self.gam16[:, ic:ic + 1],
                    woT[:, ic, half * 512:(half + 1) * 512],
                    start=(ic == 0), stop=(ic == CH - 1)),
                    wait=([ev_wo, self.ev_gam16, ev_wog] if ic == 0 else None),
                    inc=(ic == CH - 1))
            P.dve(lambda e, half=half: e.tensor_copy(
                rbs(6144 + half * 512, 512), pb[6][0:1, :]), wait=[mw])
            mb = None
            for ic in range(CH):
                mb = P.pe(lambda e, ic=ic, half=half: e.matmul(
                    pb[7][0:1, :], self.bet16[:, ic:ic + 1],
                    woT[:, ic, half * 512:(half + 1) * 512],
                    start=(ic == 0), stop=(ic == CH - 1)),
                    wait=([self.ev_bet16] if ic == 0 else None),
                    inc=(ic == CH - 1))
            ev_wog = P.dve(lambda e, half=half: e.tensor_copy(
                rbs(7168 + half * 512, 512), pb[7][0:1, :]), wait=[mb])
        prev_x1dma = None
        prev_sg = [None] * CH
        prev_tr = None
        for TT in range(NTT):
            nsc = 4 * TT + 4
            exp_ev = [None, None, None]
            prev_av = None
            prev_qkvt_user = getattr(self, "_wo_done", None)
            prev_dinv = None
            for half in range(2):
                for hpi in range(4):
                    hp = half * 4 + hpi
                    wl = [ev_q, ev_k, ev_v]
                    if prev_av:
                        wl.append(prev_av)
                    ha = P.dma_multi("sp", [
                        (self.HA[:, 0:512],
                         d["qs"][hp][:, TT * 512:(TT + 1) * 512]),
                        (self.HB[:, 0:nsc * 128], d["ks"][hp][:, 0:nsc * 128]),
                    ], wait=wl)
                    hv = P.dma_multi("sp", [
                        (self.HD[:, 0, 0:nsc, 0:64],
                         d["vs"][2 * hp][0:nsc].rearrange("tc p dd -> p tc dd")[:, :, 0:64]),
                        (self.HD[:, 1, 0:nsc, 0:64],
                         d["vs"][2 * hp + 1][0:nsc].rearrange("tc p dd -> p tc dd")[:, :, 0:64]),
                    ], wait=wl + [self._ev_hd1])
                    for hi in range(2):
                        h = 2 * hp + hi
                        hr = hi * 64
                        eks = []
                        for sc in range(nsc):
                            bk = sc % 3
                            diag = (sc >= 4 * TT)
                            w = [ha] + ([exp_ev[bk]] if exp_ev[bk] else [])
                            mm = P.pe(lambda e, hr=hr, sc=sc, bk=bk,
                                      diag=diag: e.matmul(
                                pb[bk],
                                self.HB[hr:hr + 64, sc * 128:(sc + 1) * 128],
                                self.HA[hr:hr + 64, 0:512],
                                start=True, stop=(not diag),
                                tile_position=(hr, 0)), wait=w,
                                inc=(not diag))
                            if diag:
                                j = sc - 4 * TT
                                mm = P.pe(lambda e, j=j, bk=bk: e.matmul(
                                    pb[bk], self.eyeb, m0[:, j, :],
                                    start=False, stop=True),
                                    wait=[self.ev_m0, self.ev_eyeb], inc=True)
                            ee = P.act(lambda e, sc=sc, bk=bk: e.activation(
                                ek[:, sc, :], pb[bk], AF.Exp,
                                bias=-SHIFT0, scale=0.125), wait=[mm])
                            exp_ev[bk] = ee
                            eks.append(ee)
                        mv = None
                        for sc in range(nsc):
                            w = [eks[sc], hv]
                            if sc == 0 and prev_av:
                                w.append(prev_av)
                            if sc == 0 and prev_qkvt_user:
                                w.append(prev_qkvt_user)
                                prev_qkvt_user = None
                            mv = P.pe(lambda e, hi=hi, sc=sc, nsc=nsc:
                                      e.matmul(
                                pb[4][0:65, :], self.HD[:, hi, sc, :],
                                ek[:, sc, :], start=(sc == 0),
                                stop=(sc == nsc - 1)), wait=w,
                                inc=(sc == nsc - 1))
                        hc = h // 2
                        P.dve(lambda e, hr=hr, hc=hc: e.tensor_copy(
                            qkvt[hr:hr + 64, hc, :], pb[4][0:64, :]),
                            wait=[mv])
                        prev_av = P.dve(lambda e, h=h: e.tensor_copy(
                            rfs((h % 8) * 512, 512), pb[4][64:65, :]))
                einv = None
                for piece in range(2):
                    eL = P.act(lambda e, piece=piece: e.activation(
                        rfs(4096, 2048), rfs(piece * 2048, 2048),
                        AF.Ln, bias=0.0, scale=1.0), wait=[prev_av, einv])
                    einv = P.act(lambda e, piece=piece: e.activation(
                        rbs(piece * 2048, 2048), rfs(4096, 2048),
                        AF.Exp, bias=0.0, scale=-1.0), wait=[eL])
                for cc in range(4):
                    c = half * 4 + cc
                    w = [einv] + ([prev_dinv] if prev_dinv else [])
                    mA = P.pe(lambda e, cc=cc: e.matmul(
                        pb[5][0:64, :], self.ones_r[0:1, 0:64],
                        rbs((2 * cc) * 512, 512),
                        start=True, stop=True), wait=w, inc=False)
                    mB = P.pe(lambda e, cc=cc: e.matmul(
                        pb[5][64:128, :], self.ones_r[0:1, 0:64],
                        rbs((2 * cc + 1) * 512, 512),
                        start=True, stop=True,
                        tile_position=(0, 64)), inc=True)
                    prev_dinv = P.dve(lambda e, c=c: e.scalar_tensor_tensor(
                        qkvt[:, c, :], qkvt[:, c, :], 1.0, pb[5],
                        AL.bypass, AL.mult), wait=[mB])
            ev_qt = prev_dinv
            sqt = self.U5.rearrange("p (c f) -> p c f", f=512)
            esq = P.pool(lambda e: e.tensor_tensor(
                self.U5[:, 0:4096], self.U4[:, 8192:12288],
                self.U4[:, 8192:12288], AL.mult),
                wait=[ev_qt] + [x for x in prev_sg if x])
            mS = mQ = None
            for c in range(CH):
                mS = P.pe(lambda e, c=c: e.matmul(
                    pb[6][0:1, :], self.ones_c, qkvt[:, c, :],
                    start=(c == 0), stop=(c == CH - 1)),
                    wait=[ev_qt] if c == 0 else None, inc=(c == CH - 1))
                mQ = P.pe(lambda e, c=c: e.matmul(
                    pb[7][0:1, :], self.ones_c, sqt[:, c, :],
                    start=(c == 0), stop=(c == CH - 1)),
                    wait=[esq] if c == 0 else None, inc=(c == CH - 1))
            eA = P.dve(lambda e: e.tensor_copy(rfs(4096, 512), pb[6][0:1, :]),
                       wait=[mS])
            eB = P.dve(lambda e: e.tensor_copy(rfs(4608, 512), pb[7][0:1, :]),
                       wait=[mQ])
            eC = P.act(lambda e: e.activation(rfs(5120, 512), rfs(4096, 512),
                                              AF.Copy, bias=0.0, scale=1.0 / E),
                       wait=[eA])
            eD = P.pool(lambda e: e.tensor_tensor(
                rfs(4096, 512), rfs(5120, 512), rfs(5120, 512), AL.mult),
                wait=[eC])
            eE = P.dve(lambda e: e.scalar_tensor_tensor(
                rfs(4608, 512), rfs(4608, 512), 1.0 / E, rfs(4096, 512),
                AL.mult, AL.subtract), wait=[eD, eB])
            eF = P.act(lambda e: e.activation(rfs(4096, 512), rfs(4608, 512),
                                              AF.Ln, bias=EPS_LN, scale=1.0),
                       wait=[eE])
            eG = P.act(lambda e: e.activation(rfs(4608, 512), rfs(4096, 512),
                                              AF.Exp, bias=0.0, scale=-0.5),
                       wait=[eF])
            eH = P.dve(lambda e: e.scalar_tensor_tensor(
                rbs(4096, 512), rfs(5120, 512), -1.0, rfs(4608, 512),
                AL.mult, AL.mult), wait=[eG])
            eI = P.dve(lambda e: e.tensor_copy(rbs(4608, 512), rfs(4608, 512)))
            mR = P.pe(lambda e: e.matmul(
                pb[5], self.ones_r, rbs(4608, 512), start=True, stop=True),
                wait=[eI, prev_dinv], inc=True)
            ev_af = None
            for c in range(CH):
                ev_af = P.dve(lambda e, c=c: e.scalar_tensor_tensor(
                    qkvt[:, c, :], qkvt[:, c, :], self.gam[:, c:c + 1],
                    pb[5], AL.mult, AL.mult), wait=[mR, self.ev_gam, esq])
            ev_wo_mms = []
            for c in range(CH):
                mw = None
                for ic in range(CH):
                    w = [ev_af] if ic == 0 else None
                    if ic == 0 and prev_sg[c]:
                        w = w + [prev_sg[c]]
                    mw = P.pe(lambda e, c=c, ic=ic: e.matmul(
                        pb[c], woT[:, ic, c * 128:(c + 1) * 128],
                        qkvt[:, ic, :], start=(ic == 0), stop=False), wait=w)
                P.pe(lambda e, c=c: e.matmul(
                    pb[c], rbs(6144 + c * 128, 128), rbs(4096, 512),
                    start=False, stop=False), wait=[eH] if c == 0 else None)
                mw = P.pe(lambda e, c=c: e.matmul(
                    pb[c], rbs(7168 + c * 128, 128), self.ones512,
                    start=False, stop=True), wait=[ev_wog] if c == 0 else None,
                    inc=True)
                ev_wo_mms.append(mw)
            eg0 = P.dma("sp", g0t, d["g0"].rearrange(
                "c p t -> p c t")[:, :, TT * 512:(TT + 1) * 512],
                wait=[ev_g0, exp_ev[0], exp_ev[1], exp_ev[2], prev_av])
            tes = [P.dmaT(xbt[:, c, :],
                          d["xb16"][s][TT * 512:(TT + 1) * 512,
                                       c * 128:(c + 1) * 128],
                          wait=[self.ev_xcast, prev_av, exp_ev[0],
                                exp_ev[1], exp_ev[2]])
                   for c in range(CH)]
            ex = P.dma("sp", xTf,
                       self.i["x"][s][TT * 512:(TT + 1) * 512, :].rearrange(
                           "(sub p) e -> p sub e", p=128),
                       wait=[prev_x1dma, prev_tr] if prev_x1dma else
                       ([prev_tr] if prev_tr else None))
            trs = []
            for c in range(CH):
                e1 = P.dve(lambda e, c=c: e.scalar_tensor_tensor(
                    sgf[:, c, :], pb[c], 1.0, g0t[:, c, :],
                    AL.bypass, AL.mult), wait=[ev_wo_mms[c], eg0])
                prev_sg[c] = e1
                P.pool(lambda e, c=c, TT=TT: e.tensor_tensor(
                    self.U0[:, c, TT * 512:(TT + 1) * 512], xbt[:, c, :],
                    sgf[:, c, :], AL.add), wait=[e1, tes[c], ev_xn_user])
                tr = None
                for sub in range(4):
                    mt = P.pe(lambda e, c=c, sub=sub: e.transpose(
                        pb[c][:, 0:128], sgf[:, c, sub * 128:(sub + 1) * 128],
                        self.eyef), wait=[e1, ex] + ([tr] if tr else []),
                        inc=True)
                    tr = P.dve(lambda e, c=c, sub=sub: e.tensor_tensor(
                        xTf[:, sub, c * 128:(c + 1) * 128],
                        xTf[:, sub, c * 128:(c + 1) * 128],
                        pb[c][:, 0:128], AL.add), wait=[mt])
                    trs.append(tr)
            prev_tr = trs[-1]
            prev_x1dma = P.dma(
                "sp", d["x1T"][s][TT * 512:(TT + 1) * 512, :].rearrange(
                    "(sub p) e -> p sub e", p=128), xTf, wait=trs)
            self._wo_done = prev_sg[7]
            P.commit(f"at{s}_{TT}")
        if "x1T" in self.dbg:
            o = self.dbg_dram(f"x1T{s}", [T, E])
            self.out_evs.append(P.dma("sp", o, d["x1T"][s],
                                      wait=[prev_x1dma]))
        return prev_x1dma

    def s_ln1_y(self, s, ev_x1):
        P = self.P
        ev = self.ln_stats(self.U0, [ev_x1])
        pbA, pbB = self.pb[4], self.pb[5]
        ee = None
        prev = None
        for tt in range(NTT):
            w = [ev] + ([prev] if prev else [])
            mA, mB = self.bcast2(
                self.rb[0:1, tt * 512:(tt + 1) * 512],
                self.rb[0:1, 2048 + tt * 512:2048 + (tt + 1) * 512],
                pbA, pbB, wait=w)
            eB = P.dve(lambda e: e.tensor_copy(self.sb512, pbB), wait=[mB])
            for c in range(CH):
                P.dve(lambda e, c=c, tt=tt: e.tensor_tensor(
                    self.sb512b, self.U0[:, c, tt * 512:(tt + 1) * 512],
                    pbA, AL.mult), wait=[mA, eB])
                ee = P.dve(lambda e, c=c, tt=tt: e.tensor_tensor(
                    self.U0[:, c, tt * 512:(tt + 1) * 512], self.sb512b,
                    self.sb512, AL.add))
            prev = ee
        if "y" in self.dbg:
            o = self.dbg_dram(f"y{s}", [CH, 128, T])
            for c in range(CH):
                self.out_evs.append(P.dma("gp", o[c], self.U0[:, c, :],
                                          wait=[ee]))
        P.commit(f"ln1_{s}")
        return ee


    def s_attn1(self, s, ev_q, ev_k, ev_v):
        P = self.P
        d = self.d
        pb = self.pb
        u5f = self.u5f
        ropeC = u5f[:, 0:2048]
        ropeS = u5f[:, 2048:4096]
        eq = self.U5[:, 8192:12288].rearrange("p (m f) -> p m f", f=T)
        ekT = self.U5[:, 12288:16384].rearrange("p (tc f) -> p tc f", f=256)
        outh_T = self.U4.rearrange("p (tc f) -> p tc f", f=1024)
        Ssb = self.HC[:, 0:130].rearrange("p (m f) -> p m f", f=65)
        ev_rope = P.dma("sp", u5f[:, 0:4096].rearrange(
            "p (j f) -> p j f", f=2048),
            self.i["C_ROPE"].rearrange("j p f -> p j f"))
        prev_eq = prev_ek = prev_S = prev_out = None
        prev_load = None
        for h in range(16):
            hc, hr = h // 2, (h % 2) * 64
            wl = [ev_q, ev_k, ev_v] + ([prev_load] if prev_load else [])
            ha = P.dma_multi("sp", [
                (self.HA[0:64, :], d["q1"][hc][hr:hr + 64, :]),
                (self.HA[64:128, :], d["k1"][hc][hr:hr + 64, :]),
                (self.HD[:, 0, :, :], d["v1"][h].rearrange("tc p dd -> p tc dd")),
            ], wait=wl)
            mask = []
            for q_ in range(16):
                mask += [2 * q_ + 1, 2 * q_]
            e1 = P.dve(lambda e: e.stream_shuffle(self.HB, self.HA, mask),
                       wait=[ha] + ([prev_out] if prev_out else []))
            e2 = P.pool(lambda e: e.tensor_tensor(self.HC, self.HA, ropeC,
                                                  AL.mult),
                        wait=[ha, ev_rope] + ([prev_S] if prev_S else []) +
                        ([prev_out] if prev_out else []))
            e3 = P.pool(lambda e: e.tensor_tensor(self.HB, self.HB, ropeS,
                                                  AL.mult), wait=[e1])
            e4 = P.pool(lambda e: e.tensor_tensor(self.HA, self.HC, self.HB,
                                                  AL.add))
            e5 = P.pool(lambda e: e.tensor_tensor(self.HB, self.HA, self.HA,
                                                  AL.mult), wait=[e4])
            for tc in range(NTC):
                mq = P.pe(lambda e, tc=tc: e.matmul(
                    pb[5][:, tc:tc + 1], self.HB[0:64, tc * 128:(tc + 1) * 128],
                    self.ones_c[0:64, :], start=True, stop=True),
                    wait=([e5] + ([prev_eq] if prev_eq else []))
                    if tc == 0 else None, inc=(tc == NTC - 1))
                mk = P.pe(lambda e, tc=tc: e.matmul(
                    pb[6][:, tc:tc + 1], self.HB[64:128, tc * 128:(tc + 1) * 128],
                    self.ones_c[64:128, :], start=True, stop=True,
                    tile_position=(64, 0)),
                    wait=None, inc=(tc == NTC - 1))
            enq = P.dve(lambda e: e.tensor_scalar_mul(self.nq, pb[5][:, 0:16],
                                                      -0.5), wait=[mq])
            enk = P.dve(lambda e: e.tensor_scalar_mul(self.nk, pb[6][:, 0:16],
                                                      -0.5), wait=[mk])
            peq = []
            for mc in range(2):
                for tt in range(NTT):
                    bk = (mc * NTT + tt) % 2
                    w = [e4, self.ev_MT] + ([peq[-2]] if len(peq) >= 2 else [])
                    m1 = P.pe(lambda e, mc=mc, tt=tt, bk=bk, h=h: e.matmul(
                        pb[bk], self.MT[0:64, h, mc * 128:(mc + 1) * 128],
                        self.HA[0:64, tt * 512:(tt + 1) * 512],
                        start=True, stop=True), wait=w, inc=True)
                    ee = P.act(lambda e, mc=mc, tt=tt, bk=bk: e.activation(
                        eq[:, mc, tt * 512:(tt + 1) * 512], pb[bk], AF.Exp,
                        bias=0.0, scale=1.0), wait=[m1, prev_eq])
                    peq.append(ee)
            prev_eq = peq[-1]
            pek = []
            for tc in range(NTC):
                bk = 2 + tc % 2
                w = [e4, self.ev_MT] + ([pek[-2]] if len(pek) >= 2 else [])
                m1 = P.pe(lambda e, tc=tc, bk=bk, h=h: e.matmul(
                    pb[bk][:, 0:256], self.HA[64:128, tc * 128:(tc + 1) * 128],
                    self.MT[64:128, h, :], start=True, stop=True,
                    tile_position=(64, 0)), wait=w, inc=True)
                ee = P.act(lambda e, tc=tc, bk=bk: e.activation(
                    ekT[:, tc, :], pb[bk][:, 0:256], AF.Exp,
                    bias=self.nk[:, tc:tc + 1], scale=1.0),
                    wait=[m1, enk, prev_ek])
                pek.append(ee)
            prev_ek = pek[-1]
            eS = []
            for mc in range(2):
                mS = None
                for tc in range(NTC):
                    w = [pek[tc], ha] + ([prev_S] if (tc == 0 and prev_S) else [])
                    if tc == 0 and mc == 1:
                        w.append(eS[0])
                    mS = P.pe(lambda e, mc=mc, tc=tc: e.matmul(
                        pb[4][:, 0:65], ekT[:, tc, mc * 128:(mc + 1) * 128],
                        self.HD[:, 0, tc, :], start=(tc == 0),
                        stop=(tc == NTC - 1)), wait=w if tc == 0 else [pek[tc]],
                        inc=(tc == NTC - 1))
                eS.append(P.dve(lambda e, mc=mc: e.tensor_copy(
                    Ssb[:, mc, :], pb[4][:, 0:65]), wait=[mS]))
            prev_S = eS[-1]
            for tc in range(NTC):
                m1 = None
                for mc in range(2):
                    w = [eS[mc], peq[-1]]
                    if mc == 0 and prev_out:
                        w.append(prev_out)
                    m1 = P.pe(lambda e, mc=mc, tc=tc: e.matmul(
                        pb[7][:, 0:65],
                        eq[:, mc, tc * 128:(tc + 1) * 128], Ssb[:, mc, :],
                        start=(mc == 0), stop=(mc == 1)), wait=w,
                        inc=(mc == 1))
                fa = P.act(lambda e, tc=tc: e.activation(
                    self.colA, self.nq[:, tc:tc + 1], AF.Exp,
                    bias=0.0, scale=1.0),
                    wait=[enq] + ([prev_out] if prev_out else []))
                u1 = P.dve(lambda e: e.tensor_scalar(
                    self.colB, pb[7][:, 64:65], self.colA, 1.0 / 256.0,
                    AL.mult, AL.mult), wait=[m1, fa])
                u2 = P.dve(lambda e: e.tensor_scalar_max(
                    self.colB, self.colB, 1e-6))
                u3 = P.dve(lambda e: e.reciprocal(self.colC, self.colB))
                u4 = P.dve(lambda e: e.tensor_scalar(
                    self.colB, self.colC, self.colA, 1.0 / 256.0,
                    AL.mult, AL.mult))
                prev_out = P.dve(lambda e, tc=tc, h=h: e.tensor_scalar_mul(
                    outh_T[:, tc, h * 64:(h + 1) * 64], pb[7][:, 0:64],
                    self.colB), wait=[u4])
            prev_load = prev_out
            P.commit(f"l1a{s}_{h}")
        woutT = self.U6.rearrange("p (c f) -> p c f", f=1024)
        ev_wT = P.dma("sp", woutT,
                      d["wout1T"].rearrange("(c p) o -> p c o", p=128),
                      wait=[self.pret_ev])
        outF = self.U5[:, 0:4096].rearrange("p (c f) -> p c f", f=512)
        prev_mm = None
        prev_dma = None
        ev_s1 = None
        last_tr = [None, None]
        for tt in range(NTT):
            etr = []
            for ic in range(4):
                tc = tt * 4 + ic
                for c in range(CH):
                    bk = c % 2
                    w = [x for x in (prev_out, prev_mm, last_tr[bk]) if x]
                    mt = P.pe(lambda e, tc=tc, c=c, bk=bk: e.transpose(
                        pb[bk].bitcast(BF16)[:, 0:128],
                        outh_T[:, tc, c * 128:(c + 1) * 128], self.eyeb),
                        wait=w or None, inc=True)
                    last_tr[bk] = P.dve(lambda e, tc=tc, c=c, ic=ic, bk=bk:
                                        e.tensor_copy(
                        outF[:, c, ic * 128:(ic + 1) * 128],
                        pb[bk].bitcast(BF16)[:, 0:128]), wait=[mt])
                    etr.append(last_tr[bk])
            evs1 = []
            if not hasattr(self, "_wout_ec"):
                self._wout_ec = [None] * 4
            for oc in range(CH):
                mw = None
                for ic in range(CH):
                    w = ([ev_wT] + etr) if ic == 0 else None
                    if ic == 0 and self._wout_ec[oc % 4]:
                        w = w + [self._wout_ec[oc % 4]]
                    mw = P.pe(lambda e, oc=oc, ic=ic: e.matmul(
                        pb[2 + oc % 4], woutT[:, ic, oc * 128:(oc + 1) * 128],
                        outF[:, ic, :], start=(ic == 0), stop=(ic == CH - 1)),
                        wait=w, inc=(ic == CH - 1))
                ec = P.dve(lambda e, oc=oc: e.tensor_copy(self.mkb, pb[2 + oc % 4]),
                           wait=[mw] + ([prev_dma] if prev_dma else []))
                self._wout_ec[oc % 4] = ec
                prev_dma = P.dma("sp", d["s1"][oc][:, tt * 512:(tt + 1) * 512],
                                 self.mkb, wait=[ec])
                evs1.append(prev_dma)
            prev_mm = Ev("pe", P.cnt["pe"], P.blk_idx)
            ev_s1 = evs1[-1]
            P.commit(f"l1aw{s}_{tt}")
        return ev_s1


    def s_ffn1(self, s, ev_y, ev_s1):
        P = self.P
        d = self.d
        pb = self.pb
        hs = self.U4.rearrange("p (c f) -> p c f", f=T)
        vT1 = self.U5[:, 4096:12416].rearrange("p (tc h dd) -> p tc h dd",
                                               h=8, dd=65)
        m1m = self.U7.rearrange("p (j f) -> p j f", f=512)
        ev_m1 = P.dma("gp", m1m, self.i["C_MASK1"].rearrange("j p f -> p j f"),
                      wait=[ev_y])
        hdf = self.HD.rearrange("p a b c -> p (a b c)")
        sqB = hdf[:, 0:2048]
        xpt = self.U5[:, 0:2048].rearrange("p (j f) -> p j f", f=512)
        rfs = lambda o, n: self.rf[0:1, o:o + n]
        rbs = lambda o, n: self.rb[0:1, o:o + n]
        prev_done = ev_s1
        for l in range(2):
            ehs = []
            for c in range(CH):
                w = [ev_y, ev_s1] if (l == 0 and c == 0) else \
                    ([prev_done] if c == 0 else None)
                if c < 4:
                    ehs.append(P.pool(lambda e, c=c: e.tensor_copy(
                        hs[:, c, :], self.U0[:, c, :]), wait=w))
                else:
                    P.pool(lambda e, c=c: e.memset(hs[:, c, 0:1], 0.0), wait=w)
                    ehs.append(P.pool(lambda e, c=c: e.tensor_copy(
                        hs[:, c, 1:T], self.U0[:, c, 0:T - 1])))
            ev_hs = ehs[-1]
            wqk = self.U6.rearrange("p (c f) -> p c f", f=1024)
            ev_wqk = P.dma("sp", wqk, d["wqkv_l"][l][:, 0:1024].rearrange(
                "(c p) o -> p c o", p=128), wait=[self.cast_evs["wqkv_l"],
                                                  prev_done])
            wv = self.U5[:, 0:4096].rearrange("p (c f) -> p c f", f=512)
            ev_wv = P.dma("sp", wv, d["wqkv_l"][l][:, 1024:1536].rearrange(
                "(c p) o -> p c o", p=128), wait=[self.cast_evs["wqkv_l"],
                                                  prev_done])
            em0 = P.pool(lambda e: e.memset(vT1[:, :, :, 64:65], 1.0))
            prev_ev = [None, None]
            for tc in range(NTC):
                b = tc % 2
                m = None
                for ic in range(CH):
                    w = ([ev_wv, ev_hs] + ([prev_ev[b]] if prev_ev[b] else [])) \
                        if ic == 0 else None
                    m = P.pe(lambda e, ic=ic, tc=tc, b=b: e.matmul(
                        pb[b], hs[:, ic, tc * 128:(tc + 1) * 128],
                        wv[:, ic, :], start=(ic == 0), stop=(ic == CH - 1)),
                        wait=w, inc=(ic == CH - 1))
                prev_ev[b] = P.dve(lambda e, tc=tc, b=b: e.tensor_copy(
                    vT1[:, tc, :, 0:64],
                    pb[b].rearrange("p (h dd) -> p h dd", dd=64)),
                    wait=[m, em0])
            ev_v1 = [x for x in prev_ev if x]
            ek1 = self.U5[:, 0:4096].rearrange("p (j f) -> p j f", f=512)
            prev_abuser = None
            prev_osb = None
            for hp in range(4):
                h0, h1 = 2 * hp, 2 * hp + 1
                evq = []
                for hi in range(2):
                    h = 2 * hp + hi
                    for tt in range(NTT):
                        bk = hi * 4 + tt
                        m = None
                        for ic in range(CH):
                            w = None
                            if ic == 0:
                                w = [ev_wqk, ev_hs] + list(ev_v1) + \
                                    ([prev_abuser] if prev_abuser else [])
                            P.pe(lambda e, h=h, ic=ic, tt=tt, bk=bk:
                                 e.matmul(
                                pb[bk][0:64, :],
                                wqk[:, ic, h * 64:h * 64 + 64],
                                hs[:, ic, tt * 512:(tt + 1) * 512],
                                start=(ic == 0), stop=(ic == CH - 1)),
                                wait=w)
                        for ic in range(CH):
                            m = P.pe(lambda e, h=h, ic=ic, tt=tt, bk=bk:
                                     e.matmul(
                                pb[bk][64:128, :],
                                wqk[:, ic, 512 + h * 64:512 + h * 64 + 64],
                                hs[:, ic, tt * 512:(tt + 1) * 512],
                                start=(ic == 0), stop=(ic == CH - 1),
                                tile_position=(0, 64)),
                                inc=(ic == CH - 1))
                        evq.append(P.dve(lambda e, hi=hi, tt=tt, bk=bk:
                                         e.tensor_copy(
                            self.HA[hi * 64:hi * 64 + 64,
                                    tt * 512:(tt + 1) * 512],
                            pb[bk][0:64, :]), wait=[m]))
                        evq.append(P.dve(lambda e, hi=hi, tt=tt, bk=bk:
                                         e.tensor_copy(
                            self.HB[hi * 64:hi * 64 + 64,
                                    tt * 512:(tt + 1) * 512],
                            pb[bk][64:128, :]), wait=[m]))
                eA = P.pool(lambda e: e.tensor_tensor(self.HC, self.HA,
                                                      self.HA, AL.mult),
                            wait=evq + ([prev_osb] if prev_osb else []))
                eB = P.pool(lambda e: e.tensor_tensor(sqB, self.HB, self.HB,
                                                      AL.mult), wait=evq)
                rb_slot = {(0, 0): 0, (0, 1): 2048, (1, 0): 4096, (1, 1): 6144}
                prev_r = None
                for (ti, hi) in [(0, 0), (0, 1), (1, 0), (1, 1)]:
                    sq = self.HC if ti == 0 else sqB
                    esq = eA if ti == 0 else eB
                    ecp = []
                    for tt in range(NTT):
                        mr = P.pe(lambda e, sq=sq, hi=hi, tt=tt: e.matmul(
                            pb[6][0:1, :], self.ones_c[hi * 64:hi * 64 + 64, :],
                            sq[hi * 64:hi * 64 + 64, tt * 512:(tt + 1) * 512],
                            start=True, stop=True,
                            tile_position=(hi * 64, 0)),
                            wait=[esq] + ([prev_r] if prev_r else []),
                            inc=True)
                        prev_r = P.dve(lambda e, tt=tt: e.tensor_copy(
                            rfs(4096 + tt * 512, 512), pb[6][0:1, :]),
                            wait=[mr])
                        ecp.append(prev_r)
                    e1 = P.dve(lambda e: e.tensor_scalar_max(
                        rfs(4096, 2048), rfs(4096, 2048), 1e-24))
                    e2 = P.act(lambda e: e.activation(
                        rfs(0, 2048), rfs(4096, 2048), AF.Ln,
                        bias=0.0, scale=1.0), wait=[e1])
                    e3 = P.act(lambda e: e.activation(
                        rfs(4096, 2048), rfs(0, 2048), AF.Exp,
                        bias=0.0, scale=-0.5), wait=[e2])
                    prev_r = P.dve(lambda e, sl=rb_slot[(ti, hi)]:
                                   e.tensor_copy(rbs(sl, 2048),
                                                 rfs(4096, 2048)),
                                   wait=[e3])
                ev_inv = prev_r
                prev_x = None
                for tt in range(NTT):
                    ext = P.dma("gp", xpt,
                                self.i["C_XPOS"][:, :, tt * 512:(tt + 1) * 512]
                                .rearrange("j p f -> p j f"),
                                wait=[prev_x] if prev_x else None)
                    mqa = P.pe(lambda e, tt=tt: e.matmul(
                        pb[2][0:64, :], self.ones_r[0:1, 0:64],
                        rbs(0 + tt * 512, 512), start=True, stop=True),
                        wait=[ev_inv], inc=True)
                    mqb = P.pe(lambda e, tt=tt: e.matmul(
                        pb[2][64:128, :], self.ones_r[0:1, 0:64],
                        rbs(2048 + tt * 512, 512), start=True, stop=True,
                        tile_position=(0, 64)), inc=True)
                    mka = P.pe(lambda e, tt=tt: e.matmul(
                        pb[3][0:64, :], self.ones_r[0:1, 0:64],
                        rbs(4096 + tt * 512, 512), start=True, stop=True),
                        inc=True)
                    mkb = P.pe(lambda e, tt=tt: e.matmul(
                        pb[3][64:128, :], self.ones_r[0:1, 0:64],
                        rbs(6144 + tt * 512, 512), start=True, stop=True,
                        tile_position=(0, 64)), inc=True)
                    en1 = P.dve(lambda e, tt=tt, l=l: e.scalar_tensor_tensor(
                        self.HA[:, tt * 512:(tt + 1) * 512],
                        self.HA[:, tt * 512:(tt + 1) * 512],
                        self.qs2[:, l:l + 1], pb[2], AL.mult, AL.mult),
                        wait=[mqa, mqb])
                    en2 = P.dve(lambda e, tt=tt, l=l: e.scalar_tensor_tensor(
                        self.HB[:, tt * 512:(tt + 1) * 512],
                        self.HB[:, tt * 512:(tt + 1) * 512],
                        self.ks2[:, l:l + 1], pb[3], AL.mult, AL.mult),
                        wait=[mka, mkb])
                    msa = P.pe(lambda e, tt=tt: e.matmul(
                        pb[4], self.p32, self.HA[:, tt * 512:(tt + 1) * 512],
                        start=True, stop=True),
                        wait=[en1, self.ev_p32], inc=True)
                    msb = P.pe(lambda e, tt=tt: e.matmul(
                        pb[5], self.p32, self.HB[:, tt * 512:(tt + 1) * 512],
                        start=True, stop=True), wait=[en2], inc=True)
                    t1 = P.dve(lambda e, tt=tt: e.tensor_tensor(
                        self.sb512, self.HA[:, tt * 512:(tt + 1) * 512],
                        xpt[:, 0, :], AL.mult), wait=[ext, en1])
                    t2 = P.dve(lambda e, tt=tt: e.scalar_tensor_tensor(
                        self.sb512b, pb[4], 1.0, xpt[:, 1, :],
                        AL.mult, AL.mult), wait=[msa])
                    ea1 = P.dve(lambda e, tt=tt: e.tensor_tensor(
                        self.HA[:, tt * 512:(tt + 1) * 512], self.sb512,
                        self.sb512b, AL.add))
                    t3 = P.dve(lambda e, tt=tt: e.tensor_tensor(
                        self.sb512, self.HB[:, tt * 512:(tt + 1) * 512],
                        xpt[:, 2, :], AL.mult), wait=[en2])
                    t4 = P.dve(lambda e, tt=tt: e.scalar_tensor_tensor(
                        self.sb512b, pb[5], 1.0, xpt[:, 3, :],
                        AL.mult, AL.mult), wait=[msb])
                    prev_x = P.dve(lambda e, tt=tt: e.tensor_tensor(
                        self.HB[:, tt * 512:(tt + 1) * 512], self.sb512,
                        self.sb512b, AL.add))
                exp_ev = [None, None]
                prev_av = None
                osb_ev = []
                for hi in range(2):
                    h = 2 * hp + hi
                    for wi in range(NTT):
                        jcs = list(range(4, 8)) if wi == 0 else list(range(8))
                        eks = {}
                        for jc in jcs:
                            kc = 4 * (wi - 1) + jc
                            bk = jc % 2
                            w = [prev_x] + ([exp_ev[bk]] if exp_ev[bk] else [])
                            P.pe(lambda e, hi=hi, jc=jc, kc=kc, wi=wi, bk=bk:
                                 e.matmul(
                                pb[bk],
                                self.HB[hi * 64:hi * 64 + 64,
                                        kc * 128:(kc + 1) * 128],
                                self.HA[hi * 64:hi * 64 + 64,
                                        wi * 512:(wi + 1) * 512],
                                start=True, stop=False,
                                tile_position=(hi * 64, 0)), wait=w)
                            mm = P.pe(lambda e, jc=jc, bk=bk: e.matmul(
                                pb[bk], self.eyeb, m1m[:, jc, :],
                                start=False, stop=True),
                                wait=[ev_m1, self.ev_eyeb], inc=True)
                            ee = P.act(lambda e, jc=jc, bk=bk: e.activation(
                                ek1[:, jc, :], pb[bk], AF.Exp,
                                bias=-SHIFT1, scale=8.0), wait=[mm])
                            exp_ev[bk] = ee
                            eks[jc] = ee
                        mv = None
                        first = True
                        for jc in jcs:
                            kc = 4 * (wi - 1) + jc
                            w = [eks[jc]] + ([prev_av] if (first and prev_av)
                                             else [])
                            if hi == 0:
                                mv = P.pe(lambda e, h=h, kc=kc, jc=jc,
                                          first=first: e.matmul(
                                    pb[4][0:65, :], vT1[:, kc, h, :],
                                    ek1[:, jc, :], start=first,
                                    stop=(jc == jcs[-1])), wait=w,
                                    inc=(jc == jcs[-1]))
                            else:
                                mv = P.pe(lambda e, h=h, kc=kc, jc=jc,
                                          first=first: e.matmul(
                                    pb[5][64:128, :], vT1[:, kc, h, 0:64],
                                    ek1[:, jc, :], start=first,
                                    stop=(jc == jcs[-1]),
                                    tile_position=(0, 64)), wait=w,
                                    inc=False)
                                mv = P.pe(lambda e, h=h, kc=kc, jc=jc,
                                          first=first: e.matmul(
                                    pb[6][0:1, :], vT1[:, kc, h, 64:65],
                                    ek1[:, jc, :], start=first,
                                    stop=(jc == jcs[-1])), wait=[eks[jc]],
                                    inc=(jc == jcs[-1]))
                            first = False
                        if hi == 0:
                            prev_av = P.dve(lambda e, wi=wi: e.tensor_copy(
                                rfs(wi * 512, 512), pb[4][64:65, :]),
                                wait=[mv])
                        else:
                            prev_av = P.dve(lambda e, wi=wi: e.tensor_copy(
                                rfs(2048 + wi * 512, 512), pb[6][0:1, :]),
                                wait=[mv])
                        if hi == 0:
                            prev_av = P.dve(lambda e, wi=wi: e.tensor_copy(
                                self.HC[0:64, wi * 512:(wi + 1) * 512],
                                pb[4][0:64, :]))
                        else:
                            prev_av = P.dve(lambda e, wi=wi: e.tensor_copy(
                                self.HC[64:128, wi * 512:(wi + 1) * 512],
                                pb[5][64:128, :]))
                        osb_ev.append(prev_av)
                eL = P.act(lambda e: e.activation(rfs(4096, 2048),
                                                  rfs(0, 2048), AF.Ln,
                                                  bias=0.0, scale=1.0),
                           wait=[prev_av])
                eE1 = P.act(lambda e: e.activation(rbs(0, 2048),
                                                   rfs(4096, 2048), AF.Exp,
                                                   bias=0.0, scale=-1.0),
                            wait=[eL])
                eL2 = P.act(lambda e: e.activation(rfs(4096, 2048),
                                                   rfs(2048, 2048), AF.Ln,
                                                   bias=0.0, scale=1.0))
                eE2 = P.act(lambda e: e.activation(rbs(2048, 2048),
                                                   rfs(4096, 2048), AF.Exp,
                                                   bias=0.0, scale=-1.0),
                            wait=[eL2])
                prev_bc = None
                eo = []
                for wi in range(NTT):
                    ma = P.pe(lambda e, wi=wi: e.matmul(
                        pb[7][0:64, :], self.ones_r[0:1, 0:64],
                        rbs(wi * 512, 512), start=True, stop=True),
                        wait=[eE1, eE2] + ([prev_bc] if prev_bc else []),
                        inc=True)
                    mb = P.pe(lambda e, wi=wi: e.matmul(
                        pb[7][64:128, :], self.ones_r[0:1, 0:64],
                        rbs(2048 + wi * 512, 512), start=True, stop=True,
                        tile_position=(0, 64)), inc=True)
                    ec = P.dve(lambda e: e.tensor_copy(self.sb512, pb[7]),
                               wait=[ma, mb] + ([eo[-1]] if eo else []))
                    prev_bc = ec
                    eo.append(P.pool(lambda e, wi=wi: e.tensor_tensor(
                        self.HC[:, wi * 512:(wi + 1) * 512],
                        self.HC[:, wi * 512:(wi + 1) * 512],
                        self.sb512, AL.mult), wait=[ec] + osb_ev))
                prev_osb = P.dma("sp", d["ybk"][hp], self.HC, wait=eo)
                prev_abuser = eo[-1]
                if hp == 3:
                    ev_opairs = prev_osb
            ob = self.U6.rearrange("p (c f) -> p c f", f=T)[:, 0:4, :]
            ev_ob = P.dma("sp", self.U6.rearrange("p (c f) -> p c f", f=T),
                          d["ybk"], wait=[prev_osb])
            wo_sb = self.U5[:, 0:4096].rearrange("p (c f) -> p c f", f=1024)
            ev_wos = P.dma("sp", wo_sb, d["wo_l"][l].rearrange(
                "(c p) o -> p c o", p=128), wait=[self.cast_evs["wo_l"],
                                                  prev_osb])
            ev_yup = None
            yup_bank = [None] * CH
            for tt in range(NTT):
                for oc in range(CH):
                    mw = None
                    for ic in range(4):
                        w = None
                        if ic == 0:
                            w = [ev_ob, ev_wos] + \
                                ([yup_bank[oc]] if yup_bank[oc] else [])
                        mw = P.pe(lambda e, oc=oc, ic=ic, tt=tt: e.matmul(
                            pb[oc], wo_sb[:, ic, oc * 128:(oc + 1) * 128],
                            ob[:, ic, tt * 512:(tt + 1) * 512],
                            start=(ic == 0), stop=(ic == 3)),
                            wait=w, inc=(ic == 3))
                    ev_yup = P.dve(lambda e, oc=oc, tt=tt: e.scalar_tensor_tensor(
                        self.U0[:, oc, tt * 512:(tt + 1) * 512], pb[oc], 1.0,
                        self.U0[:, oc, tt * 512:(tt + 1) * 512],
                        AL.mult, AL.add), wait=[mw])
                    yup_bank[oc] = ev_yup
            P.commit(f"f1a{s}_{l}")
            ehs = []
            for c in range(CH):
                w = [ev_yup] if c == 0 else None
                if c < 4:
                    ehs.append(P.pool(lambda e, c=c: e.tensor_copy(
                        hs[:, c, :], self.U0[:, c, :]), wait=w))
                else:
                    P.pool(lambda e, c=c: e.memset(hs[:, c, 0:1], 0.0), wait=w)
                    ehs.append(P.pool(lambda e, c=c: e.tensor_copy(
                        hs[:, c, 1:T], self.U0[:, c, 0:T - 1])))
            ev = self.ln_stats(hs, [ehs[-1]])
            pbA, pbB = self.pb[4], self.pb[5]
            eaf = None
            prev = None
            for tt in range(NTT):
                w = [ev] + ([prev] if prev else [])
                mA, mB = self.bcast2(
                    self.rb[0:1, tt * 512:(tt + 1) * 512],
                    self.rb[0:1, 2048 + tt * 512:2048 + (tt + 1) * 512],
                    pbA, pbB, wait=w)
                eBc = P.dve(lambda e: e.tensor_copy(self.sb512, pbB),
                            wait=[mB])
                for c in range(CH):
                    P.dve(lambda e, c=c, tt=tt: e.tensor_tensor(
                        self.sb512b, hs[:, c, tt * 512:(tt + 1) * 512], pbA,
                        AL.mult), wait=[mA, eBc])
                    t2 = P.dve(lambda e: e.tensor_tensor(
                        self.gb, self.sb512b, self.sb512, AL.add))
                    eaf = P.dve(lambda e, c=c, tt=tt, l=l: e.tensor_scalar(
                        hs[:, c, tt * 512:(tt + 1) * 512], self.gb,
                        self.lgam[:, l, c:c + 1], self.lbet[:, l, c:c + 1],
                        AL.mult, AL.add))
                prev = eaf
            P.commit(f"f1b{s}_{l}")
            h3 = self.U5[:, 0:11264].rearrange("p (k f) -> p k f", f=512)
            wsl = self.U6[:, 0:4096].rearrange("p (b m c f) -> p b m c f",
                                               b=2, m=2, f=128)
            ev_w2done = None
            prev_sl = [None, None]
            for tt in range(NTT):
                evh3 = []
                for ocp in range(22):
                    wdt = 128 if ocp < 21 else 42
                    b = ocp % 2
                    acol = ocp * 128
                    gcol = 2730 + ocp * 128
                    w = [self.cast_evs["w1_l"]] + \
                        ([prev_sl[b]] if prev_sl[b] else []) + \
                        ([eaf] if (tt == 0 and ocp < 2) else [])
                    esl = P.dma_multi("sp", [
                        (wsl[:, b, 0, :, 0:wdt],
                         d["w1_l"][l][:, acol:acol + wdt].rearrange(
                             "(c p) o -> p c o", p=128)),
                        (wsl[:, b, 1, :, 0:wdt],
                         d["w1_l"][l][:, gcol:gcol + wdt].rearrange(
                             "(c p) o -> p c o", p=128)),
                    ], wait=w)
                    mA2 = mG = None
                    prev_eh3 = getattr(self, "_w1_prev_eh3", None)
                    prev_eg = getattr(self, "_w1_prev_eg", None)
                    w2ev = getattr(self, "_w2_evac", [None] * CH)
                    for ic in range(CH):
                        w0 = None
                        if ic == 0:
                            w0 = [esl] + ([prev_eh3] if prev_eh3 else []) + \
                                ([w2ev[0]] if (ocp < 2 and w2ev[0]) else [])
                        mA2 = P.pe(lambda e, b=b, ic=ic, tt=tt, wdt=wdt:
                                   e.matmul(
                            pb[0][0:wdt, :], wsl[:, b, 0, ic, 0:wdt],
                            hs[:, ic, tt * 512:(tt + 1) * 512],
                            start=(ic == 0), stop=(ic == CH - 1)),
                            wait=w0, inc=(ic == CH - 1))
                    for ic in range(CH):
                        w0 = None
                        if ic == 0:
                            w0 = ([prev_eg] if prev_eg else []) + \
                                ([w2ev[1]] if (ocp < 2 and w2ev[1]) else [])
                            w0 = w0 or None
                        mG = P.pe(lambda e, b=b, ic=ic, tt=tt, wdt=wdt:
                                  e.matmul(
                            pb[1][0:wdt, :], wsl[:, b, 1, ic, 0:wdt],
                            hs[:, ic, tt * 512:(tt + 1) * 512],
                            start=(ic == 0), stop=(ic == CH - 1)),
                            wait=w0, inc=(ic == CH - 1))
                    prev_sl[b] = mG
                    eg = P.act(lambda e, ocp=ocp, wdt=wdt, l=l: e.activation(
                        self.mkb[0:wdt, :], pb[1][0:wdt, :], AF.Gelu,
                        bias=self.b1c[0:wdt, l, 22 + ocp:23 + ocp],
                        scale=1.0), wait=[mG])
                    eh3 = P.dve(lambda e, ocp=ocp, wdt=wdt, l=l:
                                e.scalar_tensor_tensor(
                        h3[:, ocp, :][0:wdt, :], pb[0][0:wdt, :],
                        self.b1c[0:wdt, l, ocp:ocp + 1], self.mkb[0:wdt, :],
                        AL.add, AL.mult), wait=[mA2, eg])
                    self._w1_prev_eh3 = eh3
                    self._w1_prev_eg = eg
                    evh3.append(eh3)
                prev_w2sl = getattr(self, "_w2_prev_sl", [None, None])
                mws = []
                for kc in range(22):
                    wk = 128 if kc < 21 else 42
                    b2 = kc % 2
                    w = [self.cast_evs["w2_l"]] + \
                        ([prev_w2sl[b2]] if prev_w2sl[b2] else [])
                    es2 = P.dma("sp",
                                self.U6[0:wk, 4096 + b2 * 1024:
                                        4096 + b2 * 1024 + 1024],
                                d["w2_l"][l][kc * 128:kc * 128 + wk, :],
                                wait=w)
                    for oc in range(CH):
                        w0 = [es2, evh3[kc]] if oc == 0 else None
                        if kc == 0:
                            w0 = (w0 or []) + [evh3[-1]] + \
                                ([ev_w2done] if ev_w2done else [])
                        mw2 = P.pe(lambda e, kc=kc, oc=oc, b2=b2, wk=wk:
                                   e.matmul(
                            pb[oc],
                            self.U6[0:wk, 4096 + b2 * 1024 + oc * 128:
                                    4096 + b2 * 1024 + (oc + 1) * 128],
                            h3[0:wk, kc, :], start=(kc == 0),
                            stop=(kc == 21)), wait=w0,
                            inc=(kc == 21 or oc == 7))
                        if kc == 21:
                            mws.append(mw2)
                    prev_w2sl[b2] = mw2
                self._w2_prev_sl = prev_w2sl
                if not hasattr(self, "_w2_evac"):
                    self._w2_evac = [None] * CH
                for oc in range(CH):
                    ev_w2done = P.dve(lambda e, oc=oc, tt=tt, l=l:
                                      e.scalar_tensor_tensor(
                        self.U0[:, oc, tt * 512:(tt + 1) * 512], pb[oc],
                        self.b2c[:, l, oc:oc + 1],
                        self.U0[:, oc, tt * 512:(tt + 1) * 512],
                        AL.add, AL.add), wait=[mws[oc]])
                    self._w2_evac[oc] = ev_w2done
            prev_done = ev_w2done
            P.commit(f"f1c{s}_{l}")
        return prev_done

    def s_final(self, s, ev_ffn):
        P = self.P
        d = self.d
        pb = self.pb
        s1t = self.U4[:, 0:4096].rearrange("p (c f) -> p c f", f=512)
        sgf = self.u5f[:, 0:4096].rearrange("p (c f) -> p c f", f=512)
        xTf = self.u5f[:, 4096:8192].rearrange("p (sub f) -> p sub f", f=1024)
        ev_out = None
        prev_tr = None
        prev_sg = None
        prev_s1 = None
        for TT in range(NTT):
            es1 = P.dma("sp", s1t, d["s1"].rearrange(
                "c p t -> p c t")[:, :, TT * 512:(TT + 1) * 512],
                wait=[ev_ffn] + ([prev_s1] if prev_s1 else []))
            prev_s1 = es1
            ex = P.dma("sp", xTf,
                       d["x1T"][s][TT * 512:(TT + 1) * 512, :].rearrange(
                           "(sub p) e -> p sub e", p=128),
                       wait=([prev_tr] if prev_tr else []) +
                       ([ev_out] if ev_out else []))
            trs = []
            for c in range(CH):
                eg1 = P.act(lambda e, c=c, TT=TT: e.activation(
                    self.mkb, self.U0[:, c, TT * 512:(TT + 1) * 512],
                    AF.Silu, bias=0.0, scale=1.0),
                    wait=[ev_ffn] + ([prev_sg] if prev_sg else []))
                e1 = P.dve(lambda e, c=c: e.tensor_tensor(
                    sgf[:, c, :], s1t[:, c, :], self.mkb, AL.mult),
                    wait=[eg1, es1] + ([prev_tr] if prev_tr else []))
                prev_sg = e1
                tr = None
                for sub in range(4):
                    mt = P.pe(lambda e, c=c, sub=sub: e.transpose(
                        pb[c][:, 0:128], sgf[:, c, sub * 128:(sub + 1) * 128],
                        self.eyef), wait=[e1, ex] + ([tr] if tr else []),
                        inc=True)
                    tr = P.dve(lambda e, c=c, sub=sub: e.tensor_tensor(
                        xTf[:, sub, c * 128:(c + 1) * 128],
                        xTf[:, sub, c * 128:(c + 1) * 128],
                        pb[c][:, 0:128], AL.add), wait=[mt])
                    trs.append(tr)
            prev_tr = trs[-1]
            ev_out = P.dma(
                "sp", self.o_out[s][TT * 512:(TT + 1) * 512, :].rearrange(
                    "(sub p) e -> p sub e", p=128), xTf, wait=trs)
            self.out_evs.append(ev_out)
            P.commit(f"fc{s}_{TT}")
        return ev_out

    def finish(self):
        P = self.P
        P.op("pool", lambda e: e.memset(self.fint, 0.0), wait=self.out_evs)
        P.commit("fin")
        return self.nc


def build_l0(dbg=(), nseq=1):
    k = K(dbg=dbg)
    k.prelude()
    d = k.d
    for s in range(nseq):
        ev_xn = k.s_stats_xn(s)
        ev_g0 = k.s_moe(s, ev_xn)
        ev_q, ev_k, ev_v = k.s_proj(
            s, k.U0, ev_xn,
            lambda oc: d["wqT"][:, oc * 128:(oc + 1) * 128],
            lambda oc: d["wkT"][:, oc * 128:(oc + 1) * 128],
            d["wvT"], d["qs"], d["ks"], d["vs"], "L0")
        ev_x1 = k.s_attn0(s, ev_q, ev_k, ev_v, ev_g0, ev_xn)
        k.s_ln1_y(s, ev_x1)
    return k.finish(), k


def build_full(dbg=(), nseq=NSEQ):
    k = K(dbg=dbg)
    k.prelude()
    d = k.d
    for s in range(nseq):
        ev_xn = k.s_stats_xn(s)
        ev_g0 = k.s_moe(s, ev_xn)
        ev_q, ev_k, ev_v = k.s_proj(
            s, k.U0, ev_xn,
            lambda oc: d["wqT"][:, oc * 128:(oc + 1) * 128],
            lambda oc: d["wkT"][:, oc * 128:(oc + 1) * 128],
            d["wvT"], d["qs"], d["ks"], d["vs"], "L0")
        ev_x1 = k.s_attn0(s, ev_q, ev_k, ev_v, ev_g0, ev_xn)
        ev_y = k.s_ln1_y(s, ev_x1)
        ev_q1, ev_k1, ev_v1 = k.s_proj(
            s, k.U0, ev_y,
            lambda oc: d["wqkv1T"][:, oc * 128:(oc + 1) * 128],
            lambda oc: d["wqkv1T"][:, 1024 + oc * 128:1024 + (oc + 1) * 128],
            d["wqkv1T"][:, 2048:3072], d["q1"], d["k1"], d["v1"], "L1")
        ev_s1 = k.s_attn1(s, ev_q1, ev_k1, ev_v1)
        ev_ffn = k.s_ffn1(s, ev_y, ev_s1)
        k.s_final(s, ev_ffn)
    return k.finish(), k


_CACHE = {}


def kernel(**inputs):
    x = np.ascontiguousarray(inputs["L0_attn_x"])
    n_cores = 8
    if "nc" not in _CACHE:
        nc, k = build_full()
        _CACHE["nc"] = nc
    nc = _CACHE["nc"]
    consts = host_consts()
    base = {n: np.ascontiguousarray(v) for n, v in inputs.items()
            if n != "L0_attn_x"}
    base.update(consts)
    in_maps = []
    for c in range(n_cores):
        m = dict(base)
        m["L0_attn_x"] = x[2 * c:2 * c + 2]
        in_maps.append(m)
    res = run_bass_kernel_spmd(nc, in_maps, core_ids=list(range(n_cores)))
    out = np.concatenate([np.asarray(r["OUT"], dtype=np.float32)
                          for r in res.results], axis=0)
    return out


# version 03
# speedup vs baseline: 1.0111x; 1.0111x over previous
import numpy as np
import concourse.bass as bass
import concourse.mybir as mybir
from concourse.bass_utils import run_bass_kernel_spmd


import concourse.bass as bass
import concourse.mybir as mybir

F32 = mybir.dt.float32
BF16 = mybir.dt.bfloat16
AX = mybir.AxisListType.X
AL = mybir.AluOpType
AF = mybir.ActivationFunctionType

_SECTION = {"pe": "tensor", "dve": "vector", "act": "scalar",
            "pool": "gpsimd", "sp": "sync", "gp": "gpsimd", "actdma": "scalar"}
_QUEUES = ("sp", "gp", "actdma")
N_DMA_SEMS = 40


class Ev:
    __slots__ = ("key", "tgt", "blk", "dma")

    def __init__(self, key, tgt, blk, dma=False):
        self.key, self.tgt, self.blk, self.dma = key, tgt, blk, dma

    def __repr__(self):
        return f"Ev({self.key},{self.tgt},b{self.blk}{'D' if self.dma else ''})"


class Prog:
    def __init__(self, nc):
        self.nc = nc
        self.sem = {}
        self.cnt = {}
        for e in ("pe", "dve", "act", "pool"):
            self.sem[e] = nc.alloc_semaphore("S_" + e)
            self.cnt[e] = 0
        self._dma_pools = {"sp": [], "gp": [], "actdma": []}
        npool = {"sp": 24, "gp": 12, "actdma": 4}
        ii = 0
        for q, n in npool.items():
            for _ in range(n):
                k = f"d{ii}"
                ii += 1
                self.sem[k] = nc.alloc_semaphore("S_" + k)
                self.cnt[k] = 0
                self._dma_pools[q].append(k)
        self._dma_rr = {"sp": 0, "gp": 0, "actdma": 0}
        self.blk_idx = 0
        self._recs = {}
        self._last_wait = {}
        self._cmap = {}

    def setup_consts(self, values):
        vals = sorted(set(float(v) for v in values))
        if not vals:
            return
        self._ctile = self.nc.alloc_sbuf_tensor(
            "constcols", [128, len(vals)], F32).ap()
        for i, v in enumerate(vals):
            col = self._ctile[:, i:i + 1]
            self.op("pool", lambda e, c=col, vv=v: e.memset(c, vv))
            self._cmap[v] = col
            try:
                self.nc.const_aps.register(F32, v, col)
            except Exception:
                pass

    def ccol(self, v):
        return self._cmap[float(v)]

    def _rec(self, eng_name, fn):
        self._recs.setdefault(_SECTION[eng_name], []).append(fn)

    def _emit_waits(self, eng_name, waits):
        sec = _SECTION[eng_name]
        for ev in (waits or []):
            if ev is None:
                continue
            if ev.blk < self.blk_idx and not ev.dma:
                continue
            key = (sec, ev.key)
            if self._last_wait.get(key, -1) >= ev.tgt:
                continue
            self._last_wait[key] = ev.tgt
            sem, tgt = self.sem[ev.key], ev.tgt
            self._rec(eng_name, lambda e, s=sem, t=tgt: e.wait_ge(s, t))

    def op(self, eng_name, fn, wait=None, inc=True):
        self_serial = eng_name in ("dve", "act", "pool")
        wl = list(wait) if wait else []
        if self_serial and self.cnt[eng_name] > 0:
            wl.append(Ev(eng_name, self.cnt[eng_name], self.blk_idx))
        self._emit_waits(eng_name, wl)
        ev = None
        if inc or self_serial:
            self.cnt[eng_name] += 1
            sem, tgt = self.sem[eng_name], self.cnt[eng_name]
            ev = Ev(eng_name, tgt, self.blk_idx)
            self._rec(eng_name, lambda e, s=sem, f=fn: f(e).then_inc(s, 1))
        else:
            self._rec(eng_name, lambda e, f=fn: f(e))
        return ev

    def raw(self, eng_name, fn):
        self._rec(eng_name, fn)

    def dma_multi(self, q, transfers, wait=None, **kw):
        assert q in _QUEUES
        self._emit_waits(q, wait)
        pool = self._dma_pools[q]
        k = pool[self._dma_rr[q]]
        self._dma_rr[q] = (self._dma_rr[q] + 1) % len(pool)
        sem = self.sem[k]
        base = self.cnt[k]
        if base > 0:
            key = (_SECTION[q], k)
            if self._last_wait.get(key, -1) < base:
                self._last_wait[key] = base
                self._rec(q, lambda e, s=sem, t=base: e.wait_ge(s, t))
        n = len(transfers)
        for t in transfers:
            out, in_ = t[0], t[1]
            tkw = dict(kw)
            if len(t) > 2:
                tkw.update(t[2])
            self._rec(q, lambda e, o=out, i=in_, kww=tkw, s=sem:
                      e.dma_start(out=o, in_=i, **kww).then_inc(s, 16))
        self.cnt[k] = base + 16 * n
        return Ev(k, self.cnt[k], self.blk_idx, dma=True)

    def dma(self, q, out, in_, wait=None, **kw):
        return self.dma_multi(q, [(out, in_)], wait=wait, **kw)

    def dmaT_multi(self, transfers, wait=None):
        self._emit_waits("sp", wait)
        pool = self._dma_pools["sp"]
        k = pool[self._dma_rr["sp"]]
        self._dma_rr["sp"] = (self._dma_rr["sp"] + 1) % len(pool)
        sem = self.sem[k]
        base = self.cnt[k]
        if base > 0:
            key = (_SECTION["sp"], k)
            if self._last_wait.get(key, -1) < base:
                self._last_wait[key] = base
                self._rec("sp", lambda e, s=sem, t=base: e.wait_ge(s, t))
        for (out, in_) in transfers:
            self._rec("sp", lambda e, o=out, i=in_, s=sem:
                      e.dma_start_transpose(o, i).then_inc(s, 16))
        self.cnt[k] = base + 16 * len(transfers)
        return Ev(k, self.cnt[k], self.blk_idx, dma=True)

    def dmaT(self, out, in_, wait=None):
        return self.dmaT_multi([(out, in_)], wait=wait)

    def commit(self, name=None):
        if not self._recs:
            return
        recs, self._recs = self._recs, {}
        with self.nc.Block(name) as b:
            for sec, fns in recs.items():
                def body(eng, fns=fns):
                    for f in fns:
                        f(eng)
                getattr(b, sec)(body)
        self.blk_idx += 1
        self._last_wait = {}

    def pe(self, fn, wait=None, inc=False):
        return self.op("pe", fn, wait=wait, inc=inc)

    def dve(self, fn, wait=None, inc=False):
        return self.op("dve", fn, wait=wait, inc=inc)

    def act(self, fn, wait=None, inc=False):
        return self.op("act", fn, wait=wait, inc=inc)

    def pool(self, fn, wait=None, inc=False):
        return self.op("pool", fn, wait=wait, inc=inc)




E = 1024
T = 2048
NSEQ = 2
CH = 8
NTT = 4
NTC = 16
NEG = -1e30
EPS_LN = 1e-5
SHIFT0 = 12.0
SHIFT1 = 30.0
HID = 2730

CONST_SCALARS = [1.0 / E, -1.0, EPS_LN, 1e-6, -0.5, 1e-24, 8.0, 0.125,
                 -SHIFT0, -SHIFT1, 1e-30, 0.5, 1.0 / 4096.0, 1.0 / 256.0]


def host_consts():
    c = {}
    c["C_EYEF"] = np.eye(128, dtype=np.float32)
    p = np.arange(128)[:, None]
    f = np.arange(512)[None, :]
    m = np.zeros((4, 128, 512), np.float32)
    for j in range(4):
        m[j] = np.where(p <= f - 128 * j, 0.0, NEG)
    c["C_MASK0"] = m
    m1 = np.zeros((8, 128, 512), np.float32)
    for jc in range(8):
        keep = (f <= 128 * jc + p) if jc < 4 else (128 * (jc - 4) + p <= f)
        m1[jc] = np.where(keep, 0.0, NEG)
    c["C_MASK1"] = m1
    d = np.arange(128) % 64
    freq = 10000.0 ** (-2.0 * (d // 2) / 64)
    t = np.arange(T)
    ang = np.maximum(t[None, :] - 1, 0) * freq[:, None]
    Cc = np.cos(ang)
    Ss = np.sin(ang)
    sign = np.where(d % 2 == 0, -1.0, 1.0)[:, None]
    Cc[:, 0] = 1.0
    Ss[:, 0] = 0.0
    c["C_ROPE"] = np.stack([Cc, sign * Ss]).astype(np.float32)
    fi = d % 32
    inv_freq = 1.0 / 10000.0 ** (2.0 * fi / 64)
    base = (2.0 * fi + 0.4 * 64) / (1.4 * 64)
    fr = t[None, :] * inv_freq[:, None]
    scale = base[:, None] ** ((t - T // 2) / 256.0)[None, :]
    cos, sin = np.cos(fr), np.sin(fr)
    sgn = np.where(d < 32, -1.0, 1.0)[:, None]
    c["C_XPOS"] = np.stack([cos * scale, sgn * sin * scale,
                            cos / scale, sgn * sin / scale]).astype(np.float32)
    H = np.array([[1.0]])
    while H.shape[0] < 64:
        H = np.block([[H, H], [H, -H]])
    c["C_HAD"] = H.astype(np.float32)
    P32 = np.zeros((128, 128), np.float32)
    for p in range(128):
        blk, off = p // 64, p % 64
        P32[blk * 64 + (off + 32) % 64, p] = 1.0
    c["C_P32"] = P32
    return c


class K:
    def __init__(self, dbg=()):
        self.dbg = set(dbg)
        nc = bass.Bass("TRN2", target_bir_lowering=False)
        self.nc = nc
        self.P = Prog(nc)
        self.out_evs = []
        self.dbg_names = []
        self._io()
        self._sbuf()

    def _io(self):
        nc = self.nc
        ap = lambda n, s, dt=F32, kind="ExternalInput": nc.dram_tensor(
            n, s, dt, kind=kind).ap()
        i = self.i = {}
        i["x"] = ap("L0_attn_x", [NSEQ, T, E])
        for n, s in [("L0_attn_Wq", [E, E]), ("L0_attn_Wk", [E, E]),
                     ("L0_attn_Wv", [E, E]), ("L0_attn_Wo", [E, E]),
                     ("L0_attn_gamma", [E]), ("L0_attn_beta", [E]),
                     ("L0_ffn_router_w", [E, 8]), ("L0_ffn_router_b", [8]),
                     ("L0_ffn_expert_w", [8, E, E]), ("L0_ffn_expert_b", [8, E]),
                     ("L1_attn_w_qkv", [3 * E, E]), ("L1_attn_w_out", [E, E]),
                     ("L1_attn_sorf_signs", [16, 4, 3, 64]),
                     ("L1_attn_sorf_scales", [16, 4, 64]),
                     ("L1_ffn_wqkv", [2, E, 1536]), ("L1_ffn_q_scale", [2, 64]),
                     ("L1_ffn_k_scale", [2, 64]), ("L1_ffn_wo", [2, 512, E]),
                     ("L1_ffn_ln_g", [2, E]), ("L1_ffn_ln_b", [2, E]),
                     ("L1_ffn_w1", [2, E, 5460]), ("L1_ffn_b1", [2, 5460]),
                     ("L1_ffn_w2", [2, HID, E]), ("L1_ffn_b2", [2, E]),
                     ("C_EYEF", [128, 128]), ("C_MASK0", [4, 128, 512]),
                     ("C_MASK1", [8, 128, 512]), ("C_ROPE", [2, 128, T]),
                     ("C_XPOS", [4, 128, T]), ("C_HAD", [64, 64]),
                     ("C_P32", [128, 128])]:
            i[n] = ap(n, s)
        self.o_out = ap("OUT", [NSEQ, T, E], F32, kind="ExternalOutput")
        sc = lambda n, s, dt=BF16: nc.dram_tensor(n, s, dt, kind="Internal").ap()
        d = self.d = {}
        d["xb16"] = sc("xb16", [NSEQ, T, E])
        for n in ("wq_b", "wk_b", "wv_b", "wo_b", "wout1_b"):
            d[n] = sc(n, [E, E])
        d["wqkv1_b"] = sc("wqkv1_b", [3 * E, E])
        for n, s in [("wqT", [E, E]), ("wkT", [E, E]), ("wvT", [E, E]),
                     ("woT", [E, E]), ("wqkv1T", [E, 3 * E]),
                     ("wout1T", [E, E])]:
            d[n] = sc(n, s)
        d["wqkv_l"] = sc("wqkv_l", [2, E, 1536])
        d["wo_l"] = sc("wo_l", [2, 512, E])
        d["w1_l"] = sc("w1_l", [2, E, 5460])
        d["w2_l"] = sc("w2_l", [2, HID, E])
        d["ew"] = sc("ew", [8, E, E])
        d["disp"] = sc("disp", [8, T])
        d["gatec"] = sc("gatec", [1, T], F32)
        d["g0"] = sc("g0", [CH, 128, T])
        d["s1"] = sc("s1", [CH, 128, T])
        d["qs"] = sc("qs", [CH, 128, T])
        d["ks"] = sc("ks", [CH, 128, T])
        d["vs"] = sc("vs", [16, NTC, 128, 65])
        d["q1"] = sc("q1", [CH, 128, T])
        d["k1"] = sc("k1", [CH, 128, T])
        d["v1"] = sc("v1", [16, NTC, 128, 65])
        d["ybk"] = sc("ybk", [4, 128, T])
        d["x1T"] = sc("x1T", [NSEQ, T, E], F32)

    def dbg_dram(self, name, shape, dt=F32):
        a = self.nc.dram_tensor("DBG_" + name, shape, dt,
                                kind="ExternalOutput").ap()
        self.dbg_names.append(name)
        return a

    def _sbuf(self):
        nc = self.nc
        A = lambda n, s, dt=BF16, **kw: nc.alloc_sbuf_tensor(n, s, dt, **kw).ap()
        self.U0 = A("U0", [128, CH, T])
        self.U4 = A("U4", [128, 16384], is_transpose_dest=True)
        self.U5 = A("U5", [128, 16384], is_transpose_dest=True)
        self.U6 = A("U6", [128, 8192], is_transpose_dest=True)
        self.U7 = A("U7", [128, 4096])
        self.HA = A("HA", [128, T])
        self.HB = A("HB", [128, T])
        self.HC = A("HC", [128, T])
        self.HD = A("HD", [128, 2, 16, 65])
        self.rf = A("rf", [1, 6144], F32)
        self.rb = A("rb", [1, 8192])
        self.eyef = A("eyef", [128, 128], F32)
        self.eyeb = A("eyeb", [128, 128])
        self.ones_r = A("ones_r", [1, 128])
        self.ones_rf = A("ones_rf", [1, 128], F32)
        self.ones_c = A("ones_c", [128, 1])
        self.ones512 = A("ones512", [1, 512])
        self.gam = A("gam", [128, 8], F32)
        self.rwt = A("rwt", [128, 8, 8])
        self.rbias = A("rbias", [8, 1], F32)
        self.ebias = A("ebias", [8, 1024])
        self.MT = A("MT", [128, 16, 256])
        self.lgam = A("lgam", [128, 2, 8], F32)
        self.lbet = A("lbet", [128, 2, 8], F32)
        self.b1c = A("b1c", [128, 2, 44], F32)
        self.b2c = A("b2c", [128, 2, 8], F32)
        self.qksc = A("qksc", [128, 2], F32)
        self.ltT8 = A("ltT8", [128, 8], F32)
        self.oh8 = A("oh8", [128, 8])
        self.t8 = A("t8", [8, 128])
        self.g1x = A("g1x", [1, 128], F32)
        self.mx8 = A("mx8", [128, 1], F32)
        self.nmx8 = A("nmx8", [128, 1], F32)
        self.et8 = A("et8", [128, 8], F32)
        self.sm8 = A("sm8", [128, 1], F32)
        self.gt8 = A("gt8", [128, 1], F32)
        self.mkb = A("mkb", [128, 512])
        self.gb = A("gb", [128, 512], F32)
        self.sb512 = A("sb512", [128, 512], F32)
        self.sb512b = A("sb512b", [128, 512], F32)
        self.nq = A("nq", [128, 16], F32)
        self.nk = A("nk", [128, 16], F32)
        self.colA = A("colA", [128, 1], F32)
        self.colB = A("colB", [128, 1], F32)
        self.colC = A("colC", [128, 1], F32)
        self.p32 = A("p32", [128, 128])
        self.qs2 = A("qs2", [128, 2], F32)
        self.ks2 = A("ks2", [128, 2], F32)
        self.fint = A("fint", [1, 4], F32)
        self.pb = [nc.alloc_psum_tensor(f"pb{j}", [128, 512], F32).ap()
                   for j in range(8)]
        self.u5f = self.U5.bitcast(F32)
        self.u4f = self.U4.bitcast(F32)

    def P_last(self, eng):
        return Ev(eng, self.P.cnt[eng], self.P.blk_idx)

    def prelude(self):
        P = self.P
        i, d = self.i, self.d
        P.setup_consts(CONST_SCALARS)
        for t_ in (self.ones_r, self.ones_rf, self.ones_c, self.ones512):
            P.pool(lambda e, t=t_: e.memset(t, 1.0))
        P.pool(lambda e: e.memset(self.HD[:, :, :, 64:65], 1.0))
        self.ev_xcast = P.dma("gp", d["xb16"], i["x"])
        ce = self.cast_evs = {}
        for src, dst in [("L0_attn_Wq", "wq_b"), ("L0_attn_Wk", "wk_b"),
                         ("L0_attn_Wv", "wv_b"), ("L0_attn_Wo", "wo_b"),
                         ("L1_attn_w_qkv", "wqkv1_b"),
                         ("L1_attn_w_out", "wout1_b"),
                         ("L0_ffn_expert_w", "ew"), ("L1_ffn_wqkv", "wqkv_l"),
                         ("L1_ffn_wo", "wo_l"), ("L1_ffn_w1", "w1_l"),
                         ("L1_ffn_w2", "w2_l")]:
            ce[dst] = P.dma("gp", d[dst], i[src])
        self.ev_eye = P.dma("sp", self.eyef, i["C_EYEF"])
        self.ev_m0 = P.dma(
            "gp", self.U7[:, 0:2048].rearrange("p (j f) -> p j f", f=512),
            i["C_MASK0"].rearrange("j p f -> p j f"))
        self.ev_eyeb = P.dma("gp", self.eyeb, i["C_EYEF"])
        self.ev_p32 = P.dma("gp", self.p32, i["C_P32"])
        slow = {"allow_slow_non_contiguous": True}
        self.ev_gam = P.dma("sp", self.gam,
                            i["L0_attn_gamma"].rearrange("(c p) -> p c", p=128),
                            **slow)
        self.gam16 = self.nc.alloc_sbuf_tensor("gam16", [128, 8], BF16).ap()
        self.bet16 = self.nc.alloc_sbuf_tensor("bet16", [128, 8], BF16).ap()
        self.ev_gam16 = P.dma("gp", self.gam16,
                              i["L0_attn_gamma"].rearrange("(c p) -> p c", p=128),
                              **slow)
        self.ev_bet16 = P.dma("gp", self.bet16,
                              i["L0_attn_beta"].rearrange("(c p) -> p c", p=128),
                              **slow)
        self.ev_rw = P.dma("gp", self.rwt,
                           i["L0_ffn_router_w"].rearrange("(c p) e -> p c e",
                                                          p=128))
        self.ev_rb = P.dma("sp", self.rbias,
                           i["L0_ffn_router_b"].unsqueeze(1))
        self.ev_eb = P.dma("gp", self.ebias, i["L0_ffn_expert_b"])
        for l in range(2):
            self.ev_lgam = P.dma("sp", self.lgam[:, l, :],
                                 i["L1_ffn_ln_g"][l].rearrange(
                                     "(c p) -> p c", p=128), **slow)
            self.ev_lbet = P.dma("sp", self.lbet[:, l, :],
                                 i["L1_ffn_ln_b"][l].rearrange(
                                     "(c p) -> p c", p=128), **slow)
            self.ev_b1a = P.dma("sp", self.b1c[:, l, 0:21],
                                i["L1_ffn_b1"][l][0:2688].rearrange(
                                    "(c p) -> p c", p=128), **slow)
            P.dma("sp", self.b1c[0:42, l, 21:22],
                  i["L1_ffn_b1"][l][2688:2730].rearrange(
                      "(c p) -> p c", p=42), **slow)
            P.dma("sp", self.b1c[:, l, 22:43],
                  i["L1_ffn_b1"][l][2730:5418].rearrange(
                      "(c p) -> p c", p=128), **slow)
            self.ev_b1b = P.dma("sp", self.b1c[0:42, l, 43:44],
                                i["L1_ffn_b1"][l][5418:5460].rearrange(
                                    "(c p) -> p c", p=42), **slow)
            self.ev_b2c = P.dma("sp", self.b2c[:, l, :],
                                i["L1_ffn_b2"][l].rearrange(
                                    "(c p) -> p c", p=128), **slow)
            self.ev_qks = P.dma("sp", self.qs2[0:64, l:l + 1],
                                i["L1_ffn_q_scale"][l].unsqueeze(1), **slow)
            P.dma("sp", self.qs2[64:128, l:l + 1],
                  i["L1_ffn_q_scale"][l].unsqueeze(1), **slow)
            P.dma("sp", self.ks2[0:64, l:l + 1],
                  i["L1_ffn_k_scale"][l].unsqueeze(1), **slow)
            self.ev_kks = P.dma("sp", self.ks2[64:128, l:l + 1],
                                i["L1_ffn_k_scale"][l].unsqueeze(1), **slow)
        P.commit("pre0")
        for (src, dst) in [("wq_b", "wqT"), ("wk_b", "wkT"), ("wv_b", "wvT"),
                           ("wo_b", "woT"), ("wout1_b", "wout1T"),
                           ("wqkv1_b", "wqkv1T")]:
            self._pretranspose(d[src], d[dst], ce[src])
        self._build_MT()
        P.commit("pre1")

    def _pretranspose(self, src, dst, ev):
        P = self.P
        R, C = src.shape[0], src.shape[1]
        prev = [getattr(self, "_pt_prev0", None), getattr(self, "_pt_prev1", None)]
        for c in range(C // 128):
            buf = c % 2
            st = self.U6[:, buf * 4096:buf * 4096 + R]
            w = [ev] + ([prev[buf]] if prev[buf] else [])
            e1 = P.dmaT(st, src[:, c * 128:(c + 1) * 128], wait=w)
            prev[buf] = P.dma("sp", dst[c * 128:(c + 1) * 128, :], st, wait=[e1])
        self._pt_prev0, self._pt_prev1 = prev[0], prev[1]
        self.pret_ev = prev[(C // 128 - 1) % 2]

    def _build_MT(self):
        P = self.P
        i = self.i
        nc = self.nc
        had = nc.alloc_sbuf_tensor("had", [64, 64], BF16).ap()
        e_h = P.dma("gp", had, i["C_HAD"])
        sg = nc.alloc_sbuf_tensor("sg_", [64, 16, 4, 3], F32).ap()
        e_s = P.dma("sp", sg, i["L1_attn_sorf_signs"].rearrange(
            "h r k p -> p h r k"), allow_slow_non_contiguous=True)
        scl = nc.alloc_sbuf_tensor("scl", [64, 16, 4], F32).ap()
        e_c = P.dma("sp", scl, i["L1_attn_sorf_scales"].rearrange(
            "h r p -> p h r"), allow_slow_non_contiguous=True)
        eye64 = nc.alloc_sbuf_tensor("eye64", [64, 64], F32).ap()
        e_e = P.dma("sp", eye64, i["C_EYEF"][0:64, 0:64])
        tmp = nc.alloc_sbuf_tensor("mtmp", [64, 64], BF16).ap()
        pb = self.pb
        prev_m3 = None
        for h in range(16):
            for r in range(4):
                w0 = [e_e, e_c, e_h, e_s] + ([prev_m3] if prev_m3 else [])
                e0 = P.dve(lambda e, h=h, r=r: e.tensor_scalar(
                    tmp, eye64, scl[:, h:h + 1, r], 1.0 / 4096.0,
                    AL.mult, AL.mult), wait=w0)
                m1 = P.pe(lambda e: e.matmul(pb[6][0:64, 0:64], had, tmp,
                                             start=True, stop=True),
                          wait=[e0], inc=True)
                e1 = P.dve(lambda e, h=h, r=r: e.tensor_scalar_mul(
                    tmp, pb[6][0:64, 0:64], sg[:, h:h + 1, r, 2]), wait=[m1])
                m2 = P.pe(lambda e: e.matmul(pb[7][0:64, 0:64], had, tmp,
                                             start=True, stop=True),
                          wait=[e1], inc=True)
                e2 = P.dve(lambda e, h=h, r=r: e.tensor_scalar_mul(
                    tmp, pb[7][0:64, 0:64], sg[:, h:h + 1, r, 1]), wait=[m2])
                m3 = P.pe(lambda e: e.matmul(pb[6][0:64, 0:64], had, tmp,
                                             start=True, stop=True),
                          wait=[e2], inc=True)
                prev_m3 = m3
                P.dve(lambda e, h=h, r=r: e.tensor_scalar_mul(
                    self.MT[0:64, h, r * 64:(r + 1) * 64],
                    pb[6][0:64, 0:64], sg[:, h:h + 1, r, 0]), wait=[m3])
                self.ev_MT = P.dve(lambda e, h=h, r=r: e.tensor_copy(
                    self.MT[64:128, h, r * 64:(r + 1) * 64],
                    self.MT[0:64, h, r * 64:(r + 1) * 64]))

    def ln_stats(self, xF, wait_evs):
        P = self.P
        pbS, pbQ = self.pb[6], self.pb[7]
        sq = self.U5.rearrange("p (c f) -> p c f", f=512)
        r = lambda o: self.rf[0:1, o:o + 2048]
        rb = lambda o: self.rb[0:1, o:o + 2048]
        evS = evQ = None
        prev_q = None
        for tt in range(NTT):
            sqe = []
            for c in range(CH):
                w = list(wait_evs) if tt == 0 else [prev_q]
                sqe.append(P.pool(lambda e, c=c, tt=tt: e.tensor_tensor(
                    sq[:, c, :], xF[:, c, tt * 512:(tt + 1) * 512],
                    xF[:, c, tt * 512:(tt + 1) * 512], AL.mult), wait=w))
            mS = mQ = None
            for c in range(CH):
                st = (c == 0)
                wS = (list(wait_evs) if tt == 0 else []) + \
                    ([evS] if (st and evS) else [])
                mS = P.pe(lambda e, c=c, tt=tt, st=st: e.matmul(
                    pbS[0:1, :], self.ones_c,
                    xF[:, c, tt * 512:(tt + 1) * 512],
                    start=st, stop=(c == CH - 1)), wait=wS or None,
                    inc=(c == CH - 1))
                wQ = [sqe[c]] + ([evQ] if (st and evQ) else [])
                mQ = P.pe(lambda e, c=c, tt=tt, st=st: e.matmul(
                    pbQ[0:1, :], self.ones_c, sq[:, c, :],
                    start=st, stop=(c == CH - 1)), wait=wQ,
                    inc=(c == CH - 1))
            prev_q = mQ
            evS = P.dve(lambda e, tt=tt: e.tensor_copy(
                self.rf[0:1, tt * 512:(tt + 1) * 512], pbS[0:1, :]), wait=[mS])
            evQ = P.dve(lambda e, tt=tt: e.tensor_copy(
                self.rf[0:1, 2048 + tt * 512:2048 + (tt + 1) * 512],
                pbQ[0:1, :]), wait=[mQ])
        P.act(lambda e: e.activation(rb(4096), r(0), AF.Copy,
                                     bias=0.0, scale=1.0 / E), wait=[evS])
        ea = P.act(lambda e: e.activation(r(4096), r(0), AF.Copy,
                                          bias=0.0, scale=1.0 / E))
        e3 = P.pool(lambda e: e.tensor_tensor(r(0), r(4096), r(4096), AL.mult),
                    wait=[ea])
        e4 = P.dve(lambda e: e.scalar_tensor_tensor(
            r(2048), r(2048), 1.0 / E, r(0), AL.mult, AL.subtract),
            wait=[e3, evQ])
        e5 = P.act(lambda e: e.activation(r(0), r(2048), AF.Ln,
                                          bias=EPS_LN, scale=1.0), wait=[e4])
        e6 = P.act(lambda e: e.activation(r(2048), r(0), AF.Exp,
                                          bias=0.0, scale=-0.5), wait=[e5])
        e7 = P.dve(lambda e: e.scalar_tensor_tensor(
            rb(2048), r(4096), -1.0, r(2048), AL.mult, AL.mult), wait=[e6])
        P.dve(lambda e: e.tensor_copy(r(4096), r(2048)), wait=[e7])
        e9 = P.dve(lambda e: e.tensor_copy(rb(0), r(2048)))
        return e9

    def bcast2(self, rowA, rowB, pbA, pbB, wait=None):
        P = self.P
        mA = P.pe(lambda e: e.matmul(pbA, self.ones_r, rowA,
                                     start=True, stop=True), wait=wait,
                  inc=True)
        mB = P.pe(lambda e: e.matmul(pbB, self.ones_r, rowB,
                                     start=True, stop=True), inc=True)
        return mA, mB

    def s_stats_xn(self, s):
        P = self.P
        d = self.d
        xbF = self.U4.rearrange("p (c f) -> p c f", f=T)
        te = P.dmaT_multi(
            [(xbF[:, c, :], d["xb16"][s][:, c * 128:(c + 1) * 128])
             for c in range(CH)], wait=[self.ev_xcast])
        tes = [te]
        ev = self.ln_stats(xbF, [te])
        xn = self.U0
        pbA, pbB = self.pb[4], self.pb[5]
        ee = None
        prevA = prevB = None
        for tt in range(NTT):
            w = [ev] + ([prevA] if prevA else []) + ([prevB] if prevB else [])
            mA, mB = self.bcast2(
                self.rb[0:1, tt * 512:(tt + 1) * 512],
                self.rb[0:1, 2048 + tt * 512:2048 + (tt + 1) * 512],
                pbA, pbB, wait=w)
            eB = P.dve(lambda e: e.tensor_copy(self.sb512, pbB), wait=[mB])
            for c in range(CH):
                t1 = P.dve(lambda e, c=c, tt=tt: e.tensor_tensor(
                    self.sb512b, xbF[:, c, tt * 512:(tt + 1) * 512], pbA,
                    AL.mult), wait=[mA, eB])
                ee = P.dve(lambda e, c=c, tt=tt: e.tensor_tensor(
                    xn[:, c, tt * 512:(tt + 1) * 512], self.sb512b,
                    self.sb512, AL.add))
            prevA, prevB = ee, ee
        if "xn" in self.dbg:
            o = self.dbg_dram(f"xn{s}", [CH, 128, T])
            for c in range(CH):
                self.out_evs.append(P.dma("gp", o[c], xn[:, c, :], wait=[ee]))
        P.commit(f"xn{s}")
        return ee

    def s_moe(self, s, ev_xn):
        P = self.P
        d = self.d
        xn = self.U0
        pb = self.pb
        lt = self.u5f[0:8, 2048:4096]
        ev_lt = None
        for tt in range(NTT):
            m = None
            for ic in range(CH):
                m = P.pe(lambda e, ic=ic, tt=tt: e.matmul(
                    pb[0][0:8, :], self.rwt[:, ic, :],
                    xn[:, ic, tt * 512:(tt + 1) * 512],
                    start=(ic == 0), stop=(ic == CH - 1)),
                    wait=([ev_xn, self.ev_rw, ev_lt] if ic == 0 else None),
                    inc=(ic == CH - 1))
            ev_lt = P.dve(lambda e, tt=tt: e.tensor_scalar_add(
                lt[:, tt * 512:(tt + 1) * 512], pb[0][0:8, :], self.rbias),
                wait=[m, self.ev_rb])
        evd = []
        prev = [None, None, None]
        for tc in range(NTC):
            mt = P.pe(lambda e, tc=tc: e.transpose(
                pb[1][:, 0:8], lt[0:8, tc * 128:(tc + 1) * 128],
                self.eyef[0:8, 0:8]),
                wait=[ev_lt, self.ev_eye] + ([prev[0]] if prev[0] else []),
                inc=True)
            e1 = P.dve(lambda e: e.tensor_copy(self.ltT8, pb[1][:, 0:8]),
                       wait=[mt])
            prev[0] = e1
            P.dve(lambda e: e.reduce_max(self.mx8, self.ltT8, axis=AX))
            e3 = P.dve(lambda e: e.tensor_scalar_mul(self.nmx8, self.mx8, -1.0))
            e4 = P.act(lambda e: e.activation(self.et8, self.ltT8, AF.Exp,
                                              bias=self.nmx8, scale=1.0),
                       wait=[e3])
            P.dve(lambda e: e.reduce_sum(self.sm8, self.et8, axis=AX),
                  wait=[e4])
            e6 = P.dve(lambda e: e.reciprocal(self.gt8, self.sm8))
            e7 = P.dve(lambda e: e.tensor_single_scalar(
                self.oh8, self.ltT8, self.mx8, AL.is_ge))
            mo = P.pe(lambda e: e.transpose(
                pb[2].bitcast(BF16)[0:8, 0:128], self.oh8, self.eyeb),
                wait=[e7, self.ev_eyeb] + ([prev[1]] if prev[1] else []),
                inc=True)
            mg = P.pe(lambda e: e.transpose(
                pb[3][0:1, 0:128], self.gt8, self.eyef),
                wait=[e6] + ([prev[2]] if prev[2] else []), inc=True)
            e8 = P.dve(lambda e: e.tensor_copy(
                self.t8, pb[2].bitcast(BF16)[0:8, 0:128]),
                wait=[mo] + evd[-2:])
            prev[1] = e8
            e9 = P.dve(lambda e: e.tensor_copy(self.g1x, pb[3][0:1, 0:128]),
                       wait=[mg])
            prev[2] = e9
            evd.append(P.dma("sp", d["disp"][:, tc * 128:(tc + 1) * 128],
                             self.t8, wait=[e8]))
            evd.append(P.dma("sp", d["gatec"][0:1, tc * 128:(tc + 1) * 128],
                             self.g1x, wait=[e9]))
        P.commit(f"rt{s}")
        ev_gr = P.dma("sp", self.rf[0:1, 0:2048], d["gatec"], wait=evd)
        disp8 = self.U7[0:8, 2048:4096]
        ev_d8 = P.dma("sp", disp8, d["disp"], wait=evd)
        xm = self.U5.rearrange("p (c f) -> p c f", f=512)
        mkb8 = self.U5[:, 8192:12288].rearrange("p (e f) -> p e f", f=512)
        dtt = self.U5[0:1, 12288:16384].rearrange("o (e f) -> o e f", f=512)
        g0t = self.U5[:, 12800:13312]
        ew = self.U4.rearrange("p (b c f) -> p b c f", b=2, f=1024)
        ev_ew = [None, None]
        ev_free = [None, None]
        last_evac = None
        for tt in range(NTT):
            e_dt = P.dma("sp", dtt,
                         d["disp"][:, tt * 512:(tt + 1) * 512].unsqueeze(0),
                         wait=evd + ([last_evac] if last_evac else []))
            mg = P.pe(lambda e, tt=tt: e.matmul(
                pb[0], self.ones_rf, self.rf[0:1, tt * 512:(tt + 1) * 512],
                start=True, stop=True),
                wait=[ev_gr] + ([last_evac] if last_evac else []), inc=True)
            eg = P.dve(lambda e: e.tensor_copy(self.gb, pb[0]), wait=[mg])
            emk = []
            mkprev = None
            for ee in range(8):
                mk = P.pe(lambda e, ee=ee: e.matmul(
                    pb[1], self.ones_r, dtt[:, ee, :],
                    start=True, stop=True),
                    wait=[e_dt] + ([mkprev] if mkprev else []), inc=True)
                mkprev = P.dve(lambda e, ee=ee: e.tensor_copy(
                    mkb8[:, ee, :], pb[1]), wait=[mk])
                emk.append(mkprev)
            mm_done = None
            for ee in range(8):
                buf = ee % 2
                w = [self.cast_evs["ew"]] + \
                    ([ev_free[buf]] if ev_free[buf] else [])
                ev_ew[buf] = P.dma("gp", ew[:, buf, :, :],
                                   d["ew"][ee].rearrange("(c p) o -> p c o",
                                                         p=128), wait=w)
                exm = []
                for c in range(CH):
                    exm.append(P.pool(lambda e, c=c, tt=tt, ee=ee:
                                      e.tensor_tensor(
                        xm[:, c, :], xn[:, c, tt * 512:(tt + 1) * 512],
                        mkb8[:, ee, :], AL.mult),
                        wait=[emk[ee], ev_xn] + ([mm_done] if mm_done else [])))
                for oc in range(8):
                    for ic in range(CH):
                        st = (ee == 0 and ic == 0)
                        w = [ev_ew[buf], exm[ic]]
                        if st:
                            w.append(emk[7])
                        mm = P.pe(lambda e, oc=oc, ic=ic, st=st, buf=buf:
                                  e.matmul(
                            pb[oc], ew[:, buf, ic, oc * 128:(oc + 1) * 128],
                            xm[:, ic, :], start=st, stop=False),
                            wait=w, inc=(oc == 7 and ic == CH - 1))
                mm_done = mm
                ev_free[buf] = mm
            evE = []
            for oc in range(8):
                evE.append(P.pe(lambda e, oc=oc, tt=tt: e.matmul(
                    pb[oc], self.ebias[0:8, oc * 128:(oc + 1) * 128],
                    disp8[:, tt * 512:(tt + 1) * 512],
                    start=False, stop=True),
                    wait=[ev_d8, self.ev_eb, mm_done], inc=True))
            prev_e2 = getattr(self, "_moe_prev_e2", None)
            for oc in range(8):
                e1 = P.dve(lambda e, oc=oc: e.scalar_tensor_tensor(
                    self.sb512, pb[oc], 1.0, self.gb, AL.mult, AL.mult),
                    wait=[evE[oc], eg] + ([prev_e2] if prev_e2 else []))
                e2 = P.act(lambda e: e.activation(
                    self.mkb, self.sb512, AF.Silu, bias=0.0, scale=1.0),
                    wait=[e1] + ([last_evac] if last_evac else []))
                prev_e2 = e2
                last_evac = P.dma("sp", d["g0"][oc][:, tt * 512:(tt + 1) * 512],
                                  self.mkb, wait=[e2])
            self._moe_prev_e2 = prev_e2
        if "g0" in self.dbg:
            o = self.dbg_dram(f"g0_{s}", [CH, 128, T], BF16)
            self.out_evs.append(P.dma("gp", o, d["g0"], wait=[last_evac]))
        P.commit(f"moe{s}")
        return last_evac

    def s_proj(self, s, xsrc, ev_x, wq_ap, wk_ap, wv_ap, dq, dk, dv, vmode):
        P = self.P
        pb = self.pb
        wv = self.U6.rearrange("p (c f) -> p c f", f=1024)
        ev_wv = P.dma("sp", wv, wv_ap.rearrange("(c p) o -> p c o", p=128))
        vstage = self.HC[:, 0:1040].rearrange("p (h dd) -> p h dd", dd=65)
        em = P.pool(lambda e: e.memset(vstage[:, :, 64:65], 1.0))
        ev_vdma = None
        prev_ev = [None, None]
        for tc in range(NTC):
            evh = []
            for half in range(2):
                b = half
                m = None
                for ic in range(CH):
                    w = ([ev_wv, ev_x] if ic == 0 else None)
                    if ic == 0 and prev_ev[b] is not None:
                        w = w + [prev_ev[b]]
                    m = P.pe(lambda e, ic=ic, tc=tc, half=half, b=b: e.matmul(
                        pb[b], xsrc[:, ic, tc * 128:(tc + 1) * 128],
                        wv[:, ic, half * 512:(half + 1) * 512],
                        start=(ic == 0), stop=(ic == CH - 1)),
                        wait=w, inc=(ic == CH - 1))
                if vmode == "L0":
                    srcap = pb[b].rearrange("p (dd h) -> p dd h", h=16)
                    dstap = vstage[:, :, 32 * half:32 * half + 32].rearrange(
                        "p h dd -> p dd h")
                else:
                    srcap = pb[b].rearrange("p (h dd) -> p h dd", dd=64)
                    dstap = vstage[:, 8 * half:8 * half + 8, 0:64]
                ee = P.dve(lambda e, dst=dstap, sr=srcap: e.tensor_copy(
                    dst, sr), wait=[m, ev_vdma] if ev_vdma else [m])
                prev_ev[b] = ee
                evh.append(ee)
            ev_vdma = P.dma("sp", dv[:, tc, :, :].rearrange("h p dd -> p h dd"),
                            vstage, wait=evh + [em])
        qsl = self.U6[:, 0:4096].rearrange("p (b m c f) -> p b m c f", b=2, m=2, f=128)
        ev_sl = [None, None]
        prev_mm = [None, None]
        ev_qdma = ev_kdma = None
        prev_evq = [None] * NTT
        prev_evk = [None] * NTT
        for oc in range(CH):
            b = oc % 2
            w = [ev_vdma] + ([prev_mm[b]] if prev_mm[b] else [])
            ev_sl[b] = P.dma_multi("sp", [
                (qsl[:, b, 0, :, :], wq_ap(oc).rearrange("(c p) o -> p c o", p=128)),
                (qsl[:, b, 1, :, :], wk_ap(oc).rearrange("(c p) o -> p c o", p=128)),
            ], wait=w)
            evq = []
            evk = []
            mlast = None
            for tt in range(NTT):
                mq = mk = None
                for ic in range(CH):
                    w0 = None
                    if ic == 0:
                        w0 = [ev_sl[b], ev_x]
                        if oc == 0:
                            w0 += [x for x in prev_ev if x]
                        if prev_evq[tt]:
                            w0.append(prev_evq[tt])
                    mq = P.pe(lambda e, ic=ic, tt=tt, b=b: e.matmul(
                        pb[tt], qsl[:, b, 0, ic, :],
                        xsrc[:, ic, tt * 512:(tt + 1) * 512],
                        start=(ic == 0), stop=(ic == CH - 1)),
                        wait=w0, inc=(ic == CH - 1))
                for ic in range(CH):
                    w0 = [prev_evk[tt]] if (ic == 0 and prev_evk[tt]) else None
                    mk = P.pe(lambda e, ic=ic, tt=tt, b=b: e.matmul(
                        pb[4 + tt], qsl[:, b, 1, ic, :],
                        xsrc[:, ic, tt * 512:(tt + 1) * 512],
                        start=(ic == 0), stop=(ic == CH - 1)),
                        wait=w0, inc=(ic == CH - 1))
                wq = [mq] + ([ev_qdma] if (tt == 0 and ev_qdma) else [])
                prev_evq[tt] = P.dve(lambda e, tt=tt: e.tensor_copy(
                    self.HA[:, tt * 512:(tt + 1) * 512], pb[tt]), wait=wq)
                evq.append(prev_evq[tt])
                wk = [mk] + ([ev_kdma] if (tt == 0 and ev_kdma) else [])
                prev_evk[tt] = P.dve(lambda e, tt=tt: e.tensor_copy(
                    self.HB[:, tt * 512:(tt + 1) * 512], pb[4 + tt]), wait=wk)
                evk.append(prev_evk[tt])
                mlast = mk
            prev_mm[b] = mlast
            ev_qdma = P.dma("sp", dq[oc], self.HA, wait=evq)
            ev_kdma = P.dma("sp", dk[oc], self.HB, wait=evk)
        P.commit(f"proj{s}_{vmode}")
        return ev_qdma, ev_kdma, ev_vdma

    def s_attn0(self, s, ev_q, ev_k, ev_v, ev_g0, ev_xn_user):
        P = self.P
        d = self.d
        pb = self.pb
        ek = self.U4.rearrange("p (sc f) -> p sc f", f=512)
        qkvt = self.U4[:, 8192:12288].rearrange("p (c f) -> p c f", f=512)
        g0t = self.U4[:, 0:4096].rearrange("p (c f) -> p c f", f=512)
        xbt = self.U4[:, 4096:8192].rearrange("p (c f) -> p c f", f=512)
        m0 = self.U7[:, 0:2048].rearrange("p (j f) -> p j f", f=512)
        sgf = self.u5f[:, 0:4096].rearrange("p (c f) -> p c f", f=512)
        xTf = self.u5f[:, 4096:8192].rearrange("p (sub f) -> p sub f", f=1024)
        woT = self.U6.rearrange("p (c f) -> p c f", f=1024)
        rbs = lambda o, n: self.rb[0:1, o:o + n]
        rfs = lambda o, n: self.rf[0:1, o:o + n]
        ev_hd1 = P.pool(lambda e: e.memset(self.HD[:, :, :, 64:65], 1.0))
        self._ev_hd1 = ev_hd1
        self.ev_m0 = P.dma(
            "gp", self.U7[:, 0:2048].rearrange("p (j f) -> p j f", f=512),
            self.i["C_MASK0"].rearrange("j p f -> p j f"))
        ev_wo = P.dma("sp", woT, d["woT"].rearrange("(c p) o -> p c o", p=128),
                      wait=[self.pret_ev])
        ev_wog = None
        for half in range(2):
            mw = None
            for ic in range(CH):
                mw = P.pe(lambda e, ic=ic, half=half: e.matmul(
                    pb[6][0:1, :], self.gam16[:, ic:ic + 1],
                    woT[:, ic, half * 512:(half + 1) * 512],
                    start=(ic == 0), stop=(ic == CH - 1)),
                    wait=([ev_wo, self.ev_gam16, ev_wog] if ic == 0 else None),
                    inc=(ic == CH - 1))
            P.dve(lambda e, half=half: e.tensor_copy(
                rbs(6144 + half * 512, 512), pb[6][0:1, :]), wait=[mw])
            mb = None
            for ic in range(CH):
                mb = P.pe(lambda e, ic=ic, half=half: e.matmul(
                    pb[7][0:1, :], self.bet16[:, ic:ic + 1],
                    woT[:, ic, half * 512:(half + 1) * 512],
                    start=(ic == 0), stop=(ic == CH - 1)),
                    wait=([self.ev_bet16] if ic == 0 else None),
                    inc=(ic == CH - 1))
            ev_wog = P.dve(lambda e, half=half: e.tensor_copy(
                rbs(7168 + half * 512, 512), pb[7][0:1, :]), wait=[mb])
        prev_x1dma = None
        prev_sg = [None] * CH
        prev_tr = None
        for TT in range(NTT):
            nsc = 4 * TT + 4
            exp_ev = [None, None, None]
            prev_av = None
            prev_qkvt_user = getattr(self, "_wo_done", None)
            prev_dinv = None
            for half in range(2):
                for hpi in range(4):
                    hp = half * 4 + hpi
                    wl = [ev_q, ev_k, ev_v]
                    if prev_av:
                        wl.append(prev_av)
                    ha = P.dma_multi("sp", [
                        (self.HA[:, 0:512],
                         d["qs"][hp][:, TT * 512:(TT + 1) * 512]),
                        (self.HB[:, 0:nsc * 128], d["ks"][hp][:, 0:nsc * 128]),
                    ], wait=wl)
                    hv = P.dma_multi("sp", [
                        (self.HD[:, 0, 0:nsc, 0:64],
                         d["vs"][2 * hp][0:nsc].rearrange("tc p dd -> p tc dd")[:, :, 0:64]),
                        (self.HD[:, 1, 0:nsc, 0:64],
                         d["vs"][2 * hp + 1][0:nsc].rearrange("tc p dd -> p tc dd")[:, :, 0:64]),
                    ], wait=wl + [self._ev_hd1])
                    ek2 = [ek, self.U5.rearrange("p (sc f) -> p sc f",
                                                 f=512)]
                    all_eks = [[], []]
                    for hi in range(2):
                        hr = hi * 64
                        for sc in range(nsc):
                            bk = (hi * nsc + sc) % 3
                            diag = (sc >= 4 * TT)
                            w = [ha] + ([exp_ev[bk]] if exp_ev[bk] else [])
                            mm = P.pe(lambda e, hr=hr, sc=sc, bk=bk,
                                      diag=diag: e.matmul(
                                pb[bk],
                                self.HB[hr:hr + 64, sc * 128:(sc + 1) * 128],
                                self.HA[hr:hr + 64, 0:512],
                                start=True, stop=(not diag),
                                tile_position=(hr, 0)), wait=w,
                                inc=(not diag))
                            if diag:
                                j = sc - 4 * TT
                                mm = P.pe(lambda e, j=j, bk=bk: e.matmul(
                                    pb[bk], self.eyeb, m0[:, j, :],
                                    start=False, stop=True),
                                    wait=[self.ev_m0, self.ev_eyeb], inc=True)
                            ee = P.act(lambda e, sc=sc, bk=bk, hi=hi,
                                       e2=ek2[hi]: e.activation(
                                e2[:, sc, :], pb[bk], AF.Exp,
                                bias=-SHIFT0, scale=0.125),
                                wait=[mm] + ([prev_av] if (sc == 0 and hi == 0
                                                           and prev_av)
                                             else []))
                            exp_ev[bk] = ee
                            all_eks[hi].append(ee)
                    for hi in range(2):
                        h = 2 * hp + hi
                        hr = hi * 64
                        eks = all_eks[hi]
                        mv = None
                        for sc in range(nsc):
                            w = [eks[sc], hv]
                            if sc == 0 and prev_av:
                                w.append(prev_av)
                            if sc == 0 and prev_qkvt_user:
                                w.append(prev_qkvt_user)
                                prev_qkvt_user = None
                            mv = P.pe(lambda e, hi=hi, sc=sc, nsc=nsc:
                                      e.matmul(
                                pb[4][0:65, :], self.HD[:, hi, sc, :],
                                ek2[hi][:, sc, :], start=(sc == 0),
                                stop=(sc == nsc - 1)), wait=w,
                                inc=(sc == nsc - 1))
                        hc = h // 2
                        P.dve(lambda e, hr=hr, hc=hc: e.tensor_copy(
                            qkvt[hr:hr + 64, hc, :], pb[4][0:64, :]),
                            wait=[mv])
                        prev_av = P.dve(lambda e, h=h: e.tensor_copy(
                            rfs((h % 8) * 512, 512), pb[4][64:65, :]))
                einv = None
                for piece in range(2):
                    eL = P.act(lambda e, piece=piece: e.activation(
                        rfs(4096, 2048), rfs(piece * 2048, 2048),
                        AF.Ln, bias=0.0, scale=1.0), wait=[prev_av, einv])
                    einv = P.act(lambda e, piece=piece: e.activation(
                        rbs(piece * 2048, 2048), rfs(4096, 2048),
                        AF.Exp, bias=0.0, scale=-1.0), wait=[eL])
                for cc in range(4):
                    c = half * 4 + cc
                    w = [einv] + ([prev_dinv] if prev_dinv else [])
                    mA = P.pe(lambda e, cc=cc: e.matmul(
                        pb[5][0:64, :], self.ones_r[0:1, 0:64],
                        rbs((2 * cc) * 512, 512),
                        start=True, stop=True), wait=w, inc=False)
                    mB = P.pe(lambda e, cc=cc: e.matmul(
                        pb[5][64:128, :], self.ones_r[0:1, 0:64],
                        rbs((2 * cc + 1) * 512, 512),
                        start=True, stop=True,
                        tile_position=(0, 64)), inc=True)
                    prev_dinv = P.dve(lambda e, c=c: e.scalar_tensor_tensor(
                        qkvt[:, c, :], qkvt[:, c, :], 1.0, pb[5],
                        AL.bypass, AL.mult), wait=[mB])
            ev_qt = prev_dinv
            sqt = self.U5.rearrange("p (c f) -> p c f", f=512)
            esq = P.pool(lambda e: e.tensor_tensor(
                self.U5[:, 0:4096], self.U4[:, 8192:12288],
                self.U4[:, 8192:12288], AL.mult),
                wait=[ev_qt] + [x for x in prev_sg if x])
            mS = mQ = None
            for c in range(CH):
                mS = P.pe(lambda e, c=c: e.matmul(
                    pb[6][0:1, :], self.ones_c, qkvt[:, c, :],
                    start=(c == 0), stop=(c == CH - 1)),
                    wait=[ev_qt] if c == 0 else None, inc=(c == CH - 1))
                mQ = P.pe(lambda e, c=c: e.matmul(
                    pb[7][0:1, :], self.ones_c, sqt[:, c, :],
                    start=(c == 0), stop=(c == CH - 1)),
                    wait=[esq] if c == 0 else None, inc=(c == CH - 1))
            eA = P.dve(lambda e: e.tensor_copy(rfs(4096, 512), pb[6][0:1, :]),
                       wait=[mS])
            eB = P.dve(lambda e: e.tensor_copy(rfs(4608, 512), pb[7][0:1, :]),
                       wait=[mQ])
            eC = P.act(lambda e: e.activation(rfs(5120, 512), rfs(4096, 512),
                                              AF.Copy, bias=0.0, scale=1.0 / E),
                       wait=[eA])
            eD = P.pool(lambda e: e.tensor_tensor(
                rfs(4096, 512), rfs(5120, 512), rfs(5120, 512), AL.mult),
                wait=[eC])
            eE = P.dve(lambda e: e.scalar_tensor_tensor(
                rfs(4608, 512), rfs(4608, 512), 1.0 / E, rfs(4096, 512),
                AL.mult, AL.subtract), wait=[eD, eB])
            eF = P.act(lambda e: e.activation(rfs(4096, 512), rfs(4608, 512),
                                              AF.Ln, bias=EPS_LN, scale=1.0),
                       wait=[eE])
            eG = P.act(lambda e: e.activation(rfs(4608, 512), rfs(4096, 512),
                                              AF.Exp, bias=0.0, scale=-0.5),
                       wait=[eF])
            eH = P.dve(lambda e: e.scalar_tensor_tensor(
                rbs(4096, 512), rfs(5120, 512), -1.0, rfs(4608, 512),
                AL.mult, AL.mult), wait=[eG])
            eI = P.dve(lambda e: e.tensor_copy(rbs(4608, 512), rfs(4608, 512)))
            mR = P.pe(lambda e: e.matmul(
                pb[5], self.ones_r, rbs(4608, 512), start=True, stop=True),
                wait=[eI, prev_dinv], inc=True)
            ev_af = None
            for c in range(CH):
                ev_af = P.dve(lambda e, c=c: e.scalar_tensor_tensor(
                    qkvt[:, c, :], qkvt[:, c, :], self.gam[:, c:c + 1],
                    pb[5], AL.mult, AL.mult), wait=[mR, self.ev_gam, esq])
            ev_wo_mms = []
            for c in range(CH):
                mw = None
                for ic in range(CH):
                    w = [ev_af] if ic == 0 else None
                    if ic == 0 and prev_sg[c]:
                        w = w + [prev_sg[c]]
                    mw = P.pe(lambda e, c=c, ic=ic: e.matmul(
                        pb[c], woT[:, ic, c * 128:(c + 1) * 128],
                        qkvt[:, ic, :], start=(ic == 0), stop=False), wait=w)
                P.pe(lambda e, c=c: e.matmul(
                    pb[c], rbs(6144 + c * 128, 128), rbs(4096, 512),
                    start=False, stop=False), wait=[eH] if c == 0 else None)
                mw = P.pe(lambda e, c=c: e.matmul(
                    pb[c], rbs(7168 + c * 128, 128), self.ones512,
                    start=False, stop=True), wait=[ev_wog] if c == 0 else None,
                    inc=True)
                ev_wo_mms.append(mw)
            eg0 = P.dma("sp", g0t, d["g0"].rearrange(
                "c p t -> p c t")[:, :, TT * 512:(TT + 1) * 512],
                wait=[ev_g0, exp_ev[0], exp_ev[1], exp_ev[2], prev_av])
            tes = [P.dmaT(xbt[:, c, :],
                          d["xb16"][s][TT * 512:(TT + 1) * 512,
                                       c * 128:(c + 1) * 128],
                          wait=[self.ev_xcast, prev_av, exp_ev[0],
                                exp_ev[1], exp_ev[2]])
                   for c in range(CH)]
            ex = P.dma("sp", xTf,
                       self.i["x"][s][TT * 512:(TT + 1) * 512, :].rearrange(
                           "(sub p) e -> p sub e", p=128),
                       wait=[prev_x1dma, prev_tr] if prev_x1dma else
                       ([prev_tr] if prev_tr else None))
            trs = []
            for c in range(CH):
                e1 = P.dve(lambda e, c=c: e.scalar_tensor_tensor(
                    sgf[:, c, :], pb[c], 1.0, g0t[:, c, :],
                    AL.bypass, AL.mult), wait=[ev_wo_mms[c], eg0])
                prev_sg[c] = e1
                P.pool(lambda e, c=c, TT=TT: e.tensor_tensor(
                    self.U0[:, c, TT * 512:(TT + 1) * 512], xbt[:, c, :],
                    sgf[:, c, :], AL.add), wait=[e1, tes[c], ev_xn_user])
                tr = None
                for sub in range(4):
                    mt = P.pe(lambda e, c=c, sub=sub: e.transpose(
                        pb[c][:, 0:128], sgf[:, c, sub * 128:(sub + 1) * 128],
                        self.eyef), wait=[e1, ex] + ([tr] if tr else []),
                        inc=True)
                    tr = P.dve(lambda e, c=c, sub=sub: e.tensor_tensor(
                        xTf[:, sub, c * 128:(c + 1) * 128],
                        xTf[:, sub, c * 128:(c + 1) * 128],
                        pb[c][:, 0:128], AL.add), wait=[mt])
                    trs.append(tr)
            prev_tr = trs[-1]
            prev_x1dma = P.dma(
                "sp", d["x1T"][s][TT * 512:(TT + 1) * 512, :].rearrange(
                    "(sub p) e -> p sub e", p=128), xTf, wait=trs)
            self._wo_done = prev_sg[7]
            P.commit(f"at{s}_{TT}")
        if "x1T" in self.dbg:
            o = self.dbg_dram(f"x1T{s}", [T, E])
            self.out_evs.append(P.dma("sp", o, d["x1T"][s],
                                      wait=[prev_x1dma]))
        return prev_x1dma

    def s_ln1_y(self, s, ev_x1):
        P = self.P
        ev = self.ln_stats(self.U0, [ev_x1])
        pbA, pbB = self.pb[4], self.pb[5]
        ee = None
        prev = None
        for tt in range(NTT):
            w = [ev] + ([prev] if prev else [])
            mA, mB = self.bcast2(
                self.rb[0:1, tt * 512:(tt + 1) * 512],
                self.rb[0:1, 2048 + tt * 512:2048 + (tt + 1) * 512],
                pbA, pbB, wait=w)
            eB = P.dve(lambda e: e.tensor_copy(self.sb512, pbB), wait=[mB])
            for c in range(CH):
                P.dve(lambda e, c=c, tt=tt: e.tensor_tensor(
                    self.sb512b, self.U0[:, c, tt * 512:(tt + 1) * 512],
                    pbA, AL.mult), wait=[mA, eB])
                ee = P.dve(lambda e, c=c, tt=tt: e.tensor_tensor(
                    self.U0[:, c, tt * 512:(tt + 1) * 512], self.sb512b,
                    self.sb512, AL.add))
            prev = ee
        if "y" in self.dbg:
            o = self.dbg_dram(f"y{s}", [CH, 128, T])
            for c in range(CH):
                self.out_evs.append(P.dma("gp", o[c], self.U0[:, c, :],
                                          wait=[ee]))
        P.commit(f"ln1_{s}")
        return ee


    def s_attn1(self, s, ev_q, ev_k, ev_v):
        P = self.P
        d = self.d
        pb = self.pb
        u5f = self.u5f
        ropeC = u5f[:, 0:2048]
        ropeS = u5f[:, 2048:4096]
        eq = self.U5[:, 8192:12288].rearrange("p (m f) -> p m f", f=T)
        ekT = self.U5[:, 12288:16384].rearrange("p (tc f) -> p tc f", f=256)
        outh_T = self.U4.rearrange("p (tc f) -> p tc f", f=1024)
        Ssb = self.HC[:, 0:130].rearrange("p (m f) -> p m f", f=65)
        ev_rope = P.dma("sp", u5f[:, 0:4096].rearrange(
            "p (j f) -> p j f", f=2048),
            self.i["C_ROPE"].rearrange("j p f -> p j f"))
        prev_eq = prev_ek = prev_S = prev_out = None
        prev_load = None
        for h in range(16):
            hc, hr = h // 2, (h % 2) * 64
            wl = [ev_q, ev_k, ev_v] + ([prev_load] if prev_load else [])
            ha = P.dma_multi("sp", [
                (self.HA[0:64, :], d["q1"][hc][hr:hr + 64, :]),
                (self.HA[64:128, :], d["k1"][hc][hr:hr + 64, :]),
                (self.HD[:, 0, :, :], d["v1"][h].rearrange("tc p dd -> p tc dd")),
            ], wait=wl)
            mask = []
            for q_ in range(16):
                mask += [2 * q_ + 1, 2 * q_]
            e1 = P.dve(lambda e: e.stream_shuffle(self.HB, self.HA, mask),
                       wait=[ha] + ([prev_out] if prev_out else []))
            e2 = P.pool(lambda e: e.tensor_tensor(self.HC, self.HA, ropeC,
                                                  AL.mult),
                        wait=[ha, ev_rope] + ([prev_S] if prev_S else []) +
                        ([prev_out] if prev_out else []))
            e3 = P.pool(lambda e: e.tensor_tensor(self.HB, self.HB, ropeS,
                                                  AL.mult), wait=[e1])
            e4 = P.pool(lambda e: e.tensor_tensor(self.HA, self.HC, self.HB,
                                                  AL.add))
            e5 = P.pool(lambda e: e.tensor_tensor(self.HB, self.HA, self.HA,
                                                  AL.mult), wait=[e4])
            for tc in range(NTC):
                mq = P.pe(lambda e, tc=tc: e.matmul(
                    pb[5][:, tc:tc + 1], self.HB[0:64, tc * 128:(tc + 1) * 128],
                    self.ones_c[0:64, :], start=True, stop=True),
                    wait=([e5] + ([prev_eq] if prev_eq else []))
                    if tc == 0 else None, inc=(tc == NTC - 1))
                mk = P.pe(lambda e, tc=tc: e.matmul(
                    pb[6][:, tc:tc + 1], self.HB[64:128, tc * 128:(tc + 1) * 128],
                    self.ones_c[64:128, :], start=True, stop=True,
                    tile_position=(64, 0)),
                    wait=None, inc=(tc == NTC - 1))
            enq = P.dve(lambda e: e.tensor_scalar_mul(self.nq, pb[5][:, 0:16],
                                                      -0.5), wait=[mq])
            enk = P.dve(lambda e: e.tensor_scalar_mul(self.nk, pb[6][:, 0:16],
                                                      -0.5), wait=[mk])
            peq = []
            for mc in range(2):
                for tt in range(NTT):
                    bk = (mc * NTT + tt) % 2
                    w = [e4, self.ev_MT] + ([peq[-2]] if len(peq) >= 2 else [])
                    m1 = P.pe(lambda e, mc=mc, tt=tt, bk=bk, h=h: e.matmul(
                        pb[bk], self.MT[0:64, h, mc * 128:(mc + 1) * 128],
                        self.HA[0:64, tt * 512:(tt + 1) * 512],
                        start=True, stop=True), wait=w, inc=True)
                    ee = P.act(lambda e, mc=mc, tt=tt, bk=bk: e.activation(
                        eq[:, mc, tt * 512:(tt + 1) * 512], pb[bk], AF.Exp,
                        bias=0.0, scale=1.0), wait=[m1, prev_eq])
                    peq.append(ee)
            prev_eq = peq[-1]
            pek = []
            for tc in range(NTC):
                bk = 2 + tc % 2
                w = [e4, self.ev_MT] + ([pek[-2]] if len(pek) >= 2 else [])
                m1 = P.pe(lambda e, tc=tc, bk=bk, h=h: e.matmul(
                    pb[bk][:, 0:256], self.HA[64:128, tc * 128:(tc + 1) * 128],
                    self.MT[64:128, h, :], start=True, stop=True,
                    tile_position=(64, 0)), wait=w, inc=True)
                ee = P.act(lambda e, tc=tc, bk=bk: e.activation(
                    ekT[:, tc, :], pb[bk][:, 0:256], AF.Exp,
                    bias=self.nk[:, tc:tc + 1], scale=1.0),
                    wait=[m1, enk, prev_ek])
                pek.append(ee)
            prev_ek = pek[-1]
            eS = []
            for mc in range(2):
                mS = None
                for tc in range(NTC):
                    w = [pek[tc], ha] + ([prev_S] if (tc == 0 and prev_S) else [])
                    if tc == 0 and mc == 1:
                        w.append(eS[0])
                    mS = P.pe(lambda e, mc=mc, tc=tc: e.matmul(
                        pb[4][:, 0:65], ekT[:, tc, mc * 128:(mc + 1) * 128],
                        self.HD[:, 0, tc, :], start=(tc == 0),
                        stop=(tc == NTC - 1)), wait=w if tc == 0 else [pek[tc]],
                        inc=(tc == NTC - 1))
                eS.append(P.dve(lambda e, mc=mc: e.tensor_copy(
                    Ssb[:, mc, :], pb[4][:, 0:65]), wait=[mS]))
            prev_S = eS[-1]
            for tc in range(NTC):
                m1 = None
                for mc in range(2):
                    w = [eS[mc], peq[-1]]
                    if mc == 0 and prev_out:
                        w.append(prev_out)
                    m1 = P.pe(lambda e, mc=mc, tc=tc: e.matmul(
                        pb[7][:, 0:65],
                        eq[:, mc, tc * 128:(tc + 1) * 128], Ssb[:, mc, :],
                        start=(mc == 0), stop=(mc == 1)), wait=w,
                        inc=(mc == 1))
                fa = P.act(lambda e, tc=tc: e.activation(
                    self.colA, self.nq[:, tc:tc + 1], AF.Exp,
                    bias=0.0, scale=1.0),
                    wait=[enq] + ([prev_out] if prev_out else []))
                u1 = P.dve(lambda e: e.tensor_scalar(
                    self.colB, pb[7][:, 64:65], self.colA, 1.0 / 256.0,
                    AL.mult, AL.mult), wait=[m1, fa])
                u2 = P.dve(lambda e: e.tensor_scalar_max(
                    self.colB, self.colB, 1e-6))
                u3 = P.dve(lambda e: e.reciprocal(self.colC, self.colB))
                u4 = P.dve(lambda e: e.tensor_scalar(
                    self.colB, self.colC, self.colA, 1.0 / 256.0,
                    AL.mult, AL.mult))
                prev_out = P.dve(lambda e, tc=tc, h=h: e.tensor_scalar_mul(
                    outh_T[:, tc, h * 64:(h + 1) * 64], pb[7][:, 0:64],
                    self.colB), wait=[u4])
            prev_load = prev_out
            P.commit(f"l1a{s}_{h}")
        woutT = self.U6.rearrange("p (c f) -> p c f", f=1024)
        ev_wT = P.dma("sp", woutT,
                      d["wout1T"].rearrange("(c p) o -> p c o", p=128),
                      wait=[self.pret_ev])
        outF = self.U5[:, 0:4096].rearrange("p (c f) -> p c f", f=512)
        prev_mm = None
        prev_dma = None
        ev_s1 = None
        last_tr = [None, None]
        for tt in range(NTT):
            etr = []
            for ic in range(4):
                tc = tt * 4 + ic
                for c in range(CH):
                    bk = c % 2
                    w = [x for x in (prev_out, prev_mm, last_tr[bk]) if x]
                    mt = P.pe(lambda e, tc=tc, c=c, bk=bk: e.transpose(
                        pb[bk].bitcast(BF16)[:, 0:128],
                        outh_T[:, tc, c * 128:(c + 1) * 128], self.eyeb),
                        wait=w or None, inc=True)
                    last_tr[bk] = P.dve(lambda e, tc=tc, c=c, ic=ic, bk=bk:
                                        e.tensor_copy(
                        outF[:, c, ic * 128:(ic + 1) * 128],
                        pb[bk].bitcast(BF16)[:, 0:128]), wait=[mt])
                    etr.append(last_tr[bk])
            evs1 = []
            if not hasattr(self, "_wout_ec"):
                self._wout_ec = [None] * 4
            for oc in range(CH):
                mw = None
                for ic in range(CH):
                    w = ([ev_wT] + etr) if ic == 0 else None
                    if ic == 0 and self._wout_ec[oc % 4]:
                        w = w + [self._wout_ec[oc % 4]]
                    mw = P.pe(lambda e, oc=oc, ic=ic: e.matmul(
                        pb[2 + oc % 4], woutT[:, ic, oc * 128:(oc + 1) * 128],
                        outF[:, ic, :], start=(ic == 0), stop=(ic == CH - 1)),
                        wait=w, inc=(ic == CH - 1))
                ec = P.dve(lambda e, oc=oc: e.tensor_copy(self.mkb, pb[2 + oc % 4]),
                           wait=[mw] + ([prev_dma] if prev_dma else []))
                self._wout_ec[oc % 4] = ec
                prev_dma = P.dma("sp", d["s1"][oc][:, tt * 512:(tt + 1) * 512],
                                 self.mkb, wait=[ec])
                evs1.append(prev_dma)
            prev_mm = Ev("pe", P.cnt["pe"], P.blk_idx)
            ev_s1 = evs1[-1]
            P.commit(f"l1aw{s}_{tt}")
        return ev_s1


    def s_ffn1(self, s, ev_y, ev_s1):
        P = self.P
        d = self.d
        pb = self.pb
        hs = self.U4.rearrange("p (c f) -> p c f", f=T)
        vT1 = self.U5[:, 4096:12416].rearrange("p (tc h dd) -> p tc h dd",
                                               h=8, dd=65)
        m1m = self.U7.rearrange("p (j f) -> p j f", f=512)
        ev_m1 = P.dma("gp", m1m, self.i["C_MASK1"].rearrange("j p f -> p j f"),
                      wait=[ev_y])
        hdf = self.HD.rearrange("p a b c -> p (a b c)")
        sqB = hdf[:, 0:2048]
        xpt = self.U5[:, 0:2048].rearrange("p (j f) -> p j f", f=512)
        rfs = lambda o, n: self.rf[0:1, o:o + n]
        rbs = lambda o, n: self.rb[0:1, o:o + n]
        prev_done = ev_s1
        for l in range(2):
            ehs = []
            for c in range(CH):
                w = [ev_y, ev_s1] if (l == 0 and c == 0) else \
                    ([prev_done] if c == 0 else None)
                if c < 4:
                    ehs.append(P.pool(lambda e, c=c: e.tensor_copy(
                        hs[:, c, :], self.U0[:, c, :]), wait=w))
                else:
                    P.pool(lambda e, c=c: e.memset(hs[:, c, 0:1], 0.0), wait=w)
                    ehs.append(P.pool(lambda e, c=c: e.tensor_copy(
                        hs[:, c, 1:T], self.U0[:, c, 0:T - 1])))
            ev_hs = ehs[-1]
            wqk = self.U6.rearrange("p (c f) -> p c f", f=1024)
            ev_wqk = P.dma("sp", wqk, d["wqkv_l"][l][:, 0:1024].rearrange(
                "(c p) o -> p c o", p=128), wait=[self.cast_evs["wqkv_l"],
                                                  prev_done])
            wv = self.U5[:, 0:4096].rearrange("p (c f) -> p c f", f=512)
            ev_wv = P.dma("sp", wv, d["wqkv_l"][l][:, 1024:1536].rearrange(
                "(c p) o -> p c o", p=128), wait=[self.cast_evs["wqkv_l"],
                                                  prev_done])
            em0 = P.pool(lambda e: e.memset(vT1[:, :, :, 64:65], 1.0))
            prev_ev = [None, None]
            for tc in range(NTC):
                b = tc % 2
                m = None
                for ic in range(CH):
                    w = ([ev_wv, ev_hs] + ([prev_ev[b]] if prev_ev[b] else [])) \
                        if ic == 0 else None
                    m = P.pe(lambda e, ic=ic, tc=tc, b=b: e.matmul(
                        pb[b], hs[:, ic, tc * 128:(tc + 1) * 128],
                        wv[:, ic, :], start=(ic == 0), stop=(ic == CH - 1)),
                        wait=w, inc=(ic == CH - 1))
                prev_ev[b] = P.dve(lambda e, tc=tc, b=b: e.tensor_copy(
                    vT1[:, tc, :, 0:64],
                    pb[b].rearrange("p (h dd) -> p h dd", dd=64)),
                    wait=[m, em0])
            ev_v1 = [x for x in prev_ev if x]
            ek1 = self.U5[:, 0:4096].rearrange("p (j f) -> p j f", f=512)
            prev_abuser = None
            prev_osb = None
            for hp in range(4):
                h0, h1 = 2 * hp, 2 * hp + 1
                evq = []
                for hi in range(2):
                    h = 2 * hp + hi
                    for tt in range(NTT):
                        bk = hi * 4 + tt
                        m = None
                        for ic in range(CH):
                            w = None
                            if ic == 0:
                                w = [ev_wqk, ev_hs] + list(ev_v1) + \
                                    ([prev_abuser] if prev_abuser else [])
                            P.pe(lambda e, h=h, ic=ic, tt=tt, bk=bk:
                                 e.matmul(
                                pb[bk][0:64, :],
                                wqk[:, ic, h * 64:h * 64 + 64],
                                hs[:, ic, tt * 512:(tt + 1) * 512],
                                start=(ic == 0), stop=(ic == CH - 1)),
                                wait=w)
                        for ic in range(CH):
                            m = P.pe(lambda e, h=h, ic=ic, tt=tt, bk=bk:
                                     e.matmul(
                                pb[bk][64:128, :],
                                wqk[:, ic, 512 + h * 64:512 + h * 64 + 64],
                                hs[:, ic, tt * 512:(tt + 1) * 512],
                                start=(ic == 0), stop=(ic == CH - 1),
                                tile_position=(0, 64)),
                                inc=(ic == CH - 1))
                        evq.append(P.dve(lambda e, hi=hi, tt=tt, bk=bk:
                                         e.tensor_copy(
                            self.HA[hi * 64:hi * 64 + 64,
                                    tt * 512:(tt + 1) * 512],
                            pb[bk][0:64, :]), wait=[m]))
                        evq.append(P.dve(lambda e, hi=hi, tt=tt, bk=bk:
                                         e.tensor_copy(
                            self.HB[hi * 64:hi * 64 + 64,
                                    tt * 512:(tt + 1) * 512],
                            pb[bk][64:128, :]), wait=[m]))
                eA = P.pool(lambda e: e.tensor_tensor(self.HC, self.HA,
                                                      self.HA, AL.mult),
                            wait=evq + ([prev_osb] if prev_osb else []))
                eB = P.pool(lambda e: e.tensor_tensor(sqB, self.HB, self.HB,
                                                      AL.mult), wait=evq)
                rb_slot = {(0, 0): 0, (0, 1): 2048, (1, 0): 4096, (1, 1): 6144}
                prev_r = None
                for (ti, hi) in [(0, 0), (0, 1), (1, 0), (1, 1)]:
                    sq = self.HC if ti == 0 else sqB
                    esq = eA if ti == 0 else eB
                    ecp = []
                    for tt in range(NTT):
                        mr = P.pe(lambda e, sq=sq, hi=hi, tt=tt: e.matmul(
                            pb[6][0:1, :], self.ones_c[hi * 64:hi * 64 + 64, :],
                            sq[hi * 64:hi * 64 + 64, tt * 512:(tt + 1) * 512],
                            start=True, stop=True,
                            tile_position=(hi * 64, 0)),
                            wait=[esq] + ([prev_r] if prev_r else []),
                            inc=True)
                        prev_r = P.dve(lambda e, tt=tt: e.tensor_copy(
                            rfs(4096 + tt * 512, 512), pb[6][0:1, :]),
                            wait=[mr])
                        ecp.append(prev_r)
                    e1 = P.dve(lambda e: e.tensor_scalar_max(
                        rfs(4096, 2048), rfs(4096, 2048), 1e-24))
                    e2 = P.act(lambda e: e.activation(
                        rfs(0, 2048), rfs(4096, 2048), AF.Ln,
                        bias=0.0, scale=1.0), wait=[e1])
                    e3 = P.act(lambda e: e.activation(
                        rfs(4096, 2048), rfs(0, 2048), AF.Exp,
                        bias=0.0, scale=-0.5), wait=[e2])
                    prev_r = P.dve(lambda e, sl=rb_slot[(ti, hi)]:
                                   e.tensor_copy(rbs(sl, 2048),
                                                 rfs(4096, 2048)),
                                   wait=[e3])
                ev_inv = prev_r
                prev_x = None
                for tt in range(NTT):
                    ext = P.dma("gp", xpt,
                                self.i["C_XPOS"][:, :, tt * 512:(tt + 1) * 512]
                                .rearrange("j p f -> p j f"),
                                wait=[prev_x] if prev_x else None)
                    mqa = P.pe(lambda e, tt=tt: e.matmul(
                        pb[2][0:64, :], self.ones_r[0:1, 0:64],
                        rbs(0 + tt * 512, 512), start=True, stop=True),
                        wait=[ev_inv], inc=True)
                    mqb = P.pe(lambda e, tt=tt: e.matmul(
                        pb[2][64:128, :], self.ones_r[0:1, 0:64],
                        rbs(2048 + tt * 512, 512), start=True, stop=True,
                        tile_position=(0, 64)), inc=True)
                    mka = P.pe(lambda e, tt=tt: e.matmul(
                        pb[3][0:64, :], self.ones_r[0:1, 0:64],
                        rbs(4096 + tt * 512, 512), start=True, stop=True),
                        inc=True)
                    mkb = P.pe(lambda e, tt=tt: e.matmul(
                        pb[3][64:128, :], self.ones_r[0:1, 0:64],
                        rbs(6144 + tt * 512, 512), start=True, stop=True,
                        tile_position=(0, 64)), inc=True)
                    en1 = P.dve(lambda e, tt=tt, l=l: e.scalar_tensor_tensor(
                        self.HA[:, tt * 512:(tt + 1) * 512],
                        self.HA[:, tt * 512:(tt + 1) * 512],
                        self.qs2[:, l:l + 1], pb[2], AL.mult, AL.mult),
                        wait=[mqa, mqb])
                    en2 = P.dve(lambda e, tt=tt, l=l: e.scalar_tensor_tensor(
                        self.HB[:, tt * 512:(tt + 1) * 512],
                        self.HB[:, tt * 512:(tt + 1) * 512],
                        self.ks2[:, l:l + 1], pb[3], AL.mult, AL.mult),
                        wait=[mka, mkb])
                    msa = P.pe(lambda e, tt=tt: e.matmul(
                        pb[4], self.p32, self.HA[:, tt * 512:(tt + 1) * 512],
                        start=True, stop=True),
                        wait=[en1, self.ev_p32], inc=True)
                    msb = P.pe(lambda e, tt=tt: e.matmul(
                        pb[5], self.p32, self.HB[:, tt * 512:(tt + 1) * 512],
                        start=True, stop=True), wait=[en2], inc=True)
                    t1 = P.dve(lambda e, tt=tt: e.tensor_tensor(
                        self.sb512, self.HA[:, tt * 512:(tt + 1) * 512],
                        xpt[:, 0, :], AL.mult), wait=[ext, en1])
                    t2 = P.dve(lambda e, tt=tt: e.scalar_tensor_tensor(
                        self.sb512b, pb[4], 1.0, xpt[:, 1, :],
                        AL.mult, AL.mult), wait=[msa])
                    ea1 = P.dve(lambda e, tt=tt: e.tensor_tensor(
                        self.HA[:, tt * 512:(tt + 1) * 512], self.sb512,
                        self.sb512b, AL.add))
                    t3 = P.dve(lambda e, tt=tt: e.tensor_tensor(
                        self.sb512, self.HB[:, tt * 512:(tt + 1) * 512],
                        xpt[:, 2, :], AL.mult), wait=[en2])
                    t4 = P.dve(lambda e, tt=tt: e.scalar_tensor_tensor(
                        self.sb512b, pb[5], 1.0, xpt[:, 3, :],
                        AL.mult, AL.mult), wait=[msb])
                    prev_x = P.dve(lambda e, tt=tt: e.tensor_tensor(
                        self.HB[:, tt * 512:(tt + 1) * 512], self.sb512,
                        self.sb512b, AL.add))
                exp_ev = [None, None]
                prev_av = None
                osb_ev = []
                for hi in range(2):
                    h = 2 * hp + hi
                    for wi in range(NTT):
                        jcs = list(range(4, 8)) if wi == 0 else list(range(8))
                        eks = {}
                        for jc in jcs:
                            kc = 4 * (wi - 1) + jc
                            bk = jc % 2
                            w = [prev_x] + ([exp_ev[bk]] if exp_ev[bk] else [])
                            P.pe(lambda e, hi=hi, jc=jc, kc=kc, wi=wi, bk=bk:
                                 e.matmul(
                                pb[bk],
                                self.HB[hi * 64:hi * 64 + 64,
                                        kc * 128:(kc + 1) * 128],
                                self.HA[hi * 64:hi * 64 + 64,
                                        wi * 512:(wi + 1) * 512],
                                start=True, stop=False,
                                tile_position=(hi * 64, 0)), wait=w)
                            mm = P.pe(lambda e, jc=jc, bk=bk: e.matmul(
                                pb[bk], self.eyeb, m1m[:, jc, :],
                                start=False, stop=True),
                                wait=[ev_m1, self.ev_eyeb], inc=True)
                            ee = P.act(lambda e, jc=jc, bk=bk: e.activation(
                                ek1[:, jc, :], pb[bk], AF.Exp,
                                bias=-SHIFT1, scale=8.0), wait=[mm])
                            exp_ev[bk] = ee
                            eks[jc] = ee
                        mv = None
                        first = True
                        for jc in jcs:
                            kc = 4 * (wi - 1) + jc
                            w = [eks[jc]] + ([prev_av] if (first and prev_av)
                                             else [])
                            if hi == 0:
                                mv = P.pe(lambda e, h=h, kc=kc, jc=jc,
                                          first=first: e.matmul(
                                    pb[4][0:65, :], vT1[:, kc, h, :],
                                    ek1[:, jc, :], start=first,
                                    stop=(jc == jcs[-1])), wait=w,
                                    inc=(jc == jcs[-1]))
                            else:
                                mv = P.pe(lambda e, h=h, kc=kc, jc=jc,
                                          first=first: e.matmul(
                                    pb[5][64:128, :], vT1[:, kc, h, 0:64],
                                    ek1[:, jc, :], start=first,
                                    stop=(jc == jcs[-1]),
                                    tile_position=(0, 64)), wait=w,
                                    inc=False)
                                mv = P.pe(lambda e, h=h, kc=kc, jc=jc,
                                          first=first: e.matmul(
                                    pb[6][0:1, :], vT1[:, kc, h, 64:65],
                                    ek1[:, jc, :], start=first,
                                    stop=(jc == jcs[-1])), wait=[eks[jc]],
                                    inc=(jc == jcs[-1]))
                            first = False
                        if hi == 0:
                            prev_av = P.dve(lambda e, wi=wi: e.tensor_copy(
                                rfs(wi * 512, 512), pb[4][64:65, :]),
                                wait=[mv])
                        else:
                            prev_av = P.dve(lambda e, wi=wi: e.tensor_copy(
                                rfs(2048 + wi * 512, 512), pb[6][0:1, :]),
                                wait=[mv])
                        if hi == 0:
                            prev_av = P.dve(lambda e, wi=wi: e.tensor_copy(
                                self.HC[0:64, wi * 512:(wi + 1) * 512],
                                pb[4][0:64, :]))
                        else:
                            prev_av = P.dve(lambda e, wi=wi: e.tensor_copy(
                                self.HC[64:128, wi * 512:(wi + 1) * 512],
                                pb[5][64:128, :]))
                        osb_ev.append(prev_av)
                eL = P.act(lambda e: e.activation(rfs(4096, 2048),
                                                  rfs(0, 2048), AF.Ln,
                                                  bias=0.0, scale=1.0),
                           wait=[prev_av])
                eE1 = P.act(lambda e: e.activation(rbs(0, 2048),
                                                   rfs(4096, 2048), AF.Exp,
                                                   bias=0.0, scale=-1.0),
                            wait=[eL])
                eL2 = P.act(lambda e: e.activation(rfs(4096, 2048),
                                                   rfs(2048, 2048), AF.Ln,
                                                   bias=0.0, scale=1.0))
                eE2 = P.act(lambda e: e.activation(rbs(2048, 2048),
                                                   rfs(4096, 2048), AF.Exp,
                                                   bias=0.0, scale=-1.0),
                            wait=[eL2])
                prev_bc = None
                eo = []
                for wi in range(NTT):
                    ma = P.pe(lambda e, wi=wi: e.matmul(
                        pb[7][0:64, :], self.ones_r[0:1, 0:64],
                        rbs(wi * 512, 512), start=True, stop=True),
                        wait=[eE1, eE2] + ([prev_bc] if prev_bc else []),
                        inc=True)
                    mb = P.pe(lambda e, wi=wi: e.matmul(
                        pb[7][64:128, :], self.ones_r[0:1, 0:64],
                        rbs(2048 + wi * 512, 512), start=True, stop=True,
                        tile_position=(0, 64)), inc=True)
                    ec = P.dve(lambda e: e.tensor_copy(self.sb512, pb[7]),
                               wait=[ma, mb] + ([eo[-1]] if eo else []))
                    prev_bc = ec
                    eo.append(P.pool(lambda e, wi=wi: e.tensor_tensor(
                        self.HC[:, wi * 512:(wi + 1) * 512],
                        self.HC[:, wi * 512:(wi + 1) * 512],
                        self.sb512, AL.mult), wait=[ec] + osb_ev))
                prev_osb = P.dma("sp", d["ybk"][hp], self.HC, wait=eo)
                prev_abuser = eo[-1]
                if hp == 3:
                    ev_opairs = prev_osb
            ob = self.U6.rearrange("p (c f) -> p c f", f=T)[:, 0:4, :]
            ev_ob = P.dma("sp", self.U6.rearrange("p (c f) -> p c f", f=T),
                          d["ybk"], wait=[prev_osb])
            wo_sb = self.U5[:, 0:4096].rearrange("p (c f) -> p c f", f=1024)
            ev_wos = P.dma("sp", wo_sb, d["wo_l"][l].rearrange(
                "(c p) o -> p c o", p=128), wait=[self.cast_evs["wo_l"],
                                                  prev_osb])
            ev_yup = None
            yup_bank = [None] * CH
            for tt in range(NTT):
                for oc in range(CH):
                    mw = None
                    for ic in range(4):
                        w = None
                        if ic == 0:
                            w = [ev_ob, ev_wos] + \
                                ([yup_bank[oc]] if yup_bank[oc] else [])
                        mw = P.pe(lambda e, oc=oc, ic=ic, tt=tt: e.matmul(
                            pb[oc], wo_sb[:, ic, oc * 128:(oc + 1) * 128],
                            ob[:, ic, tt * 512:(tt + 1) * 512],
                            start=(ic == 0), stop=(ic == 3)),
                            wait=w, inc=(ic == 3))
                    ev_yup = P.dve(lambda e, oc=oc, tt=tt: e.scalar_tensor_tensor(
                        self.U0[:, oc, tt * 512:(tt + 1) * 512], pb[oc], 1.0,
                        self.U0[:, oc, tt * 512:(tt + 1) * 512],
                        AL.mult, AL.add), wait=[mw])
                    yup_bank[oc] = ev_yup
            P.commit(f"f1a{s}_{l}")
            ehs = []
            for c in range(CH):
                w = [ev_yup] if c == 0 else None
                if c < 4:
                    ehs.append(P.pool(lambda e, c=c: e.tensor_copy(
                        hs[:, c, :], self.U0[:, c, :]), wait=w))
                else:
                    P.pool(lambda e, c=c: e.memset(hs[:, c, 0:1], 0.0), wait=w)
                    ehs.append(P.pool(lambda e, c=c: e.tensor_copy(
                        hs[:, c, 1:T], self.U0[:, c, 0:T - 1])))
            ev = self.ln_stats(hs, [ehs[-1]])
            pbA, pbB = self.pb[4], self.pb[5]
            eaf = None
            prev = None
            for tt in range(NTT):
                w = [ev] + ([prev] if prev else [])
                mA, mB = self.bcast2(
                    self.rb[0:1, tt * 512:(tt + 1) * 512],
                    self.rb[0:1, 2048 + tt * 512:2048 + (tt + 1) * 512],
                    pbA, pbB, wait=w)
                eBc = P.dve(lambda e: e.tensor_copy(self.sb512, pbB),
                            wait=[mB])
                for c in range(CH):
                    P.dve(lambda e, c=c, tt=tt: e.tensor_tensor(
                        self.sb512b, hs[:, c, tt * 512:(tt + 1) * 512], pbA,
                        AL.mult), wait=[mA, eBc])
                    t2 = P.dve(lambda e: e.tensor_tensor(
                        self.gb, self.sb512b, self.sb512, AL.add))
                    eaf = P.dve(lambda e, c=c, tt=tt, l=l: e.tensor_scalar(
                        hs[:, c, tt * 512:(tt + 1) * 512], self.gb,
                        self.lgam[:, l, c:c + 1], self.lbet[:, l, c:c + 1],
                        AL.mult, AL.add))
                prev = eaf
            P.commit(f"f1b{s}_{l}")
            h3 = self.U5[:, 0:11264].rearrange("p (k f) -> p k f", f=512)
            if True:
                pass
            wsl = self.U6[:, 0:4096].rearrange("p (b m c f) -> p b m c f",
                                               b=2, m=2, f=128)
            ev_w2done = None
            prev_sl = [None, None]
            for tt in range(NTT):
                evh3 = []
                for ocp in range(22):
                    wdt = 128 if ocp < 21 else 42
                    b = ocp % 2
                    acol = ocp * 128
                    gcol = 2730 + ocp * 128
                    w = [self.cast_evs["w1_l"]] + \
                        ([prev_sl[b]] if prev_sl[b] else []) + \
                        ([eaf] if (tt == 0 and ocp < 2) else [])
                    esl = P.dma_multi("sp", [
                        (wsl[:, b, 0, :, 0:wdt],
                         d["w1_l"][l][:, acol:acol + wdt].rearrange(
                             "(c p) o -> p c o", p=128)),
                        (wsl[:, b, 1, :, 0:wdt],
                         d["w1_l"][l][:, gcol:gcol + wdt].rearrange(
                             "(c p) o -> p c o", p=128)),
                    ], wait=w)
                    mA2 = mG = None
                    prev_eh3 = getattr(self, "_w1_prev_eh3", None)
                    prev_eg = getattr(self, "_w1_prev_eg", None)
                    w2ev = getattr(self, "_w2_evac", [None] * CH)
                    for ic in range(CH):
                        w0 = None
                        if ic == 0:
                            w0 = [esl] + ([prev_eh3] if prev_eh3 else []) + \
                                ([w2ev[0]] if (ocp < 2 and w2ev[0]) else [])
                        mA2 = P.pe(lambda e, b=b, ic=ic, tt=tt, wdt=wdt:
                                   e.matmul(
                            pb[0][0:wdt, :], wsl[:, b, 0, ic, 0:wdt],
                            hs[:, ic, tt * 512:(tt + 1) * 512],
                            start=(ic == 0), stop=(ic == CH - 1)),
                            wait=w0, inc=(ic == CH - 1))
                    for ic in range(CH):
                        w0 = None
                        if ic == 0:
                            w0 = ([prev_eg] if prev_eg else []) + \
                                ([w2ev[1]] if (ocp < 2 and w2ev[1]) else [])
                            w0 = w0 or None
                        mG = P.pe(lambda e, b=b, ic=ic, tt=tt, wdt=wdt:
                                  e.matmul(
                            pb[1][0:wdt, :], wsl[:, b, 1, ic, 0:wdt],
                            hs[:, ic, tt * 512:(tt + 1) * 512],
                            start=(ic == 0), stop=(ic == CH - 1)),
                            wait=w0, inc=(ic == CH - 1))
                    prev_sl[b] = mG
                    eg = P.act(lambda e, ocp=ocp, wdt=wdt, l=l: e.activation(
                        self.mkb[0:wdt, :], pb[1][0:wdt, :], AF.Gelu,
                        bias=self.b1c[0:wdt, l, 22 + ocp:23 + ocp],
                        scale=1.0), wait=[mG])
                    eh3 = P.dve(lambda e, ocp=ocp, wdt=wdt, l=l:
                                e.scalar_tensor_tensor(
                        h3[:, ocp, :][0:wdt, :], pb[0][0:wdt, :],
                        self.b1c[0:wdt, l, ocp:ocp + 1], self.mkb[0:wdt, :],
                        AL.add, AL.mult), wait=[mA2, eg])
                    self._w1_prev_eh3 = eh3
                    self._w1_prev_eg = eg
                    evh3.append(eh3)
                prev_w2sl = getattr(self, "_w2_prev_sl", [None] * 4)
                mws = []
                for kc in range(22):
                    wk = 128 if kc < 21 else 42
                    b2 = kc % 4
                    w = [self.cast_evs["w2_l"]] + \
                        ([prev_w2sl[b2]] if prev_w2sl[b2] else [])
                    es2 = P.dma("sp",
                                self.U6[0:wk, 4096 + b2 * 1024:
                                        4096 + b2 * 1024 + 1024],
                                d["w2_l"][l][kc * 128:kc * 128 + wk, :],
                                wait=w)
                    for oc in range(CH):
                        w0 = [es2, evh3[kc]] if oc == 0 else None
                        if kc == 0:
                            w0 = (w0 or []) + [evh3[-1]] + \
                                ([ev_w2done] if ev_w2done else [])
                        mw2 = P.pe(lambda e, kc=kc, oc=oc, b2=b2, wk=wk:
                                   e.matmul(
                            pb[oc],
                            self.U6[0:wk, 4096 + b2 * 1024 + oc * 128:
                                    4096 + b2 * 1024 + (oc + 1) * 128],
                            h3[0:wk, kc, :], start=(kc == 0),
                            stop=(kc == 21)), wait=w0,
                            inc=(kc == 21 or oc == 7))
                        if kc == 21:
                            mws.append(mw2)
                    prev_w2sl[b2] = mw2
                self._w2_prev_sl = prev_w2sl
                if not hasattr(self, "_w2_evac"):
                    self._w2_evac = [None] * CH
                for oc in range(CH):
                    ev_w2done = P.dve(lambda e, oc=oc, tt=tt, l=l:
                                      e.scalar_tensor_tensor(
                        self.U0[:, oc, tt * 512:(tt + 1) * 512], pb[oc],
                        self.b2c[:, l, oc:oc + 1],
                        self.U0[:, oc, tt * 512:(tt + 1) * 512],
                        AL.add, AL.add), wait=[mws[oc]])
                    self._w2_evac[oc] = ev_w2done
            prev_done = ev_w2done
            P.commit(f"f1c{s}_{l}")
        return prev_done

    def s_final(self, s, ev_ffn):
        P = self.P
        d = self.d
        pb = self.pb
        s1t = self.U4[:, 0:4096].rearrange("p (c f) -> p c f", f=512)
        sgf = self.u5f[:, 0:4096].rearrange("p (c f) -> p c f", f=512)
        xTf = self.u5f[:, 4096:8192].rearrange("p (sub f) -> p sub f", f=1024)
        ev_out = None
        prev_tr = None
        prev_sg = None
        prev_s1 = None
        for TT in range(NTT):
            es1 = P.dma("sp", s1t, d["s1"].rearrange(
                "c p t -> p c t")[:, :, TT * 512:(TT + 1) * 512],
                wait=[ev_ffn] + ([prev_s1] if prev_s1 else []))
            prev_s1 = es1
            ex = P.dma("sp", xTf,
                       d["x1T"][s][TT * 512:(TT + 1) * 512, :].rearrange(
                           "(sub p) e -> p sub e", p=128),
                       wait=([prev_tr] if prev_tr else []) +
                       ([ev_out] if ev_out else []))
            trs = []
            for c in range(CH):
                eg1 = P.act(lambda e, c=c, TT=TT: e.activation(
                    self.mkb, self.U0[:, c, TT * 512:(TT + 1) * 512],
                    AF.Silu, bias=0.0, scale=1.0),
                    wait=[ev_ffn] + ([prev_sg] if prev_sg else []))
                e1 = P.dve(lambda e, c=c: e.tensor_tensor(
                    sgf[:, c, :], s1t[:, c, :], self.mkb, AL.mult),
                    wait=[eg1, es1] + ([prev_tr] if prev_tr else []))
                prev_sg = e1
                tr = None
                for sub in range(4):
                    mt = P.pe(lambda e, c=c, sub=sub: e.transpose(
                        pb[c][:, 0:128], sgf[:, c, sub * 128:(sub + 1) * 128],
                        self.eyef), wait=[e1, ex] + ([tr] if tr else []),
                        inc=True)
                    tr = P.dve(lambda e, c=c, sub=sub: e.tensor_tensor(
                        xTf[:, sub, c * 128:(c + 1) * 128],
                        xTf[:, sub, c * 128:(c + 1) * 128],
                        pb[c][:, 0:128], AL.add), wait=[mt])
                    trs.append(tr)
            prev_tr = trs[-1]
            ev_out = P.dma(
                "sp", self.o_out[s][TT * 512:(TT + 1) * 512, :].rearrange(
                    "(sub p) e -> p sub e", p=128), xTf, wait=trs)
            self.out_evs.append(ev_out)
            P.commit(f"fc{s}_{TT}")
        return ev_out

    def finish(self):
        P = self.P
        P.op("pool", lambda e: e.memset(self.fint, 0.0), wait=self.out_evs)
        P.commit("fin")
        return self.nc


def build_l0(dbg=(), nseq=1):
    k = K(dbg=dbg)
    k.prelude()
    d = k.d
    for s in range(nseq):
        ev_xn = k.s_stats_xn(s)
        ev_g0 = k.s_moe(s, ev_xn)
        ev_q, ev_k, ev_v = k.s_proj(
            s, k.U0, ev_xn,
            lambda oc: d["wqT"][:, oc * 128:(oc + 1) * 128],
            lambda oc: d["wkT"][:, oc * 128:(oc + 1) * 128],
            d["wvT"], d["qs"], d["ks"], d["vs"], "L0")
        ev_x1 = k.s_attn0(s, ev_q, ev_k, ev_v, ev_g0, ev_xn)
        k.s_ln1_y(s, ev_x1)
    return k.finish(), k


def build_full(dbg=(), nseq=NSEQ):
    k = K(dbg=dbg)
    k.prelude()
    d = k.d
    for s in range(nseq):
        ev_xn = k.s_stats_xn(s)
        ev_g0 = k.s_moe(s, ev_xn)
        ev_q, ev_k, ev_v = k.s_proj(
            s, k.U0, ev_xn,
            lambda oc: d["wqT"][:, oc * 128:(oc + 1) * 128],
            lambda oc: d["wkT"][:, oc * 128:(oc + 1) * 128],
            d["wvT"], d["qs"], d["ks"], d["vs"], "L0")
        ev_x1 = k.s_attn0(s, ev_q, ev_k, ev_v, ev_g0, ev_xn)
        ev_y = k.s_ln1_y(s, ev_x1)
        ev_q1, ev_k1, ev_v1 = k.s_proj(
            s, k.U0, ev_y,
            lambda oc: d["wqkv1T"][:, oc * 128:(oc + 1) * 128],
            lambda oc: d["wqkv1T"][:, 1024 + oc * 128:1024 + (oc + 1) * 128],
            d["wqkv1T"][:, 2048:3072], d["q1"], d["k1"], d["v1"], "L1")
        ev_s1 = k.s_attn1(s, ev_q1, ev_k1, ev_v1)
        ev_ffn = k.s_ffn1(s, ev_y, ev_s1)
        k.s_final(s, ev_ffn)
    return k.finish(), k


_CACHE = {}


def kernel(**inputs):
    x = np.ascontiguousarray(inputs["L0_attn_x"])
    n_cores = 8
    if "nc" not in _CACHE:
        nc, k = build_full()
        _CACHE["nc"] = nc
    nc = _CACHE["nc"]
    consts = host_consts()
    base = {n: np.ascontiguousarray(v) for n, v in inputs.items()
            if n != "L0_attn_x"}
    base.update(consts)
    in_maps = []
    for c in range(n_cores):
        m = dict(base)
        m["L0_attn_x"] = x[2 * c:2 * c + 2]
        in_maps.append(m)
    res = run_bass_kernel_spmd(nc, in_maps, core_ids=list(range(n_cores)))
    out = np.concatenate([np.asarray(r["OUT"], dtype=np.float32)
                          for r in res.results], axis=0)
    return out


# version 04 (final)
# speedup vs baseline: 1.0301x; 1.0188x over previous
import numpy as np
import concourse.bass as bass
import concourse.mybir as mybir
from concourse.bass_utils import run_bass_kernel_spmd


import concourse.bass as bass
import concourse.mybir as mybir

F32 = mybir.dt.float32
BF16 = mybir.dt.bfloat16
AX = mybir.AxisListType.X
AL = mybir.AluOpType
AF = mybir.ActivationFunctionType

_SECTION = {"pe": "tensor", "dve": "vector", "act": "scalar",
            "pool": "gpsimd", "sp": "sync", "gp": "gpsimd", "actdma": "scalar"}
_QUEUES = ("sp", "gp", "actdma")
N_DMA_SEMS = 40


class Ev:
    __slots__ = ("key", "tgt", "blk", "dma")

    def __init__(self, key, tgt, blk, dma=False):
        self.key, self.tgt, self.blk, self.dma = key, tgt, blk, dma

    def __repr__(self):
        return f"Ev({self.key},{self.tgt},b{self.blk}{'D' if self.dma else ''})"


class Prog:
    def __init__(self, nc):
        self.nc = nc
        self.sem = {}
        self.cnt = {}
        for e in ("pe", "dve", "act", "pool"):
            self.sem[e] = nc.alloc_semaphore("S_" + e)
            self.cnt[e] = 0
        self._dma_pools = {"sp": [], "gp": [], "actdma": []}
        npool = {"sp": 24, "gp": 12, "actdma": 4}
        ii = 0
        for q, n in npool.items():
            for _ in range(n):
                k = f"d{ii}"
                ii += 1
                self.sem[k] = nc.alloc_semaphore("S_" + k)
                self.cnt[k] = 0
                self._dma_pools[q].append(k)
        self._dma_rr = {"sp": 0, "gp": 0, "actdma": 0}
        self.blk_idx = 0
        self._recs = {}
        self._last_wait = {}
        self._cmap = {}

    def setup_consts(self, values):
        vals = sorted(set(float(v) for v in values))
        if not vals:
            return
        self._ctile = self.nc.alloc_sbuf_tensor(
            "constcols", [128, len(vals)], F32).ap()
        for i, v in enumerate(vals):
            col = self._ctile[:, i:i + 1]
            self.op("pool", lambda e, c=col, vv=v: e.memset(c, vv))
            self._cmap[v] = col
            try:
                self.nc.const_aps.register(F32, v, col)
            except Exception:
                pass

    def ccol(self, v):
        return self._cmap[float(v)]

    def _rec(self, eng_name, fn):
        self._recs.setdefault(_SECTION[eng_name], []).append(fn)

    def _emit_waits(self, eng_name, waits):
        sec = _SECTION[eng_name]
        for ev in (waits or []):
            if ev is None:
                continue
            if ev.blk < self.blk_idx and not ev.dma:
                continue
            key = (sec, ev.key)
            if self._last_wait.get(key, -1) >= ev.tgt:
                continue
            self._last_wait[key] = ev.tgt
            sem, tgt = self.sem[ev.key], ev.tgt
            self._rec(eng_name, lambda e, s=sem, t=tgt: e.wait_ge(s, t))

    def op(self, eng_name, fn, wait=None, inc=True):
        self_serial = eng_name in ("dve", "act", "pool")
        wl = list(wait) if wait else []
        if self_serial and self.cnt[eng_name] > 0:
            wl.append(Ev(eng_name, self.cnt[eng_name], self.blk_idx))
        self._emit_waits(eng_name, wl)
        ev = None
        if inc or self_serial:
            self.cnt[eng_name] += 1
            sem, tgt = self.sem[eng_name], self.cnt[eng_name]
            ev = Ev(eng_name, tgt, self.blk_idx)
            self._rec(eng_name, lambda e, s=sem, f=fn: f(e).then_inc(s, 1))
        else:
            self._rec(eng_name, lambda e, f=fn: f(e))
        return ev

    def raw(self, eng_name, fn):
        self._rec(eng_name, fn)

    def dma_multi(self, q, transfers, wait=None, **kw):
        assert q in _QUEUES
        self._emit_waits(q, wait)
        pool = self._dma_pools[q]
        k = pool[self._dma_rr[q]]
        self._dma_rr[q] = (self._dma_rr[q] + 1) % len(pool)
        sem = self.sem[k]
        base = self.cnt[k]
        if base > 0:
            key = (_SECTION[q], k)
            if self._last_wait.get(key, -1) < base:
                self._last_wait[key] = base
                self._rec(q, lambda e, s=sem, t=base: e.wait_ge(s, t))
        n = len(transfers)
        for t in transfers:
            out, in_ = t[0], t[1]
            tkw = dict(kw)
            if len(t) > 2:
                tkw.update(t[2])
            self._rec(q, lambda e, o=out, i=in_, kww=tkw, s=sem:
                      e.dma_start(out=o, in_=i, **kww).then_inc(s, 16))
        self.cnt[k] = base + 16 * n
        return Ev(k, self.cnt[k], self.blk_idx, dma=True)

    def dma(self, q, out, in_, wait=None, **kw):
        return self.dma_multi(q, [(out, in_)], wait=wait, **kw)

    def dmaT_multi(self, transfers, wait=None):
        self._emit_waits("sp", wait)
        pool = self._dma_pools["sp"]
        k = pool[self._dma_rr["sp"]]
        self._dma_rr["sp"] = (self._dma_rr["sp"] + 1) % len(pool)
        sem = self.sem[k]
        base = self.cnt[k]
        if base > 0:
            key = (_SECTION["sp"], k)
            if self._last_wait.get(key, -1) < base:
                self._last_wait[key] = base
                self._rec("sp", lambda e, s=sem, t=base: e.wait_ge(s, t))
        for (out, in_) in transfers:
            self._rec("sp", lambda e, o=out, i=in_, s=sem:
                      e.dma_start_transpose(o, i).then_inc(s, 16))
        self.cnt[k] = base + 16 * len(transfers)
        return Ev(k, self.cnt[k], self.blk_idx, dma=True)

    def dmaT(self, out, in_, wait=None):
        return self.dmaT_multi([(out, in_)], wait=wait)

    def commit(self, name=None):
        if not self._recs:
            return
        recs, self._recs = self._recs, {}
        with self.nc.Block(name) as b:
            for sec, fns in recs.items():
                def body(eng, fns=fns):
                    for f in fns:
                        f(eng)
                getattr(b, sec)(body)
        self.blk_idx += 1
        self._last_wait = {}

    def pe(self, fn, wait=None, inc=False):
        return self.op("pe", fn, wait=wait, inc=inc)

    def dve(self, fn, wait=None, inc=False):
        return self.op("dve", fn, wait=wait, inc=inc)

    def act(self, fn, wait=None, inc=False):
        return self.op("act", fn, wait=wait, inc=inc)

    def pool(self, fn, wait=None, inc=False):
        return self.op("pool", fn, wait=wait, inc=inc)




E = 1024
T = 2048
NSEQ = 2
CH = 8
NTT = 4
NTC = 16
NEG = -1e30
EPS_LN = 1e-5
SHIFT0 = 12.0
SHIFT1 = 30.0
HID = 2730

CONST_SCALARS = [1.0 / E, -1.0, EPS_LN, 1e-6, -0.5, 1e-24, 8.0, 0.125,
                 -SHIFT0, -SHIFT1, 1e-30, 0.5, 1.0 / 4096.0, 1.0 / 256.0]


def host_consts():
    c = {}
    c["C_EYEF"] = np.eye(128, dtype=np.float32)
    p = np.arange(128)[:, None]
    f = np.arange(512)[None, :]
    m = np.zeros((4, 128, 512), np.float32)
    for j in range(4):
        m[j] = np.where(p <= f - 128 * j, 0.0, NEG)
    c["C_MASK0"] = m
    m1 = np.zeros((8, 128, 512), np.float32)
    for jc in range(8):
        keep = (f <= 128 * jc + p) if jc < 4 else (128 * (jc - 4) + p <= f)
        m1[jc] = np.where(keep, 0.0, NEG)
    c["C_MASK1"] = m1
    d = np.arange(128) % 64
    freq = 10000.0 ** (-2.0 * (d // 2) / 64)
    t = np.arange(T)
    ang = np.maximum(t[None, :] - 1, 0) * freq[:, None]
    Cc = np.cos(ang)
    Ss = np.sin(ang)
    sign = np.where(d % 2 == 0, -1.0, 1.0)[:, None]
    Cc[:, 0] = 1.0
    Ss[:, 0] = 0.0
    c["C_ROPE"] = np.stack([Cc, sign * Ss]).astype(np.float32)
    fi = d % 32
    inv_freq = 1.0 / 10000.0 ** (2.0 * fi / 64)
    base = (2.0 * fi + 0.4 * 64) / (1.4 * 64)
    fr = t[None, :] * inv_freq[:, None]
    scale = base[:, None] ** ((t - T // 2) / 256.0)[None, :]
    cos, sin = np.cos(fr), np.sin(fr)
    sgn = np.where(d < 32, -1.0, 1.0)[:, None]
    c["C_XPOS"] = np.stack([cos * scale, sgn * sin * scale,
                            cos / scale, sgn * sin / scale]).astype(np.float32)
    H = np.array([[1.0]])
    while H.shape[0] < 64:
        H = np.block([[H, H], [H, -H]])
    c["C_HAD"] = H.astype(np.float32)
    P32 = np.zeros((128, 128), np.float32)
    for p in range(128):
        blk, off = p // 64, p % 64
        P32[blk * 64 + (off + 32) % 64, p] = 1.0
    c["C_P32"] = P32
    return c


class K:
    def __init__(self, dbg=()):
        self.dbg = set(dbg)
        nc = bass.Bass("TRN2", target_bir_lowering=False)
        self.nc = nc
        self.P = Prog(nc)
        self.out_evs = []
        self.dbg_names = []
        self._io()
        self._sbuf()

    def _io(self):
        nc = self.nc
        ap = lambda n, s, dt=F32, kind="ExternalInput": nc.dram_tensor(
            n, s, dt, kind=kind).ap()
        i = self.i = {}
        i["x"] = ap("L0_attn_x", [NSEQ, T, E])
        for n, s in [("L0_attn_Wq", [E, E]), ("L0_attn_Wk", [E, E]),
                     ("L0_attn_Wv", [E, E]), ("L0_attn_Wo", [E, E]),
                     ("L0_attn_gamma", [E]), ("L0_attn_beta", [E]),
                     ("L0_ffn_router_w", [E, 8]), ("L0_ffn_router_b", [8]),
                     ("L0_ffn_expert_w", [8, E, E]), ("L0_ffn_expert_b", [8, E]),
                     ("L1_attn_w_qkv", [3 * E, E]), ("L1_attn_w_out", [E, E]),
                     ("L1_attn_sorf_signs", [16, 4, 3, 64]),
                     ("L1_attn_sorf_scales", [16, 4, 64]),
                     ("L1_ffn_wqkv", [2, E, 1536]), ("L1_ffn_q_scale", [2, 64]),
                     ("L1_ffn_k_scale", [2, 64]), ("L1_ffn_wo", [2, 512, E]),
                     ("L1_ffn_ln_g", [2, E]), ("L1_ffn_ln_b", [2, E]),
                     ("L1_ffn_w1", [2, E, 5460]), ("L1_ffn_b1", [2, 5460]),
                     ("L1_ffn_w2", [2, HID, E]), ("L1_ffn_b2", [2, E]),
                     ("C_EYEF", [128, 128]), ("C_MASK0", [4, 128, 512]),
                     ("C_MASK1", [8, 128, 512]), ("C_ROPE", [2, 128, T]),
                     ("C_XPOS", [4, 128, T]), ("C_HAD", [64, 64]),
                     ("C_P32", [128, 128])]:
            i[n] = ap(n, s)
        self.o_out = ap("OUT", [NSEQ, T, E], F32, kind="ExternalOutput")
        sc = lambda n, s, dt=BF16: nc.dram_tensor(n, s, dt, kind="Internal").ap()
        d = self.d = {}
        d["xb16"] = sc("xb16", [NSEQ, T, E])
        for n in ("wq_b", "wk_b", "wv_b", "wo_b", "wout1_b"):
            d[n] = sc(n, [E, E])
        d["wqkv1_b"] = sc("wqkv1_b", [3 * E, E])
        for n, s in [("wqT", [E, E]), ("wkT", [E, E]), ("wvT", [E, E]),
                     ("woT", [E, E]), ("wqkv1T", [E, 3 * E]),
                     ("wout1T", [E, E])]:
            d[n] = sc(n, s)
        d["wqkv_l"] = sc("wqkv_l", [2, E, 1536])
        d["wo_l"] = sc("wo_l", [2, 512, E])
        d["w1_l"] = sc("w1_l", [2, E, 5460])
        d["w2_l"] = sc("w2_l", [2, HID, E])
        d["ew"] = sc("ew", [8, E, E])
        d["disp"] = sc("disp", [8, T])
        d["gatec"] = sc("gatec", [1, T], F32)
        d["g0"] = sc("g0", [CH, 128, T])
        d["s1"] = sc("s1", [CH, 128, T])
        d["qs"] = sc("qs", [CH, 128, T])
        d["ks"] = sc("ks", [CH, 128, T])
        d["vs"] = sc("vs", [16, NTC, 128, 65])
        d["q1"] = sc("q1", [CH, 128, T])
        d["k1"] = sc("k1", [CH, 128, T])
        d["v1"] = sc("v1", [16, NTC, 128, 65])
        d["ybk"] = sc("ybk", [4, 128, T])
        d["x1T"] = sc("x1T", [NSEQ, T, E], F32)

    def dbg_dram(self, name, shape, dt=F32):
        a = self.nc.dram_tensor("DBG_" + name, shape, dt,
                                kind="ExternalOutput").ap()
        self.dbg_names.append(name)
        return a

    def _sbuf(self):
        nc = self.nc
        A = lambda n, s, dt=BF16, **kw: nc.alloc_sbuf_tensor(n, s, dt, **kw).ap()
        self.U0 = A("U0", [128, CH, T])
        self.U4 = A("U4", [128, 16384], is_transpose_dest=True)
        self.U5 = A("U5", [128, 16384], is_transpose_dest=True)
        self.U6 = A("U6", [128, 8192], is_transpose_dest=True)
        self.U7 = A("U7", [128, 4096])
        self.HA = A("HA", [128, T])
        self.HB = A("HB", [128, T])
        self.HC = A("HC", [128, T])
        self.HD = A("HD", [128, 2, 16, 65])
        self.HA2 = A("HA2", [128, 512])
        self.HB2 = A("HB2", [128, T])
        self.rf = A("rf", [1, 6144], F32)
        self.rb = A("rb", [1, 8192])
        self.eyef = A("eyef", [128, 128], F32)
        self.eyeb = A("eyeb", [128, 128])
        self.ones_r = A("ones_r", [1, 128])
        self.ones_rf = A("ones_rf", [1, 128], F32)
        self.ones_c = A("ones_c", [128, 1])
        self.ones512 = A("ones512", [1, 512])
        self.gam = A("gam", [128, 8], F32)
        self.rwt = A("rwt", [128, 8, 8])
        self.rbias = A("rbias", [8, 1], F32)
        self.ebias = A("ebias", [8, 1024])
        self.MT = A("MT", [128, 16, 256])
        self.lgam = A("lgam", [128, 2, 8], F32)
        self.lbet = A("lbet", [128, 2, 8], F32)
        self.b1c = A("b1c", [128, 2, 44], F32)
        self.b2c = A("b2c", [128, 2, 8], F32)
        self.qksc = A("qksc", [128, 2], F32)
        self.ltT8 = A("ltT8", [128, 8], F32)
        self.oh8 = A("oh8", [128, 8])
        self.t8 = A("t8", [8, 128])
        self.g1x = A("g1x", [1, 128], F32)
        self.mx8 = A("mx8", [128, 1], F32)
        self.nmx8 = A("nmx8", [128, 1], F32)
        self.et8 = A("et8", [128, 8], F32)
        self.sm8 = A("sm8", [128, 1], F32)
        self.gt8 = A("gt8", [128, 1], F32)
        self.mkb = A("mkb", [128, 512])
        self.gb = A("gb", [128, 512], F32)
        self.sb512 = A("sb512", [128, 512], F32)
        self.sb512b = A("sb512b", [128, 512], F32)
        self.nq = A("nq", [128, 16], F32)
        self.nk = A("nk", [128, 16], F32)
        self.colA = A("colA", [128, 1], F32)
        self.colB = A("colB", [128, 1], F32)
        self.colC = A("colC", [128, 1], F32)
        self.p32 = A("p32", [128, 128])
        self.qs2 = A("qs2", [128, 2], F32)
        self.ks2 = A("ks2", [128, 2], F32)
        self.fint = A("fint", [1, 4], F32)
        self.pb = [nc.alloc_psum_tensor(f"pb{j}", [128, 512], F32).ap()
                   for j in range(8)]
        self.u5f = self.U5.bitcast(F32)
        self.u4f = self.U4.bitcast(F32)

    def P_last(self, eng):
        return Ev(eng, self.P.cnt[eng], self.P.blk_idx)

    def prelude(self):
        P = self.P
        i, d = self.i, self.d
        P.setup_consts(CONST_SCALARS)
        for t_ in (self.ones_r, self.ones_rf, self.ones_c, self.ones512):
            P.pool(lambda e, t=t_: e.memset(t, 1.0))
        P.pool(lambda e: e.memset(self.HD[:, :, :, 64:65], 1.0))
        self.ev_xcast = P.dma("gp", d["xb16"], i["x"])
        ce = self.cast_evs = {}
        for src, dst in [("L0_attn_Wq", "wq_b"), ("L0_attn_Wk", "wk_b"),
                         ("L0_attn_Wv", "wv_b"), ("L0_attn_Wo", "wo_b"),
                         ("L1_attn_w_qkv", "wqkv1_b"),
                         ("L1_attn_w_out", "wout1_b"),
                         ("L0_ffn_expert_w", "ew"), ("L1_ffn_wqkv", "wqkv_l"),
                         ("L1_ffn_wo", "wo_l"), ("L1_ffn_w1", "w1_l"),
                         ("L1_ffn_w2", "w2_l")]:
            ce[dst] = P.dma("gp", d[dst], i[src])
        self.ev_eye = P.dma("sp", self.eyef, i["C_EYEF"])
        self.ev_m0 = P.dma(
            "gp", self.U7[:, 0:2048].rearrange("p (j f) -> p j f", f=512),
            i["C_MASK0"].rearrange("j p f -> p j f"))
        self.ev_eyeb = P.dma("gp", self.eyeb, i["C_EYEF"])
        self.ev_p32 = P.dma("gp", self.p32, i["C_P32"])
        slow = {"allow_slow_non_contiguous": True}
        self.ev_gam = P.dma("sp", self.gam,
                            i["L0_attn_gamma"].rearrange("(c p) -> p c", p=128),
                            **slow)
        self.gam16 = self.nc.alloc_sbuf_tensor("gam16", [128, 8], BF16).ap()
        self.bet16 = self.nc.alloc_sbuf_tensor("bet16", [128, 8], BF16).ap()
        self.ev_gam16 = P.dma("gp", self.gam16,
                              i["L0_attn_gamma"].rearrange("(c p) -> p c", p=128),
                              **slow)
        self.ev_bet16 = P.dma("gp", self.bet16,
                              i["L0_attn_beta"].rearrange("(c p) -> p c", p=128),
                              **slow)
        self.ev_rw = P.dma("gp", self.rwt,
                           i["L0_ffn_router_w"].rearrange("(c p) e -> p c e",
                                                          p=128))
        self.ev_rb = P.dma("sp", self.rbias,
                           i["L0_ffn_router_b"].unsqueeze(1))
        self.ev_eb = P.dma("gp", self.ebias, i["L0_ffn_expert_b"])
        for l in range(2):
            self.ev_lgam = P.dma("sp", self.lgam[:, l, :],
                                 i["L1_ffn_ln_g"][l].rearrange(
                                     "(c p) -> p c", p=128), **slow)
            self.ev_lbet = P.dma("sp", self.lbet[:, l, :],
                                 i["L1_ffn_ln_b"][l].rearrange(
                                     "(c p) -> p c", p=128), **slow)
            self.ev_b1a = P.dma("sp", self.b1c[:, l, 0:21],
                                i["L1_ffn_b1"][l][0:2688].rearrange(
                                    "(c p) -> p c", p=128), **slow)
            P.dma("sp", self.b1c[0:42, l, 21:22],
                  i["L1_ffn_b1"][l][2688:2730].rearrange(
                      "(c p) -> p c", p=42), **slow)
            P.dma("sp", self.b1c[:, l, 22:43],
                  i["L1_ffn_b1"][l][2730:5418].rearrange(
                      "(c p) -> p c", p=128), **slow)
            self.ev_b1b = P.dma("sp", self.b1c[0:42, l, 43:44],
                                i["L1_ffn_b1"][l][5418:5460].rearrange(
                                    "(c p) -> p c", p=42), **slow)
            self.ev_b2c = P.dma("sp", self.b2c[:, l, :],
                                i["L1_ffn_b2"][l].rearrange(
                                    "(c p) -> p c", p=128), **slow)
            self.ev_qks = P.dma("sp", self.qs2[0:64, l:l + 1],
                                i["L1_ffn_q_scale"][l].unsqueeze(1), **slow)
            P.dma("sp", self.qs2[64:128, l:l + 1],
                  i["L1_ffn_q_scale"][l].unsqueeze(1), **slow)
            P.dma("sp", self.ks2[0:64, l:l + 1],
                  i["L1_ffn_k_scale"][l].unsqueeze(1), **slow)
            self.ev_kks = P.dma("sp", self.ks2[64:128, l:l + 1],
                                i["L1_ffn_k_scale"][l].unsqueeze(1), **slow)
        P.commit("pre0")
        for (src, dst) in [("wq_b", "wqT"), ("wk_b", "wkT"), ("wv_b", "wvT"),
                           ("wo_b", "woT"), ("wout1_b", "wout1T"),
                           ("wqkv1_b", "wqkv1T")]:
            self._pretranspose(d[src], d[dst], ce[src])
        self._build_MT()
        P.commit("pre1")

    def _pretranspose(self, src, dst, ev):
        P = self.P
        R, C = src.shape[0], src.shape[1]
        prev = [getattr(self, "_pt_prev0", None), getattr(self, "_pt_prev1", None)]
        for c in range(C // 128):
            buf = c % 2
            st = self.U6[:, buf * 4096:buf * 4096 + R]
            w = [ev] + ([prev[buf]] if prev[buf] else [])
            e1 = P.dmaT(st, src[:, c * 128:(c + 1) * 128], wait=w)
            prev[buf] = P.dma("sp", dst[c * 128:(c + 1) * 128, :], st, wait=[e1])
        self._pt_prev0, self._pt_prev1 = prev[0], prev[1]
        self.pret_ev = prev[(C // 128 - 1) % 2]

    def _build_MT(self):
        P = self.P
        i = self.i
        nc = self.nc
        had = nc.alloc_sbuf_tensor("had", [64, 64], BF16).ap()
        e_h = P.dma("gp", had, i["C_HAD"])
        sg = nc.alloc_sbuf_tensor("sg_", [64, 16, 4, 3], F32).ap()
        e_s = P.dma("sp", sg, i["L1_attn_sorf_signs"].rearrange(
            "h r k p -> p h r k"), allow_slow_non_contiguous=True)
        scl = nc.alloc_sbuf_tensor("scl", [64, 16, 4], F32).ap()
        e_c = P.dma("sp", scl, i["L1_attn_sorf_scales"].rearrange(
            "h r p -> p h r"), allow_slow_non_contiguous=True)
        eye64 = nc.alloc_sbuf_tensor("eye64", [64, 64], F32).ap()
        e_e = P.dma("sp", eye64, i["C_EYEF"][0:64, 0:64])
        tmp = nc.alloc_sbuf_tensor("mtmp", [64, 64], BF16).ap()
        pb = self.pb
        prev_m3 = None
        for h in range(16):
            for r in range(4):
                w0 = [e_e, e_c, e_h, e_s] + ([prev_m3] if prev_m3 else [])
                e0 = P.dve(lambda e, h=h, r=r: e.tensor_scalar(
                    tmp, eye64, scl[:, h:h + 1, r], 1.0 / 4096.0,
                    AL.mult, AL.mult), wait=w0)
                m1 = P.pe(lambda e: e.matmul(pb[6][0:64, 0:64], had, tmp,
                                             start=True, stop=True),
                          wait=[e0], inc=True)
                e1 = P.dve(lambda e, h=h, r=r: e.tensor_scalar_mul(
                    tmp, pb[6][0:64, 0:64], sg[:, h:h + 1, r, 2]), wait=[m1])
                m2 = P.pe(lambda e: e.matmul(pb[7][0:64, 0:64], had, tmp,
                                             start=True, stop=True),
                          wait=[e1], inc=True)
                e2 = P.dve(lambda e, h=h, r=r: e.tensor_scalar_mul(
                    tmp, pb[7][0:64, 0:64], sg[:, h:h + 1, r, 1]), wait=[m2])
                m3 = P.pe(lambda e: e.matmul(pb[6][0:64, 0:64], had, tmp,
                                             start=True, stop=True),
                          wait=[e2], inc=True)
                prev_m3 = m3
                P.dve(lambda e, h=h, r=r: e.tensor_scalar_mul(
                    self.MT[0:64, h, r * 64:(r + 1) * 64],
                    pb[6][0:64, 0:64], sg[:, h:h + 1, r, 0]), wait=[m3])
                self.ev_MT = P.dve(lambda e, h=h, r=r: e.tensor_copy(
                    self.MT[64:128, h, r * 64:(r + 1) * 64],
                    self.MT[0:64, h, r * 64:(r + 1) * 64]))

    def ln_stats(self, xF, wait_evs):
        P = self.P
        pbS, pbQ = self.pb[6], self.pb[7]
        sq = self.U5.rearrange("p (c f) -> p c f", f=512)
        r = lambda o: self.rf[0:1, o:o + 2048]
        rb = lambda o: self.rb[0:1, o:o + 2048]
        evS = evQ = None
        prev_q = None
        for tt in range(NTT):
            sqe = []
            for c in range(CH):
                w = list(wait_evs) if tt == 0 else [prev_q]
                sqe.append(P.pool(lambda e, c=c, tt=tt: e.tensor_tensor(
                    sq[:, c, :], xF[:, c, tt * 512:(tt + 1) * 512],
                    xF[:, c, tt * 512:(tt + 1) * 512], AL.mult), wait=w))
            mS = mQ = None
            for c in range(CH):
                st = (c == 0)
                wS = (list(wait_evs) if tt == 0 else []) + \
                    ([evS] if (st and evS) else [])
                mS = P.pe(lambda e, c=c, tt=tt, st=st: e.matmul(
                    pbS[0:1, :], self.ones_c,
                    xF[:, c, tt * 512:(tt + 1) * 512],
                    start=st, stop=(c == CH - 1)), wait=wS or None,
                    inc=(c == CH - 1))
                wQ = [sqe[c]] + ([evQ] if (st and evQ) else [])
                mQ = P.pe(lambda e, c=c, tt=tt, st=st: e.matmul(
                    pbQ[0:1, :], self.ones_c, sq[:, c, :],
                    start=st, stop=(c == CH - 1)), wait=wQ,
                    inc=(c == CH - 1))
            prev_q = mQ
            evS = P.dve(lambda e, tt=tt: e.tensor_copy(
                self.rf[0:1, tt * 512:(tt + 1) * 512], pbS[0:1, :]), wait=[mS])
            evQ = P.dve(lambda e, tt=tt: e.tensor_copy(
                self.rf[0:1, 2048 + tt * 512:2048 + (tt + 1) * 512],
                pbQ[0:1, :]), wait=[mQ])
        P.act(lambda e: e.activation(rb(4096), r(0), AF.Copy,
                                     bias=0.0, scale=1.0 / E), wait=[evS])
        ea = P.act(lambda e: e.activation(r(4096), r(0), AF.Copy,
                                          bias=0.0, scale=1.0 / E))
        e3 = P.pool(lambda e: e.tensor_tensor(r(0), r(4096), r(4096), AL.mult),
                    wait=[ea])
        e4 = P.dve(lambda e: e.scalar_tensor_tensor(
            r(2048), r(2048), 1.0 / E, r(0), AL.mult, AL.subtract),
            wait=[e3, evQ])
        e5 = P.act(lambda e: e.activation(r(0), r(2048), AF.Ln,
                                          bias=EPS_LN, scale=1.0), wait=[e4])
        e6 = P.act(lambda e: e.activation(r(2048), r(0), AF.Exp,
                                          bias=0.0, scale=-0.5), wait=[e5])
        e7 = P.dve(lambda e: e.scalar_tensor_tensor(
            rb(2048), r(4096), -1.0, r(2048), AL.mult, AL.mult), wait=[e6])
        P.dve(lambda e: e.tensor_copy(r(4096), r(2048)), wait=[e7])
        e9 = P.dve(lambda e: e.tensor_copy(rb(0), r(2048)))
        return e9

    def bcast2(self, rowA, rowB, pbA, pbB, wait=None):
        P = self.P
        mA = P.pe(lambda e: e.matmul(pbA, self.ones_r, rowA,
                                     start=True, stop=True), wait=wait,
                  inc=True)
        mB = P.pe(lambda e: e.matmul(pbB, self.ones_r, rowB,
                                     start=True, stop=True), inc=True)
        return mA, mB

    def s_stats_xn(self, s):
        P = self.P
        d = self.d
        xbF = self.U4.rearrange("p (c f) -> p c f", f=T)
        te = P.dmaT_multi(
            [(xbF[:, c, :], d["xb16"][s][:, c * 128:(c + 1) * 128])
             for c in range(CH)], wait=[self.ev_xcast])
        tes = [te]
        ev = self.ln_stats(xbF, [te])
        xn = self.U0
        pbA, pbB = self.pb[4], self.pb[5]
        ee = None
        prevA = prevB = None
        for tt in range(NTT):
            w = [ev] + ([prevA] if prevA else []) + ([prevB] if prevB else [])
            mA, mB = self.bcast2(
                self.rb[0:1, tt * 512:(tt + 1) * 512],
                self.rb[0:1, 2048 + tt * 512:2048 + (tt + 1) * 512],
                pbA, pbB, wait=w)
            eB = P.dve(lambda e: e.tensor_copy(self.sb512, pbB), wait=[mB])
            for c in range(CH):
                t1 = P.dve(lambda e, c=c, tt=tt: e.tensor_tensor(
                    self.sb512b, xbF[:, c, tt * 512:(tt + 1) * 512], pbA,
                    AL.mult), wait=[mA, eB])
                ee = P.dve(lambda e, c=c, tt=tt: e.tensor_tensor(
                    xn[:, c, tt * 512:(tt + 1) * 512], self.sb512b,
                    self.sb512, AL.add))
            prevA, prevB = ee, ee
        if "xn" in self.dbg:
            o = self.dbg_dram(f"xn{s}", [CH, 128, T])
            for c in range(CH):
                self.out_evs.append(P.dma("gp", o[c], xn[:, c, :], wait=[ee]))
        P.commit(f"xn{s}")
        return ee

    def s_moe(self, s, ev_xn):
        P = self.P
        d = self.d
        xn = self.U0
        pb = self.pb
        lt = self.u5f[0:8, 2048:4096]
        ev_lt = None
        for tt in range(NTT):
            m = None
            for ic in range(CH):
                m = P.pe(lambda e, ic=ic, tt=tt: e.matmul(
                    pb[0][0:8, :], self.rwt[:, ic, :],
                    xn[:, ic, tt * 512:(tt + 1) * 512],
                    start=(ic == 0), stop=(ic == CH - 1)),
                    wait=([ev_xn, self.ev_rw, ev_lt] if ic == 0 else None),
                    inc=(ic == CH - 1))
            ev_lt = P.dve(lambda e, tt=tt: e.tensor_scalar_add(
                lt[:, tt * 512:(tt + 1) * 512], pb[0][0:8, :], self.rbias),
                wait=[m, self.ev_rb])
        evd = []
        prev = [None, None, None]
        for tc in range(NTC):
            mt = P.pe(lambda e, tc=tc: e.transpose(
                pb[1][:, 0:8], lt[0:8, tc * 128:(tc + 1) * 128],
                self.eyef[0:8, 0:8]),
                wait=[ev_lt, self.ev_eye] + ([prev[0]] if prev[0] else []),
                inc=True)
            e1 = P.dve(lambda e: e.tensor_copy(self.ltT8, pb[1][:, 0:8]),
                       wait=[mt])
            prev[0] = e1
            P.dve(lambda e: e.reduce_max(self.mx8, self.ltT8, axis=AX))
            e3 = P.dve(lambda e: e.tensor_scalar_mul(self.nmx8, self.mx8, -1.0))
            e4 = P.act(lambda e: e.activation(self.et8, self.ltT8, AF.Exp,
                                              bias=self.nmx8, scale=1.0),
                       wait=[e3])
            P.dve(lambda e: e.reduce_sum(self.sm8, self.et8, axis=AX),
                  wait=[e4])
            e6 = P.dve(lambda e: e.reciprocal(self.gt8, self.sm8))
            e7 = P.dve(lambda e: e.tensor_single_scalar(
                self.oh8, self.ltT8, self.mx8, AL.is_ge))
            mo = P.pe(lambda e: e.transpose(
                pb[2].bitcast(BF16)[0:8, 0:128], self.oh8, self.eyeb),
                wait=[e7, self.ev_eyeb] + ([prev[1]] if prev[1] else []),
                inc=True)
            mg = P.pe(lambda e: e.transpose(
                pb[3][0:1, 0:128], self.gt8, self.eyef),
                wait=[e6] + ([prev[2]] if prev[2] else []), inc=True)
            e8 = P.dve(lambda e: e.tensor_copy(
                self.t8, pb[2].bitcast(BF16)[0:8, 0:128]),
                wait=[mo] + evd[-2:])
            prev[1] = e8
            e9 = P.dve(lambda e: e.tensor_copy(self.g1x, pb[3][0:1, 0:128]),
                       wait=[mg])
            prev[2] = e9
            evd.append(P.dma("sp", d["disp"][:, tc * 128:(tc + 1) * 128],
                             self.t8, wait=[e8]))
            evd.append(P.dma("sp", d["gatec"][0:1, tc * 128:(tc + 1) * 128],
                             self.g1x, wait=[e9]))
        P.commit(f"rt{s}")
        ev_gr = P.dma("sp", self.rf[0:1, 0:2048], d["gatec"], wait=evd)
        disp8 = self.U7[0:8, 2048:4096]
        ev_d8 = P.dma("sp", disp8, d["disp"], wait=evd)
        xm = self.U5.rearrange("p (c f) -> p c f", f=512)
        mkb8 = self.U5[:, 8192:12288].rearrange("p (e f) -> p e f", f=512)
        dtt = self.U5[0:1, 12288:16384].rearrange("o (e f) -> o e f", f=512)
        g0t = self.U5[:, 12800:13312]
        ew = self.U4.rearrange("p (b c f) -> p b c f", b=2, f=1024)
        ev_ew = [None, None]
        ev_free = [None, None]
        last_evac = None
        for tt in range(NTT):
            e_dt = P.dma("sp", dtt,
                         d["disp"][:, tt * 512:(tt + 1) * 512].unsqueeze(0),
                         wait=evd + ([last_evac] if last_evac else []))
            mg = P.pe(lambda e, tt=tt: e.matmul(
                pb[0], self.ones_rf, self.rf[0:1, tt * 512:(tt + 1) * 512],
                start=True, stop=True),
                wait=[ev_gr] + ([last_evac] if last_evac else []), inc=True)
            eg = P.dve(lambda e: e.tensor_copy(self.gb, pb[0]), wait=[mg])
            emk = []
            mkprev = None
            for ee in range(8):
                mk = P.pe(lambda e, ee=ee: e.matmul(
                    pb[1], self.ones_r, dtt[:, ee, :],
                    start=True, stop=True),
                    wait=[e_dt] + ([mkprev] if mkprev else []), inc=True)
                mkprev = P.dve(lambda e, ee=ee: e.tensor_copy(
                    mkb8[:, ee, :], pb[1]), wait=[mk])
                emk.append(mkprev)
            mm_done = None
            for ee in range(8):
                buf = ee % 2
                w = [self.cast_evs["ew"]] + \
                    ([ev_free[buf]] if ev_free[buf] else [])
                ev_ew[buf] = P.dma("gp", ew[:, buf, :, :],
                                   d["ew"][ee].rearrange("(c p) o -> p c o",
                                                         p=128), wait=w)
                exm = []
                for c in range(CH):
                    exm.append(P.pool(lambda e, c=c, tt=tt, ee=ee:
                                      e.tensor_tensor(
                        xm[:, c, :], xn[:, c, tt * 512:(tt + 1) * 512],
                        mkb8[:, ee, :], AL.mult),
                        wait=[emk[ee], ev_xn] + ([mm_done] if mm_done else [])))
                for oc in range(8):
                    for ic in range(CH):
                        st = (ee == 0 and ic == 0)
                        w = [ev_ew[buf], exm[ic]]
                        if st:
                            w.append(emk[7])
                        mm = P.pe(lambda e, oc=oc, ic=ic, st=st, buf=buf:
                                  e.matmul(
                            pb[oc], ew[:, buf, ic, oc * 128:(oc + 1) * 128],
                            xm[:, ic, :], start=st, stop=False),
                            wait=w, inc=(oc == 7 and ic == CH - 1))
                mm_done = mm
                ev_free[buf] = mm
            evE = []
            for oc in range(8):
                evE.append(P.pe(lambda e, oc=oc, tt=tt: e.matmul(
                    pb[oc], self.ebias[0:8, oc * 128:(oc + 1) * 128],
                    disp8[:, tt * 512:(tt + 1) * 512],
                    start=False, stop=True),
                    wait=[ev_d8, self.ev_eb, mm_done], inc=True))
            prev_e2 = getattr(self, "_moe_prev_e2", None)
            for oc in range(8):
                e1 = P.dve(lambda e, oc=oc: e.scalar_tensor_tensor(
                    self.sb512, pb[oc], 1.0, self.gb, AL.mult, AL.mult),
                    wait=[evE[oc], eg] + ([prev_e2] if prev_e2 else []))
                e2 = P.act(lambda e: e.activation(
                    self.mkb, self.sb512, AF.Silu, bias=0.0, scale=1.0),
                    wait=[e1] + ([last_evac] if last_evac else []))
                prev_e2 = e2
                last_evac = P.dma("sp", d["g0"][oc][:, tt * 512:(tt + 1) * 512],
                                  self.mkb, wait=[e2])
            self._moe_prev_e2 = prev_e2
        if "g0" in self.dbg:
            o = self.dbg_dram(f"g0_{s}", [CH, 128, T], BF16)
            self.out_evs.append(P.dma("gp", o, d["g0"], wait=[last_evac]))
        P.commit(f"moe{s}")
        return last_evac

    def s_proj(self, s, xsrc, ev_x, wq_ap, wk_ap, wv_ap, dq, dk, dv, vmode):
        P = self.P
        pb = self.pb
        wv = self.U6.rearrange("p (c f) -> p c f", f=1024)
        ev_wv = P.dma("sp", wv, wv_ap.rearrange("(c p) o -> p c o", p=128))
        vstage = self.HC[:, 0:1040].rearrange("p (h dd) -> p h dd", dd=65)
        em = P.pool(lambda e: e.memset(vstage[:, :, 64:65], 1.0))
        ev_vdma = None
        prev_ev = [None, None]
        for tc in range(NTC):
            evh = []
            for half in range(2):
                b = half
                m = None
                for ic in range(CH):
                    w = ([ev_wv, ev_x] if ic == 0 else None)
                    if ic == 0 and prev_ev[b] is not None:
                        w = w + [prev_ev[b]]
                    m = P.pe(lambda e, ic=ic, tc=tc, half=half, b=b: e.matmul(
                        pb[b], xsrc[:, ic, tc * 128:(tc + 1) * 128],
                        wv[:, ic, half * 512:(half + 1) * 512],
                        start=(ic == 0), stop=(ic == CH - 1)),
                        wait=w, inc=(ic == CH - 1))
                if vmode == "L0":
                    srcap = pb[b].rearrange("p (dd h) -> p dd h", h=16)
                    dstap = vstage[:, :, 32 * half:32 * half + 32].rearrange(
                        "p h dd -> p dd h")
                else:
                    srcap = pb[b].rearrange("p (h dd) -> p h dd", dd=64)
                    dstap = vstage[:, 8 * half:8 * half + 8, 0:64]
                ee = P.dve(lambda e, dst=dstap, sr=srcap: e.tensor_copy(
                    dst, sr), wait=[m, ev_vdma] if ev_vdma else [m])
                prev_ev[b] = ee
                evh.append(ee)
            ev_vdma = P.dma("sp", dv[:, tc, :, :].rearrange("h p dd -> p h dd"),
                            vstage, wait=evh + [em])
        qsl = self.U6[:, 0:4096].rearrange("p (b m c f) -> p b m c f", b=2, m=2, f=128)
        ev_sl = [None, None]
        prev_mm = [None, None]
        ev_qdma = ev_kdma = None
        prev_evq = [None] * NTT
        prev_evk = [None] * NTT
        for oc in range(CH):
            b = oc % 2
            w = [ev_vdma] + ([prev_mm[b]] if prev_mm[b] else [])
            ev_sl[b] = P.dma_multi("sp", [
                (qsl[:, b, 0, :, :], wq_ap(oc).rearrange("(c p) o -> p c o", p=128)),
                (qsl[:, b, 1, :, :], wk_ap(oc).rearrange("(c p) o -> p c o", p=128)),
            ], wait=w)
            evq = []
            evk = []
            mlast = None
            for tt in range(NTT):
                mq = mk = None
                for ic in range(CH):
                    w0 = None
                    if ic == 0:
                        w0 = [ev_sl[b], ev_x]
                        if oc == 0:
                            w0 += [x for x in prev_ev if x]
                        if prev_evq[tt]:
                            w0.append(prev_evq[tt])
                    mq = P.pe(lambda e, ic=ic, tt=tt, b=b: e.matmul(
                        pb[tt], qsl[:, b, 0, ic, :],
                        xsrc[:, ic, tt * 512:(tt + 1) * 512],
                        start=(ic == 0), stop=(ic == CH - 1)),
                        wait=w0, inc=(ic == CH - 1))
                for ic in range(CH):
                    w0 = [prev_evk[tt]] if (ic == 0 and prev_evk[tt]) else None
                    mk = P.pe(lambda e, ic=ic, tt=tt, b=b: e.matmul(
                        pb[4 + tt], qsl[:, b, 1, ic, :],
                        xsrc[:, ic, tt * 512:(tt + 1) * 512],
                        start=(ic == 0), stop=(ic == CH - 1)),
                        wait=w0, inc=(ic == CH - 1))
                wq = [mq] + ([ev_qdma] if (tt == 0 and ev_qdma) else [])
                prev_evq[tt] = P.dve(lambda e, tt=tt: e.tensor_copy(
                    self.HA[:, tt * 512:(tt + 1) * 512], pb[tt]), wait=wq)
                evq.append(prev_evq[tt])
                wk = [mk] + ([ev_kdma] if (tt == 0 and ev_kdma) else [])
                prev_evk[tt] = P.dve(lambda e, tt=tt: e.tensor_copy(
                    self.HB[:, tt * 512:(tt + 1) * 512], pb[4 + tt]), wait=wk)
                evk.append(prev_evk[tt])
                mlast = mk
            prev_mm[b] = mlast
            ev_qdma = P.dma("sp", dq[oc], self.HA, wait=evq)
            ev_kdma = P.dma("sp", dk[oc], self.HB, wait=evk)
        P.commit(f"proj{s}_{vmode}")
        return ev_qdma, ev_kdma, ev_vdma

    def s_attn0(self, s, ev_q, ev_k, ev_v, ev_g0, ev_xn_user):
        P = self.P
        d = self.d
        pb = self.pb
        ek = self.U4.rearrange("p (sc f) -> p sc f", f=512)
        qkvt = self.U4[:, 8192:12288].rearrange("p (c f) -> p c f", f=512)
        g0t = self.U4[:, 0:4096].rearrange("p (c f) -> p c f", f=512)
        xbt = self.U4[:, 4096:8192].rearrange("p (c f) -> p c f", f=512)
        m0 = self.U7[:, 0:2048].rearrange("p (j f) -> p j f", f=512)
        sgf = self.u5f[:, 0:4096].rearrange("p (c f) -> p c f", f=512)
        xTf = self.u5f[:, 4096:8192].rearrange("p (sub f) -> p sub f", f=1024)
        woT = self.U6.rearrange("p (c f) -> p c f", f=1024)
        rbs = lambda o, n: self.rb[0:1, o:o + n]
        rfs = lambda o, n: self.rf[0:1, o:o + n]
        ev_hd1 = P.pool(lambda e: e.memset(self.HD[:, :, :, 64:65], 1.0))
        self._ev_hd1 = ev_hd1
        self.ev_m0 = P.dma(
            "gp", self.U7[:, 0:2048].rearrange("p (j f) -> p j f", f=512),
            self.i["C_MASK0"].rearrange("j p f -> p j f"))
        ev_wo = P.dma("sp", woT, d["woT"].rearrange("(c p) o -> p c o", p=128),
                      wait=[self.pret_ev])
        ev_wog = None
        for half in range(2):
            mw = None
            for ic in range(CH):
                mw = P.pe(lambda e, ic=ic, half=half: e.matmul(
                    pb[6][0:1, :], self.gam16[:, ic:ic + 1],
                    woT[:, ic, half * 512:(half + 1) * 512],
                    start=(ic == 0), stop=(ic == CH - 1)),
                    wait=([ev_wo, self.ev_gam16, ev_wog] if ic == 0 else None),
                    inc=(ic == CH - 1))
            P.dve(lambda e, half=half: e.tensor_copy(
                rbs(6144 + half * 512, 512), pb[6][0:1, :]), wait=[mw])
            mb = None
            for ic in range(CH):
                mb = P.pe(lambda e, ic=ic, half=half: e.matmul(
                    pb[7][0:1, :], self.bet16[:, ic:ic + 1],
                    woT[:, ic, half * 512:(half + 1) * 512],
                    start=(ic == 0), stop=(ic == CH - 1)),
                    wait=([self.ev_bet16] if ic == 0 else None),
                    inc=(ic == CH - 1))
            ev_wog = P.dve(lambda e, half=half: e.tensor_copy(
                rbs(7168 + half * 512, 512), pb[7][0:1, :]), wait=[mb])
        prev_x1dma = None
        prev_sg = [None] * CH
        prev_tr = None
        for TT in range(NTT):
            nsc = 4 * TT + 4
            exp_ev = [None, None, None]
            prev_av = None
            prev_qkvt_user = getattr(self, "_wo_done", None)
            prev_dinv = None
            HAs = [self.HA[:, 0:512], self.HA2]
            HBs = [self.HB, self.HB2]
            HDs = [self.HD, self.HD]
            if not hasattr(self, "_att_set_user"):
                self._att_set_user = [None, None]

            def issue_ha(hp, extra, nsc=nsc, TT=TT):
                b = hp % 2
                wl = [ev_q, ev_k, ev_v] + extra
                u = self._att_set_user[b]
                if u:
                    wl.append(u)
                return P.dma_multi("sp", [
                    (HAs[b], d["qs"][hp][:, TT * 512:(TT + 1) * 512]),
                    (HBs[b][:, 0:nsc * 128], d["ks"][hp][:, 0:nsc * 128]),
                ], wait=wl)

            def issue_hv(hp, extra, nsc=nsc):
                return P.dma_multi("sp", [
                    (self.HD[:, 0, 0:nsc, 0:64],
                     d["vs"][2 * hp][0:nsc].rearrange(
                         "tc p dd -> p tc dd")[:, :, 0:64]),
                    (self.HD[:, 1, 0:nsc, 0:64],
                     d["vs"][2 * hp + 1][0:nsc].rearrange(
                         "tc p dd -> p tc dd")[:, :, 0:64]),
                ], wait=[ev_v, self._ev_hd1] + extra)

            pending = issue_ha(0, [prev_av] if prev_av else [])
            for half in range(2):
                for hpi in range(4):
                    hp = half * 4 + hpi
                    bset = hp % 2
                    ha = pending
                    hv = issue_hv(hp, [prev_av] if prev_av else [])
                    if hp + 1 < 8:
                        pending = issue_ha(hp + 1, [])
                    ek2 = [ek, self.U5.rearrange("p (sc f) -> p sc f",
                                                 f=512)]
                    all_eks = [[], []]
                    for hi in range(2):
                        hr = hi * 64
                        for sc in range(nsc):
                            bk = (hi * nsc + sc) % 3
                            diag = (sc >= 4 * TT)
                            w = [ha] + ([exp_ev[bk]] if exp_ev[bk] else [])
                            mm = P.pe(lambda e, hr=hr, sc=sc, bk=bk,
                                      diag=diag, bset=bset: e.matmul(
                                pb[bk],
                                HBs[bset][hr:hr + 64, sc * 128:(sc + 1) * 128],
                                HAs[bset][hr:hr + 64, 0:512],
                                start=True, stop=(not diag),
                                tile_position=(hr, 0)), wait=w,
                                inc=(not diag))
                            if diag:
                                j = sc - 4 * TT
                                mm = P.pe(lambda e, j=j, bk=bk: e.matmul(
                                    pb[bk], self.eyeb, m0[:, j, :],
                                    start=False, stop=True),
                                    wait=[self.ev_m0, self.ev_eyeb], inc=True)
                            ee = P.act(lambda e, sc=sc, bk=bk, hi=hi,
                                       e2=ek2[hi]: e.activation(
                                e2[:, sc, :], pb[bk], AF.Exp,
                                bias=-SHIFT0, scale=0.125),
                                wait=[mm] + ([prev_av] if (sc == 0 and hi == 0
                                                           and prev_av)
                                             else []))
                            exp_ev[bk] = ee
                            all_eks[hi].append(ee)
                    for hi in range(2):
                        h = 2 * hp + hi
                        hr = hi * 64
                        eks = all_eks[hi]
                        mv = None
                        for sc in range(nsc):
                            w = [eks[sc], hv]
                            if sc == 0 and prev_av:
                                w.append(prev_av)
                            if sc == 0 and prev_qkvt_user:
                                w.append(prev_qkvt_user)
                                prev_qkvt_user = None
                            mv = P.pe(lambda e, hi=hi, sc=sc, nsc=nsc,
                                      bset=bset: e.matmul(
                                pb[4][0:65, :], HDs[bset][:, hi, sc, :],
                                ek2[hi][:, sc, :], start=(sc == 0),
                                stop=(sc == nsc - 1)), wait=w,
                                inc=(sc == nsc - 1))
                        hc = h // 2
                        P.dve(lambda e, hr=hr, hc=hc: e.tensor_copy(
                            qkvt[hr:hr + 64, hc, :], pb[4][0:64, :]),
                            wait=[mv])
                        prev_av = P.dve(lambda e, h=h: e.tensor_copy(
                            rfs((h % 8) * 512, 512), pb[4][64:65, :]))
                    self._att_set_user[bset] = prev_av
                einv = None
                for piece in range(2):
                    eL = P.act(lambda e, piece=piece: e.activation(
                        rfs(4096, 2048), rfs(piece * 2048, 2048),
                        AF.Ln, bias=0.0, scale=1.0), wait=[prev_av, einv])
                    einv = P.act(lambda e, piece=piece: e.activation(
                        rbs(piece * 2048, 2048), rfs(4096, 2048),
                        AF.Exp, bias=0.0, scale=-1.0), wait=[eL])
                for cc in range(4):
                    c = half * 4 + cc
                    w = [einv] + ([prev_dinv] if prev_dinv else [])
                    mA = P.pe(lambda e, cc=cc: e.matmul(
                        pb[5][0:64, :], self.ones_r[0:1, 0:64],
                        rbs((2 * cc) * 512, 512),
                        start=True, stop=True), wait=w, inc=False)
                    mB = P.pe(lambda e, cc=cc: e.matmul(
                        pb[5][64:128, :], self.ones_r[0:1, 0:64],
                        rbs((2 * cc + 1) * 512, 512),
                        start=True, stop=True,
                        tile_position=(0, 64)), inc=True)
                    prev_dinv = P.dve(lambda e, c=c: e.scalar_tensor_tensor(
                        qkvt[:, c, :], qkvt[:, c, :], 1.0, pb[5],
                        AL.bypass, AL.mult), wait=[mB])
            ev_qt = prev_dinv
            sqt = self.U5.rearrange("p (c f) -> p c f", f=512)
            esq = P.pool(lambda e: e.tensor_tensor(
                self.U5[:, 0:4096], self.U4[:, 8192:12288],
                self.U4[:, 8192:12288], AL.mult),
                wait=[ev_qt] + [x for x in prev_sg if x])
            mS = mQ = None
            for c in range(CH):
                mS = P.pe(lambda e, c=c: e.matmul(
                    pb[6][0:1, :], self.ones_c, qkvt[:, c, :],
                    start=(c == 0), stop=(c == CH - 1)),
                    wait=[ev_qt] if c == 0 else None, inc=(c == CH - 1))
                mQ = P.pe(lambda e, c=c: e.matmul(
                    pb[7][0:1, :], self.ones_c, sqt[:, c, :],
                    start=(c == 0), stop=(c == CH - 1)),
                    wait=[esq] if c == 0 else None, inc=(c == CH - 1))
            eA = P.dve(lambda e: e.tensor_copy(rfs(4096, 512), pb[6][0:1, :]),
                       wait=[mS])
            eB = P.dve(lambda e: e.tensor_copy(rfs(4608, 512), pb[7][0:1, :]),
                       wait=[mQ])
            eC = P.act(lambda e: e.activation(rfs(5120, 512), rfs(4096, 512),
                                              AF.Copy, bias=0.0, scale=1.0 / E),
                       wait=[eA])
            eD = P.pool(lambda e: e.tensor_tensor(
                rfs(4096, 512), rfs(5120, 512), rfs(5120, 512), AL.mult),
                wait=[eC])
            eE = P.dve(lambda e: e.scalar_tensor_tensor(
                rfs(4608, 512), rfs(4608, 512), 1.0 / E, rfs(4096, 512),
                AL.mult, AL.subtract), wait=[eD, eB])
            eF = P.act(lambda e: e.activation(rfs(4096, 512), rfs(4608, 512),
                                              AF.Ln, bias=EPS_LN, scale=1.0),
                       wait=[eE])
            eG = P.act(lambda e: e.activation(rfs(4608, 512), rfs(4096, 512),
                                              AF.Exp, bias=0.0, scale=-0.5),
                       wait=[eF])
            eH = P.dve(lambda e: e.scalar_tensor_tensor(
                rbs(4096, 512), rfs(5120, 512), -1.0, rfs(4608, 512),
                AL.mult, AL.mult), wait=[eG])
            eI = P.dve(lambda e: e.tensor_copy(rbs(4608, 512), rfs(4608, 512)))
            mR = P.pe(lambda e: e.matmul(
                pb[5], self.ones_r, rbs(4608, 512), start=True, stop=True),
                wait=[eI, prev_dinv], inc=True)
            ev_af = None
            for c in range(CH):
                ev_af = P.dve(lambda e, c=c: e.scalar_tensor_tensor(
                    qkvt[:, c, :], qkvt[:, c, :], self.gam[:, c:c + 1],
                    pb[5], AL.mult, AL.mult), wait=[mR, self.ev_gam, esq])
            ev_wo_mms = []
            for c in range(CH):
                mw = None
                for ic in range(CH):
                    w = [ev_af] if ic == 0 else None
                    if ic == 0 and prev_sg[c]:
                        w = w + [prev_sg[c]]
                    mw = P.pe(lambda e, c=c, ic=ic: e.matmul(
                        pb[c], woT[:, ic, c * 128:(c + 1) * 128],
                        qkvt[:, ic, :], start=(ic == 0), stop=False), wait=w)
                P.pe(lambda e, c=c: e.matmul(
                    pb[c], rbs(6144 + c * 128, 128), rbs(4096, 512),
                    start=False, stop=False), wait=[eH] if c == 0 else None)
                mw = P.pe(lambda e, c=c: e.matmul(
                    pb[c], rbs(7168 + c * 128, 128), self.ones512,
                    start=False, stop=True), wait=[ev_wog] if c == 0 else None,
                    inc=True)
                ev_wo_mms.append(mw)
            eg0 = P.dma("sp", g0t, d["g0"].rearrange(
                "c p t -> p c t")[:, :, TT * 512:(TT + 1) * 512],
                wait=[ev_g0, exp_ev[0], exp_ev[1], exp_ev[2], prev_av])
            tes = [P.dmaT(xbt[:, c, :],
                          d["xb16"][s][TT * 512:(TT + 1) * 512,
                                       c * 128:(c + 1) * 128],
                          wait=[self.ev_xcast, prev_av, exp_ev[0],
                                exp_ev[1], exp_ev[2]])
                   for c in range(CH)]
            ex = P.dma("sp", xTf,
                       self.i["x"][s][TT * 512:(TT + 1) * 512, :].rearrange(
                           "(sub p) e -> p sub e", p=128),
                       wait=[prev_x1dma, prev_tr] if prev_x1dma else
                       ([prev_tr] if prev_tr else None))
            trs = []
            for c in range(CH):
                e1 = P.dve(lambda e, c=c: e.scalar_tensor_tensor(
                    sgf[:, c, :], pb[c], 1.0, g0t[:, c, :],
                    AL.bypass, AL.mult), wait=[ev_wo_mms[c], eg0])
                prev_sg[c] = e1
                P.pool(lambda e, c=c, TT=TT: e.tensor_tensor(
                    self.U0[:, c, TT * 512:(TT + 1) * 512], xbt[:, c, :],
                    sgf[:, c, :], AL.add), wait=[e1, tes[c], ev_xn_user])
                tr = None
                for sub in range(4):
                    mt = P.pe(lambda e, c=c, sub=sub: e.transpose(
                        pb[c][:, 0:128], sgf[:, c, sub * 128:(sub + 1) * 128],
                        self.eyef), wait=[e1, ex] + ([tr] if tr else []),
                        inc=True)
                    tr = P.dve(lambda e, c=c, sub=sub: e.tensor_tensor(
                        xTf[:, sub, c * 128:(c + 1) * 128],
                        xTf[:, sub, c * 128:(c + 1) * 128],
                        pb[c][:, 0:128], AL.add), wait=[mt])
                    trs.append(tr)
            prev_tr = trs[-1]
            prev_x1dma = P.dma(
                "sp", d["x1T"][s][TT * 512:(TT + 1) * 512, :].rearrange(
                    "(sub p) e -> p sub e", p=128), xTf, wait=trs)
            self._wo_done = prev_sg[7]
            P.commit(f"at{s}_{TT}")
        if "x1T" in self.dbg:
            o = self.dbg_dram(f"x1T{s}", [T, E])
            self.out_evs.append(P.dma("sp", o, d["x1T"][s],
                                      wait=[prev_x1dma]))
        return prev_x1dma

    def s_ln1_y(self, s, ev_x1):
        P = self.P
        ev = self.ln_stats(self.U0, [ev_x1])
        pbA, pbB = self.pb[4], self.pb[5]
        ee = None
        prev = None
        for tt in range(NTT):
            w = [ev] + ([prev] if prev else [])
            mA, mB = self.bcast2(
                self.rb[0:1, tt * 512:(tt + 1) * 512],
                self.rb[0:1, 2048 + tt * 512:2048 + (tt + 1) * 512],
                pbA, pbB, wait=w)
            eB = P.dve(lambda e: e.tensor_copy(self.sb512, pbB), wait=[mB])
            for c in range(CH):
                P.dve(lambda e, c=c, tt=tt: e.tensor_tensor(
                    self.sb512b, self.U0[:, c, tt * 512:(tt + 1) * 512],
                    pbA, AL.mult), wait=[mA, eB])
                ee = P.dve(lambda e, c=c, tt=tt: e.tensor_tensor(
                    self.U0[:, c, tt * 512:(tt + 1) * 512], self.sb512b,
                    self.sb512, AL.add))
            prev = ee
        if "y" in self.dbg:
            o = self.dbg_dram(f"y{s}", [CH, 128, T])
            for c in range(CH):
                self.out_evs.append(P.dma("gp", o[c], self.U0[:, c, :],
                                          wait=[ee]))
        P.commit(f"ln1_{s}")
        return ee


    def s_attn1(self, s, ev_q, ev_k, ev_v):
        P = self.P
        d = self.d
        pb = self.pb
        u5f = self.u5f
        ropeC = u5f[:, 0:2048]
        ropeS = u5f[:, 2048:4096]
        eq = self.U5[:, 8192:12288].rearrange("p (m f) -> p m f", f=T)
        ekT = self.U5[:, 12288:16384].rearrange("p (tc f) -> p tc f", f=256)
        outh_T = self.U4.rearrange("p (tc f) -> p tc f", f=1024)
        Ssb = self.HC[:, 0:130].rearrange("p (m f) -> p m f", f=65)
        ev_rope = P.dma("sp", u5f[:, 0:4096].rearrange(
            "p (j f) -> p j f", f=2048),
            self.i["C_ROPE"].rearrange("j p f -> p j f"))
        prev_eq = prev_ek = prev_S = prev_out = None
        prev_load = None
        for h in range(16):
            hc, hr = h // 2, (h % 2) * 64
            wl = [ev_q, ev_k, ev_v] + ([prev_load] if prev_load else [])
            ha = P.dma_multi("sp", [
                (self.HA[0:64, :], d["q1"][hc][hr:hr + 64, :]),
                (self.HA[64:128, :], d["k1"][hc][hr:hr + 64, :]),
                (self.HD[:, 0, :, :], d["v1"][h].rearrange("tc p dd -> p tc dd")),
            ], wait=wl)
            mask = []
            for q_ in range(16):
                mask += [2 * q_ + 1, 2 * q_]
            e1 = P.dve(lambda e: e.stream_shuffle(self.HB, self.HA, mask),
                       wait=[ha] + ([prev_out] if prev_out else []))
            e2 = P.pool(lambda e: e.tensor_tensor(self.HC, self.HA, ropeC,
                                                  AL.mult),
                        wait=[ha, ev_rope] + ([prev_S] if prev_S else []) +
                        ([prev_out] if prev_out else []))
            e3 = P.pool(lambda e: e.tensor_tensor(self.HB, self.HB, ropeS,
                                                  AL.mult), wait=[e1])
            e4 = P.pool(lambda e: e.tensor_tensor(self.HA, self.HC, self.HB,
                                                  AL.add))
            e5 = P.pool(lambda e: e.tensor_tensor(self.HB, self.HA, self.HA,
                                                  AL.mult), wait=[e4])
            for tc in range(NTC):
                mq = P.pe(lambda e, tc=tc: e.matmul(
                    pb[5][:, tc:tc + 1], self.HB[0:64, tc * 128:(tc + 1) * 128],
                    self.ones_c[0:64, :], start=True, stop=True),
                    wait=([e5] + ([prev_eq] if prev_eq else []))
                    if tc == 0 else None, inc=(tc == NTC - 1))
                mk = P.pe(lambda e, tc=tc: e.matmul(
                    pb[6][:, tc:tc + 1], self.HB[64:128, tc * 128:(tc + 1) * 128],
                    self.ones_c[64:128, :], start=True, stop=True,
                    tile_position=(64, 0)),
                    wait=None, inc=(tc == NTC - 1))
            enq = P.dve(lambda e: e.tensor_scalar_mul(self.nq, pb[5][:, 0:16],
                                                      -0.5), wait=[mq])
            enk = P.dve(lambda e: e.tensor_scalar_mul(self.nk, pb[6][:, 0:16],
                                                      -0.5), wait=[mk])
            peq = []
            for mc in range(2):
                for tt in range(NTT):
                    bk = (mc * NTT + tt) % 2
                    w = [e4, self.ev_MT] + ([peq[-2]] if len(peq) >= 2 else [])
                    m1 = P.pe(lambda e, mc=mc, tt=tt, bk=bk, h=h: e.matmul(
                        pb[bk], self.MT[0:64, h, mc * 128:(mc + 1) * 128],
                        self.HA[0:64, tt * 512:(tt + 1) * 512],
                        start=True, stop=True), wait=w, inc=True)
                    ee = P.act(lambda e, mc=mc, tt=tt, bk=bk: e.activation(
                        eq[:, mc, tt * 512:(tt + 1) * 512], pb[bk], AF.Exp,
                        bias=0.0, scale=1.0), wait=[m1, prev_eq])
                    peq.append(ee)
            prev_eq = peq[-1]
            pek = []
            for tc in range(NTC):
                bk = 2 + tc % 2
                w = [e4, self.ev_MT] + ([pek[-2]] if len(pek) >= 2 else [])
                m1 = P.pe(lambda e, tc=tc, bk=bk, h=h: e.matmul(
                    pb[bk][:, 0:256], self.HA[64:128, tc * 128:(tc + 1) * 128],
                    self.MT[64:128, h, :], start=True, stop=True,
                    tile_position=(64, 0)), wait=w, inc=True)
                ee = P.act(lambda e, tc=tc, bk=bk: e.activation(
                    ekT[:, tc, :], pb[bk][:, 0:256], AF.Exp,
                    bias=self.nk[:, tc:tc + 1], scale=1.0),
                    wait=[m1, enk, prev_ek])
                pek.append(ee)
            prev_ek = pek[-1]
            eS = []
            for mc in range(2):
                mS = None
                for tc in range(NTC):
                    w = [pek[tc], ha] + ([prev_S] if (tc == 0 and prev_S) else [])
                    if tc == 0 and mc == 1:
                        w.append(eS[0])
                    mS = P.pe(lambda e, mc=mc, tc=tc: e.matmul(
                        pb[4][:, 0:65], ekT[:, tc, mc * 128:(mc + 1) * 128],
                        self.HD[:, 0, tc, :], start=(tc == 0),
                        stop=(tc == NTC - 1)), wait=w if tc == 0 else [pek[tc]],
                        inc=(tc == NTC - 1))
                eS.append(P.dve(lambda e, mc=mc: e.tensor_copy(
                    Ssb[:, mc, :], pb[4][:, 0:65]), wait=[mS]))
            prev_S = eS[-1]
            for tc in range(NTC):
                m1 = None
                for mc in range(2):
                    w = [eS[mc], peq[-1]]
                    if mc == 0 and prev_out:
                        w.append(prev_out)
                    m1 = P.pe(lambda e, mc=mc, tc=tc: e.matmul(
                        pb[7][:, 0:65],
                        eq[:, mc, tc * 128:(tc + 1) * 128], Ssb[:, mc, :],
                        start=(mc == 0), stop=(mc == 1)), wait=w,
                        inc=(mc == 1))
                fa = P.act(lambda e, tc=tc: e.activation(
                    self.colA, self.nq[:, tc:tc + 1], AF.Exp,
                    bias=0.0, scale=1.0),
                    wait=[enq] + ([prev_out] if prev_out else []))
                u1 = P.dve(lambda e: e.tensor_scalar(
                    self.colB, pb[7][:, 64:65], self.colA, 1.0 / 256.0,
                    AL.mult, AL.mult), wait=[m1, fa])
                u2 = P.dve(lambda e: e.tensor_scalar_max(
                    self.colB, self.colB, 1e-6))
                u3 = P.dve(lambda e: e.reciprocal(self.colC, self.colB))
                u4 = P.dve(lambda e: e.tensor_scalar(
                    self.colB, self.colC, self.colA, 1.0 / 256.0,
                    AL.mult, AL.mult))
                prev_out = P.dve(lambda e, tc=tc, h=h: e.tensor_scalar_mul(
                    outh_T[:, tc, h * 64:(h + 1) * 64], pb[7][:, 0:64],
                    self.colB), wait=[u4])
            prev_load = prev_out
            P.commit(f"l1a{s}_{h}")
        woutT = self.U6.rearrange("p (c f) -> p c f", f=1024)
        ev_wT = P.dma("sp", woutT,
                      d["wout1T"].rearrange("(c p) o -> p c o", p=128),
                      wait=[self.pret_ev])
        outF = self.U5[:, 0:4096].rearrange("p (c f) -> p c f", f=512)
        prev_mm = None
        prev_dma = None
        ev_s1 = None
        last_tr = [None, None]
        for tt in range(NTT):
            etr = []
            for ic in range(4):
                tc = tt * 4 + ic
                for c in range(CH):
                    bk = c % 2
                    w = [x for x in (prev_out, prev_mm, last_tr[bk]) if x]
                    mt = P.pe(lambda e, tc=tc, c=c, bk=bk: e.transpose(
                        pb[bk].bitcast(BF16)[:, 0:128],
                        outh_T[:, tc, c * 128:(c + 1) * 128], self.eyeb),
                        wait=w or None, inc=True)
                    last_tr[bk] = P.dve(lambda e, tc=tc, c=c, ic=ic, bk=bk:
                                        e.tensor_copy(
                        outF[:, c, ic * 128:(ic + 1) * 128],
                        pb[bk].bitcast(BF16)[:, 0:128]), wait=[mt])
                    etr.append(last_tr[bk])
            evs1 = []
            if not hasattr(self, "_wout_ec"):
                self._wout_ec = [None] * 4
            for oc in range(CH):
                mw = None
                for ic in range(CH):
                    w = ([ev_wT] + etr) if ic == 0 else None
                    if ic == 0 and self._wout_ec[oc % 4]:
                        w = w + [self._wout_ec[oc % 4]]
                    mw = P.pe(lambda e, oc=oc, ic=ic: e.matmul(
                        pb[2 + oc % 4], woutT[:, ic, oc * 128:(oc + 1) * 128],
                        outF[:, ic, :], start=(ic == 0), stop=(ic == CH - 1)),
                        wait=w, inc=(ic == CH - 1))
                ec = P.dve(lambda e, oc=oc: e.tensor_copy(self.mkb, pb[2 + oc % 4]),
                           wait=[mw] + ([prev_dma] if prev_dma else []))
                self._wout_ec[oc % 4] = ec
                prev_dma = P.dma("sp", d["s1"][oc][:, tt * 512:(tt + 1) * 512],
                                 self.mkb, wait=[ec])
                evs1.append(prev_dma)
            prev_mm = Ev("pe", P.cnt["pe"], P.blk_idx)
            ev_s1 = evs1[-1]
            P.commit(f"l1aw{s}_{tt}")
        return ev_s1


    def s_ffn1(self, s, ev_y, ev_s1):
        P = self.P
        d = self.d
        pb = self.pb
        hs = self.U4.rearrange("p (c f) -> p c f", f=T)
        vT1 = self.U5[:, 4096:12416].rearrange("p (tc h dd) -> p tc h dd",
                                               h=8, dd=65)
        m1m = self.U7.rearrange("p (j f) -> p j f", f=512)
        ev_m1 = P.dma("gp", m1m, self.i["C_MASK1"].rearrange("j p f -> p j f"),
                      wait=[ev_y])
        hdf = self.HD.rearrange("p a b c -> p (a b c)")
        sqB = hdf[:, 0:2048]
        xpt = self.U5[:, 0:2048].rearrange("p (j f) -> p j f", f=512)
        rfs = lambda o, n: self.rf[0:1, o:o + n]
        rbs = lambda o, n: self.rb[0:1, o:o + n]
        prev_done = ev_s1
        for l in range(2):
            ehs = []
            for c in range(CH):
                w = [ev_y, ev_s1] if (l == 0 and c == 0) else \
                    ([prev_done] if c == 0 else None)
                if c < 4:
                    ehs.append(P.pool(lambda e, c=c: e.tensor_copy(
                        hs[:, c, :], self.U0[:, c, :]), wait=w))
                else:
                    P.pool(lambda e, c=c: e.memset(hs[:, c, 0:1], 0.0), wait=w)
                    ehs.append(P.pool(lambda e, c=c: e.tensor_copy(
                        hs[:, c, 1:T], self.U0[:, c, 0:T - 1])))
            ev_hs = ehs[-1]
            wqk = self.U6.rearrange("p (c f) -> p c f", f=1024)
            ev_wqk = P.dma("sp", wqk, d["wqkv_l"][l][:, 0:1024].rearrange(
                "(c p) o -> p c o", p=128), wait=[self.cast_evs["wqkv_l"],
                                                  prev_done])
            wv = self.U5[:, 0:4096].rearrange("p (c f) -> p c f", f=512)
            ev_wv = P.dma("sp", wv, d["wqkv_l"][l][:, 1024:1536].rearrange(
                "(c p) o -> p c o", p=128), wait=[self.cast_evs["wqkv_l"],
                                                  prev_done])
            em0 = P.pool(lambda e: e.memset(vT1[:, :, :, 64:65], 1.0))
            prev_ev = [None, None]
            for tc in range(NTC):
                b = tc % 2
                m = None
                for ic in range(CH):
                    w = ([ev_wv, ev_hs] + ([prev_ev[b]] if prev_ev[b] else [])) \
                        if ic == 0 else None
                    m = P.pe(lambda e, ic=ic, tc=tc, b=b: e.matmul(
                        pb[b], hs[:, ic, tc * 128:(tc + 1) * 128],
                        wv[:, ic, :], start=(ic == 0), stop=(ic == CH - 1)),
                        wait=w, inc=(ic == CH - 1))
                prev_ev[b] = P.dve(lambda e, tc=tc, b=b: e.tensor_copy(
                    vT1[:, tc, :, 0:64],
                    pb[b].rearrange("p (h dd) -> p h dd", dd=64)),
                    wait=[m, em0])
            ev_v1 = [x for x in prev_ev if x]
            ek1 = self.U5[:, 0:4096].rearrange("p (j f) -> p j f", f=512)
            prev_abuser = None
            prev_osb = None
            for hp in range(4):
                h0, h1 = 2 * hp, 2 * hp + 1
                evq = []
                for hi in range(2):
                    h = 2 * hp + hi
                    for tt in range(NTT):
                        bk = hi * 4 + tt
                        m = None
                        for ic in range(CH):
                            w = None
                            if ic == 0:
                                w = [ev_wqk, ev_hs] + list(ev_v1) + \
                                    ([prev_abuser] if prev_abuser else [])
                            P.pe(lambda e, h=h, ic=ic, tt=tt, bk=bk:
                                 e.matmul(
                                pb[bk][0:64, :],
                                wqk[:, ic, h * 64:h * 64 + 64],
                                hs[:, ic, tt * 512:(tt + 1) * 512],
                                start=(ic == 0), stop=(ic == CH - 1)),
                                wait=w)
                        for ic in range(CH):
                            m = P.pe(lambda e, h=h, ic=ic, tt=tt, bk=bk:
                                     e.matmul(
                                pb[bk][64:128, :],
                                wqk[:, ic, 512 + h * 64:512 + h * 64 + 64],
                                hs[:, ic, tt * 512:(tt + 1) * 512],
                                start=(ic == 0), stop=(ic == CH - 1),
                                tile_position=(0, 64)),
                                inc=(ic == CH - 1))
                        evq.append(P.dve(lambda e, hi=hi, tt=tt, bk=bk:
                                         e.tensor_copy(
                            self.HA[hi * 64:hi * 64 + 64,
                                    tt * 512:(tt + 1) * 512],
                            pb[bk][0:64, :]), wait=[m]))
                        evq.append(P.dve(lambda e, hi=hi, tt=tt, bk=bk:
                                         e.tensor_copy(
                            self.HB[hi * 64:hi * 64 + 64,
                                    tt * 512:(tt + 1) * 512],
                            pb[bk][64:128, :]), wait=[m]))
                eA = P.pool(lambda e: e.tensor_tensor(self.HC, self.HA,
                                                      self.HA, AL.mult),
                            wait=evq + ([prev_osb] if prev_osb else []))
                eB = P.pool(lambda e: e.tensor_tensor(sqB, self.HB, self.HB,
                                                      AL.mult), wait=evq)
                rb_slot = {(0, 0): 0, (0, 1): 2048, (1, 0): 4096, (1, 1): 6144}
                prev_r = None
                for (ti, hi) in [(0, 0), (0, 1), (1, 0), (1, 1)]:
                    sq = self.HC if ti == 0 else sqB
                    esq = eA if ti == 0 else eB
                    ecp = []
                    for tt in range(NTT):
                        mr = P.pe(lambda e, sq=sq, hi=hi, tt=tt: e.matmul(
                            pb[6][0:1, :], self.ones_c[hi * 64:hi * 64 + 64, :],
                            sq[hi * 64:hi * 64 + 64, tt * 512:(tt + 1) * 512],
                            start=True, stop=True,
                            tile_position=(hi * 64, 0)),
                            wait=[esq] + ([prev_r] if prev_r else []),
                            inc=True)
                        prev_r = P.dve(lambda e, tt=tt: e.tensor_copy(
                            rfs(4096 + tt * 512, 512), pb[6][0:1, :]),
                            wait=[mr])
                        ecp.append(prev_r)
                    e1 = P.dve(lambda e: e.tensor_scalar_max(
                        rfs(4096, 2048), rfs(4096, 2048), 1e-24))
                    e2 = P.act(lambda e: e.activation(
                        rfs(0, 2048), rfs(4096, 2048), AF.Ln,
                        bias=0.0, scale=1.0), wait=[e1])
                    e3 = P.act(lambda e: e.activation(
                        rfs(4096, 2048), rfs(0, 2048), AF.Exp,
                        bias=0.0, scale=-0.5), wait=[e2])
                    prev_r = P.dve(lambda e, sl=rb_slot[(ti, hi)]:
                                   e.tensor_copy(rbs(sl, 2048),
                                                 rfs(4096, 2048)),
                                   wait=[e3])
                ev_inv = prev_r
                prev_x = None
                for tt in range(NTT):
                    ext = P.dma("gp", xpt,
                                self.i["C_XPOS"][:, :, tt * 512:(tt + 1) * 512]
                                .rearrange("j p f -> p j f"),
                                wait=[prev_x] if prev_x else None)
                    mqa = P.pe(lambda e, tt=tt: e.matmul(
                        pb[2][0:64, :], self.ones_r[0:1, 0:64],
                        rbs(0 + tt * 512, 512), start=True, stop=True),
                        wait=[ev_inv], inc=True)
                    mqb = P.pe(lambda e, tt=tt: e.matmul(
                        pb[2][64:128, :], self.ones_r[0:1, 0:64],
                        rbs(2048 + tt * 512, 512), start=True, stop=True,
                        tile_position=(0, 64)), inc=True)
                    mka = P.pe(lambda e, tt=tt: e.matmul(
                        pb[3][0:64, :], self.ones_r[0:1, 0:64],
                        rbs(4096 + tt * 512, 512), start=True, stop=True),
                        inc=True)
                    mkb = P.pe(lambda e, tt=tt: e.matmul(
                        pb[3][64:128, :], self.ones_r[0:1, 0:64],
                        rbs(6144 + tt * 512, 512), start=True, stop=True,
                        tile_position=(0, 64)), inc=True)
                    en1 = P.dve(lambda e, tt=tt, l=l: e.scalar_tensor_tensor(
                        self.HA[:, tt * 512:(tt + 1) * 512],
                        self.HA[:, tt * 512:(tt + 1) * 512],
                        self.qs2[:, l:l + 1], pb[2], AL.mult, AL.mult),
                        wait=[mqa, mqb])
                    en2 = P.dve(lambda e, tt=tt, l=l: e.scalar_tensor_tensor(
                        self.HB[:, tt * 512:(tt + 1) * 512],
                        self.HB[:, tt * 512:(tt + 1) * 512],
                        self.ks2[:, l:l + 1], pb[3], AL.mult, AL.mult),
                        wait=[mka, mkb])
                    msa = P.pe(lambda e, tt=tt: e.matmul(
                        pb[4], self.p32, self.HA[:, tt * 512:(tt + 1) * 512],
                        start=True, stop=True),
                        wait=[en1, self.ev_p32], inc=True)
                    msb = P.pe(lambda e, tt=tt: e.matmul(
                        pb[5], self.p32, self.HB[:, tt * 512:(tt + 1) * 512],
                        start=True, stop=True), wait=[en2], inc=True)
                    t1 = P.dve(lambda e, tt=tt: e.tensor_tensor(
                        self.sb512, self.HA[:, tt * 512:(tt + 1) * 512],
                        xpt[:, 0, :], AL.mult), wait=[ext, en1])
                    t2 = P.dve(lambda e, tt=tt: e.scalar_tensor_tensor(
                        self.sb512b, pb[4], 1.0, xpt[:, 1, :],
                        AL.mult, AL.mult), wait=[msa])
                    ea1 = P.dve(lambda e, tt=tt: e.tensor_tensor(
                        self.HA[:, tt * 512:(tt + 1) * 512], self.sb512,
                        self.sb512b, AL.add))
                    t3 = P.dve(lambda e, tt=tt: e.tensor_tensor(
                        self.sb512, self.HB[:, tt * 512:(tt + 1) * 512],
                        xpt[:, 2, :], AL.mult), wait=[en2])
                    t4 = P.dve(lambda e, tt=tt: e.scalar_tensor_tensor(
                        self.sb512b, pb[5], 1.0, xpt[:, 3, :],
                        AL.mult, AL.mult), wait=[msb])
                    prev_x = P.dve(lambda e, tt=tt: e.tensor_tensor(
                        self.HB[:, tt * 512:(tt + 1) * 512], self.sb512,
                        self.sb512b, AL.add))
                exp_ev = [None, None]
                prev_av = None
                osb_ev = []
                for hi in range(2):
                    h = 2 * hp + hi
                    for wi in range(NTT):
                        jcs = list(range(4, 8)) if wi == 0 else list(range(8))
                        eks = {}
                        for jc in jcs:
                            kc = 4 * (wi - 1) + jc
                            bk = jc % 2
                            w = [prev_x] + ([exp_ev[bk]] if exp_ev[bk] else [])
                            P.pe(lambda e, hi=hi, jc=jc, kc=kc, wi=wi, bk=bk:
                                 e.matmul(
                                pb[bk],
                                self.HB[hi * 64:hi * 64 + 64,
                                        kc * 128:(kc + 1) * 128],
                                self.HA[hi * 64:hi * 64 + 64,
                                        wi * 512:(wi + 1) * 512],
                                start=True, stop=False,
                                tile_position=(hi * 64, 0)), wait=w)
                            mm = P.pe(lambda e, jc=jc, bk=bk: e.matmul(
                                pb[bk], self.eyeb, m1m[:, jc, :],
                                start=False, stop=True),
                                wait=[ev_m1, self.ev_eyeb], inc=True)
                            ee = P.act(lambda e, jc=jc, bk=bk: e.activation(
                                ek1[:, jc, :], pb[bk], AF.Exp,
                                bias=-SHIFT1, scale=8.0), wait=[mm])
                            exp_ev[bk] = ee
                            eks[jc] = ee
                        mv = None
                        first = True
                        for jc in jcs:
                            kc = 4 * (wi - 1) + jc
                            w = [eks[jc]] + ([prev_av] if (first and prev_av)
                                             else [])
                            if hi == 0:
                                mv = P.pe(lambda e, h=h, kc=kc, jc=jc,
                                          first=first: e.matmul(
                                    pb[4][0:65, :], vT1[:, kc, h, :],
                                    ek1[:, jc, :], start=first,
                                    stop=(jc == jcs[-1])), wait=w,
                                    inc=(jc == jcs[-1]))
                            else:
                                mv = P.pe(lambda e, h=h, kc=kc, jc=jc,
                                          first=first: e.matmul(
                                    pb[5][64:128, :], vT1[:, kc, h, 0:64],
                                    ek1[:, jc, :], start=first,
                                    stop=(jc == jcs[-1]),
                                    tile_position=(0, 64)), wait=w,
                                    inc=False)
                                mv = P.pe(lambda e, h=h, kc=kc, jc=jc,
                                          first=first: e.matmul(
                                    pb[6][0:1, :], vT1[:, kc, h, 64:65],
                                    ek1[:, jc, :], start=first,
                                    stop=(jc == jcs[-1])), wait=[eks[jc]],
                                    inc=(jc == jcs[-1]))
                            first = False
                        if hi == 0:
                            prev_av = P.dve(lambda e, wi=wi: e.tensor_copy(
                                rfs(wi * 512, 512), pb[4][64:65, :]),
                                wait=[mv])
                        else:
                            prev_av = P.dve(lambda e, wi=wi: e.tensor_copy(
                                rfs(2048 + wi * 512, 512), pb[6][0:1, :]),
                                wait=[mv])
                        if hi == 0:
                            prev_av = P.dve(lambda e, wi=wi: e.tensor_copy(
                                self.HC[0:64, wi * 512:(wi + 1) * 512],
                                pb[4][0:64, :]))
                        else:
                            prev_av = P.dve(lambda e, wi=wi: e.tensor_copy(
                                self.HC[64:128, wi * 512:(wi + 1) * 512],
                                pb[5][64:128, :]))
                        osb_ev.append(prev_av)
                eL = P.act(lambda e: e.activation(rfs(4096, 2048),
                                                  rfs(0, 2048), AF.Ln,
                                                  bias=0.0, scale=1.0),
                           wait=[prev_av])
                eE1 = P.act(lambda e: e.activation(rbs(0, 2048),
                                                   rfs(4096, 2048), AF.Exp,
                                                   bias=0.0, scale=-1.0),
                            wait=[eL])
                eL2 = P.act(lambda e: e.activation(rfs(4096, 2048),
                                                   rfs(2048, 2048), AF.Ln,
                                                   bias=0.0, scale=1.0))
                eE2 = P.act(lambda e: e.activation(rbs(2048, 2048),
                                                   rfs(4096, 2048), AF.Exp,
                                                   bias=0.0, scale=-1.0),
                            wait=[eL2])
                prev_bc = None
                eo = []
                for wi in range(NTT):
                    ma = P.pe(lambda e, wi=wi: e.matmul(
                        pb[7][0:64, :], self.ones_r[0:1, 0:64],
                        rbs(wi * 512, 512), start=True, stop=True),
                        wait=[eE1, eE2] + ([prev_bc] if prev_bc else []),
                        inc=True)
                    mb = P.pe(lambda e, wi=wi: e.matmul(
                        pb[7][64:128, :], self.ones_r[0:1, 0:64],
                        rbs(2048 + wi * 512, 512), start=True, stop=True,
                        tile_position=(0, 64)), inc=True)
                    ec = P.dve(lambda e: e.tensor_copy(self.sb512, pb[7]),
                               wait=[ma, mb] + ([eo[-1]] if eo else []))
                    prev_bc = ec
                    eo.append(P.pool(lambda e, wi=wi: e.tensor_tensor(
                        self.HC[:, wi * 512:(wi + 1) * 512],
                        self.HC[:, wi * 512:(wi + 1) * 512],
                        self.sb512, AL.mult), wait=[ec] + osb_ev))
                prev_osb = P.dma("sp", d["ybk"][hp], self.HC, wait=eo)
                prev_abuser = eo[-1]
                if hp == 3:
                    ev_opairs = prev_osb
            ob = self.U6.rearrange("p (c f) -> p c f", f=T)[:, 0:4, :]
            ev_ob = P.dma("sp", self.U6.rearrange("p (c f) -> p c f", f=T),
                          d["ybk"], wait=[prev_osb])
            wo_sb = self.U5[:, 0:4096].rearrange("p (c f) -> p c f", f=1024)
            ev_wos = P.dma("sp", wo_sb, d["wo_l"][l].rearrange(
                "(c p) o -> p c o", p=128), wait=[self.cast_evs["wo_l"],
                                                  prev_osb])
            ev_yup = None
            yup_bank = [None] * CH
            for tt in range(NTT):
                for oc in range(CH):
                    mw = None
                    for ic in range(4):
                        w = None
                        if ic == 0:
                            w = [ev_ob, ev_wos] + \
                                ([yup_bank[oc]] if yup_bank[oc] else [])
                        mw = P.pe(lambda e, oc=oc, ic=ic, tt=tt: e.matmul(
                            pb[oc], wo_sb[:, ic, oc * 128:(oc + 1) * 128],
                            ob[:, ic, tt * 512:(tt + 1) * 512],
                            start=(ic == 0), stop=(ic == 3)),
                            wait=w, inc=(ic == 3))
                    ev_yup = P.dve(lambda e, oc=oc, tt=tt: e.scalar_tensor_tensor(
                        self.U0[:, oc, tt * 512:(tt + 1) * 512], pb[oc], 1.0,
                        self.U0[:, oc, tt * 512:(tt + 1) * 512],
                        AL.mult, AL.add), wait=[mw])
                    yup_bank[oc] = ev_yup
            P.commit(f"f1a{s}_{l}")
            ehs = []
            for c in range(CH):
                w = [ev_yup] if c == 0 else None
                if c < 4:
                    ehs.append(P.pool(lambda e, c=c: e.tensor_copy(
                        hs[:, c, :], self.U0[:, c, :]), wait=w))
                else:
                    P.pool(lambda e, c=c: e.memset(hs[:, c, 0:1], 0.0), wait=w)
                    ehs.append(P.pool(lambda e, c=c: e.tensor_copy(
                        hs[:, c, 1:T], self.U0[:, c, 0:T - 1])))
            ev = self.ln_stats(hs, [ehs[-1]])
            pbA, pbB = self.pb[4], self.pb[5]
            eaf = None
            prev = None
            for tt in range(NTT):
                w = [ev] + ([prev] if prev else [])
                mA, mB = self.bcast2(
                    self.rb[0:1, tt * 512:(tt + 1) * 512],
                    self.rb[0:1, 2048 + tt * 512:2048 + (tt + 1) * 512],
                    pbA, pbB, wait=w)
                eBc = P.dve(lambda e: e.tensor_copy(self.sb512, pbB),
                            wait=[mB])
                for c in range(CH):
                    P.dve(lambda e, c=c, tt=tt: e.tensor_tensor(
                        self.sb512b, hs[:, c, tt * 512:(tt + 1) * 512], pbA,
                        AL.mult), wait=[mA, eBc])
                    t2 = P.dve(lambda e: e.tensor_tensor(
                        self.gb, self.sb512b, self.sb512, AL.add))
                    eaf = P.dve(lambda e, c=c, tt=tt, l=l: e.tensor_scalar(
                        hs[:, c, tt * 512:(tt + 1) * 512], self.gb,
                        self.lgam[:, l, c:c + 1], self.lbet[:, l, c:c + 1],
                        AL.mult, AL.add))
                prev = eaf
            P.commit(f"f1b{s}_{l}")
            h3 = self.U5[:, 0:11264].rearrange("p (k f) -> p k f", f=512)
            if True:
                pass
            wsl = self.U6[:, 0:4096].rearrange("p (b m c f) -> p b m c f",
                                               b=2, m=2, f=128)
            ev_w2done = None
            prev_sl = [None, None]
            for tt in range(NTT):
                evh3 = []
                for ocp in range(22):
                    wdt = 128 if ocp < 21 else 42
                    b = ocp % 2
                    acol = ocp * 128
                    gcol = 2730 + ocp * 128
                    w = [self.cast_evs["w1_l"]] + \
                        ([prev_sl[b]] if prev_sl[b] else []) + \
                        ([eaf] if (tt == 0 and ocp < 2) else [])
                    esl = P.dma_multi("sp", [
                        (wsl[:, b, 0, :, 0:wdt],
                         d["w1_l"][l][:, acol:acol + wdt].rearrange(
                             "(c p) o -> p c o", p=128)),
                        (wsl[:, b, 1, :, 0:wdt],
                         d["w1_l"][l][:, gcol:gcol + wdt].rearrange(
                             "(c p) o -> p c o", p=128)),
                    ], wait=w)
                    mA2 = mG = None
                    prev_eh3 = getattr(self, "_w1_prev_eh3", None)
                    prev_eg = getattr(self, "_w1_prev_eg", None)
                    w2ev = getattr(self, "_w2_evac", [None] * CH)
                    for ic in range(CH):
                        w0 = None
                        if ic == 0:
                            w0 = [esl] + ([prev_eh3] if prev_eh3 else []) + \
                                ([w2ev[0]] if (ocp < 2 and w2ev[0]) else [])
                        mA2 = P.pe(lambda e, b=b, ic=ic, tt=tt, wdt=wdt:
                                   e.matmul(
                            pb[0][0:wdt, :], wsl[:, b, 0, ic, 0:wdt],
                            hs[:, ic, tt * 512:(tt + 1) * 512],
                            start=(ic == 0), stop=(ic == CH - 1)),
                            wait=w0, inc=(ic == CH - 1))
                    for ic in range(CH):
                        w0 = None
                        if ic == 0:
                            w0 = ([prev_eg] if prev_eg else []) + \
                                ([w2ev[1]] if (ocp < 2 and w2ev[1]) else [])
                            w0 = w0 or None
                        mG = P.pe(lambda e, b=b, ic=ic, tt=tt, wdt=wdt:
                                  e.matmul(
                            pb[1][0:wdt, :], wsl[:, b, 1, ic, 0:wdt],
                            hs[:, ic, tt * 512:(tt + 1) * 512],
                            start=(ic == 0), stop=(ic == CH - 1)),
                            wait=w0, inc=(ic == CH - 1))
                    prev_sl[b] = mG
                    eg = P.act(lambda e, ocp=ocp, wdt=wdt, l=l: e.activation(
                        self.mkb[0:wdt, :], pb[1][0:wdt, :], AF.Gelu,
                        bias=self.b1c[0:wdt, l, 22 + ocp:23 + ocp],
                        scale=1.0), wait=[mG])
                    eh3 = P.dve(lambda e, ocp=ocp, wdt=wdt, l=l:
                                e.scalar_tensor_tensor(
                        h3[:, ocp, :][0:wdt, :], pb[0][0:wdt, :],
                        self.b1c[0:wdt, l, ocp:ocp + 1], self.mkb[0:wdt, :],
                        AL.add, AL.mult), wait=[mA2, eg])
                    self._w1_prev_eh3 = eh3
                    self._w1_prev_eg = eg
                    evh3.append(eh3)
                prev_w2sl = getattr(self, "_w2_prev_sl", [None] * 4)
                mws = []
                for kc in range(22):
                    wk = 128 if kc < 21 else 42
                    b2 = kc % 4
                    w = [self.cast_evs["w2_l"]] + \
                        ([prev_w2sl[b2]] if prev_w2sl[b2] else [])
                    es2 = P.dma("sp",
                                self.U6[0:wk, 4096 + b2 * 1024:
                                        4096 + b2 * 1024 + 1024],
                                d["w2_l"][l][kc * 128:kc * 128 + wk, :],
                                wait=w)
                    for oc in range(CH):
                        w0 = [es2, evh3[kc]] if oc == 0 else None
                        if kc == 0:
                            w0 = (w0 or []) + [evh3[-1]] + \
                                ([ev_w2done] if ev_w2done else [])
                        mw2 = P.pe(lambda e, kc=kc, oc=oc, b2=b2, wk=wk:
                                   e.matmul(
                            pb[oc],
                            self.U6[0:wk, 4096 + b2 * 1024 + oc * 128:
                                    4096 + b2 * 1024 + (oc + 1) * 128],
                            h3[0:wk, kc, :], start=(kc == 0),
                            stop=(kc == 21)), wait=w0,
                            inc=(kc == 21 or oc == 7))
                        if kc == 21:
                            mws.append(mw2)
                    prev_w2sl[b2] = mw2
                self._w2_prev_sl = prev_w2sl
                if not hasattr(self, "_w2_evac"):
                    self._w2_evac = [None] * CH
                for oc in range(CH):
                    ev_w2done = P.dve(lambda e, oc=oc, tt=tt, l=l:
                                      e.scalar_tensor_tensor(
                        self.U0[:, oc, tt * 512:(tt + 1) * 512], pb[oc],
                        self.b2c[:, l, oc:oc + 1],
                        self.U0[:, oc, tt * 512:(tt + 1) * 512],
                        AL.add, AL.add), wait=[mws[oc]])
                    self._w2_evac[oc] = ev_w2done
            prev_done = ev_w2done
            P.commit(f"f1c{s}_{l}")
        return prev_done

    def s_final(self, s, ev_ffn):
        P = self.P
        d = self.d
        pb = self.pb
        s1t = self.U4[:, 0:4096].rearrange("p (c f) -> p c f", f=512)
        sgf = self.u5f[:, 0:4096].rearrange("p (c f) -> p c f", f=512)
        xTf = self.u5f[:, 4096:8192].rearrange("p (sub f) -> p sub f", f=1024)
        ev_out = None
        prev_tr = None
        prev_sg = None
        prev_s1 = None
        for TT in range(NTT):
            es1 = P.dma("sp", s1t, d["s1"].rearrange(
                "c p t -> p c t")[:, :, TT * 512:(TT + 1) * 512],
                wait=[ev_ffn] + ([prev_s1] if prev_s1 else []))
            prev_s1 = es1
            ex = P.dma("sp", xTf,
                       d["x1T"][s][TT * 512:(TT + 1) * 512, :].rearrange(
                           "(sub p) e -> p sub e", p=128),
                       wait=([prev_tr] if prev_tr else []) +
                       ([ev_out] if ev_out else []))
            trs = []
            for c in range(CH):
                eg1 = P.act(lambda e, c=c, TT=TT: e.activation(
                    self.mkb, self.U0[:, c, TT * 512:(TT + 1) * 512],
                    AF.Silu, bias=0.0, scale=1.0),
                    wait=[ev_ffn] + ([prev_sg] if prev_sg else []))
                e1 = P.dve(lambda e, c=c: e.tensor_tensor(
                    sgf[:, c, :], s1t[:, c, :], self.mkb, AL.mult),
                    wait=[eg1, es1] + ([prev_tr] if prev_tr else []))
                prev_sg = e1
                tr = None
                for sub in range(4):
                    mt = P.pe(lambda e, c=c, sub=sub: e.transpose(
                        pb[c][:, 0:128], sgf[:, c, sub * 128:(sub + 1) * 128],
                        self.eyef), wait=[e1, ex] + ([tr] if tr else []),
                        inc=True)
                    tr = P.dve(lambda e, c=c, sub=sub: e.tensor_tensor(
                        xTf[:, sub, c * 128:(c + 1) * 128],
                        xTf[:, sub, c * 128:(c + 1) * 128],
                        pb[c][:, 0:128], AL.add), wait=[mt])
                    trs.append(tr)
            prev_tr = trs[-1]
            ev_out = P.dma(
                "sp", self.o_out[s][TT * 512:(TT + 1) * 512, :].rearrange(
                    "(sub p) e -> p sub e", p=128), xTf, wait=trs)
            self.out_evs.append(ev_out)
            P.commit(f"fc{s}_{TT}")
        return ev_out

    def finish(self):
        P = self.P
        P.op("pool", lambda e: e.memset(self.fint, 0.0), wait=self.out_evs)
        P.commit("fin")
        return self.nc


def build_l0(dbg=(), nseq=1):
    k = K(dbg=dbg)
    k.prelude()
    d = k.d
    for s in range(nseq):
        ev_xn = k.s_stats_xn(s)
        ev_g0 = k.s_moe(s, ev_xn)
        ev_q, ev_k, ev_v = k.s_proj(
            s, k.U0, ev_xn,
            lambda oc: d["wqT"][:, oc * 128:(oc + 1) * 128],
            lambda oc: d["wkT"][:, oc * 128:(oc + 1) * 128],
            d["wvT"], d["qs"], d["ks"], d["vs"], "L0")
        ev_x1 = k.s_attn0(s, ev_q, ev_k, ev_v, ev_g0, ev_xn)
        k.s_ln1_y(s, ev_x1)
    return k.finish(), k


def build_full(dbg=(), nseq=NSEQ):
    k = K(dbg=dbg)
    k.prelude()
    d = k.d
    for s in range(nseq):
        ev_xn = k.s_stats_xn(s)
        ev_g0 = k.s_moe(s, ev_xn)
        ev_q, ev_k, ev_v = k.s_proj(
            s, k.U0, ev_xn,
            lambda oc: d["wqT"][:, oc * 128:(oc + 1) * 128],
            lambda oc: d["wkT"][:, oc * 128:(oc + 1) * 128],
            d["wvT"], d["qs"], d["ks"], d["vs"], "L0")
        ev_x1 = k.s_attn0(s, ev_q, ev_k, ev_v, ev_g0, ev_xn)
        ev_y = k.s_ln1_y(s, ev_x1)
        ev_q1, ev_k1, ev_v1 = k.s_proj(
            s, k.U0, ev_y,
            lambda oc: d["wqkv1T"][:, oc * 128:(oc + 1) * 128],
            lambda oc: d["wqkv1T"][:, 1024 + oc * 128:1024 + (oc + 1) * 128],
            d["wqkv1T"][:, 2048:3072], d["q1"], d["k1"], d["v1"], "L1")
        ev_s1 = k.s_attn1(s, ev_q1, ev_k1, ev_v1)
        ev_ffn = k.s_ffn1(s, ev_y, ev_s1)
        k.s_final(s, ev_ffn)
    return k.finish(), k


_CACHE = {}


def kernel(**inputs):
    x = np.ascontiguousarray(inputs["L0_attn_x"])
    n_cores = 8
    if "nc" not in _CACHE:
        nc, k = build_full()
        _CACHE["nc"] = nc
    nc = _CACHE["nc"]
    consts = host_consts()
    base = {n: np.ascontiguousarray(v) for n, v in inputs.items()
            if n != "L0_attn_x"}
    base.update(consts)
    in_maps = []
    for c in range(n_cores):
        m = dict(base)
        m["L0_attn_x"] = x[2 * c:2 * c + 2]
        in_maps.append(m)
    res = run_bass_kernel_spmd(nc, in_maps, core_ids=list(range(n_cores)))
    out = np.concatenate([np.asarray(r["OUT"], dtype=np.float32)
                          for r in res.results], axis=0)
    return out


# version 06
# speedup vs baseline: 1.0374x; 1.0071x over previous
import numpy as np
import concourse.bass as bass
import concourse.mybir as mybir
from concourse.bass_utils import run_bass_kernel_spmd


import concourse.bass as bass
import concourse.mybir as mybir

F32 = mybir.dt.float32
BF16 = mybir.dt.bfloat16
AX = mybir.AxisListType.X
AL = mybir.AluOpType
AF = mybir.ActivationFunctionType

_SECTION = {"pe": "tensor", "dve": "vector", "act": "scalar",
            "pool": "gpsimd", "sp": "sync", "gp": "gpsimd", "actdma": "scalar"}
_QUEUES = ("sp", "gp", "actdma")
N_DMA_SEMS = 40


class Ev:
    __slots__ = ("key", "tgt", "blk", "dma")

    def __init__(self, key, tgt, blk, dma=False):
        self.key, self.tgt, self.blk, self.dma = key, tgt, blk, dma

    def __repr__(self):
        return f"Ev({self.key},{self.tgt},b{self.blk}{'D' if self.dma else ''})"


class Prog:
    def __init__(self, nc):
        self.nc = nc
        self.sem = {}
        self.cnt = {}
        for e in ("pe", "dve", "act", "pool"):
            self.sem[e] = nc.alloc_semaphore("S_" + e)
            self.cnt[e] = 0
        self._dma_pools = {"sp": [], "gp": [], "actdma": []}
        npool = {"sp": 24, "gp": 12, "actdma": 4}
        ii = 0
        for q, n in npool.items():
            for _ in range(n):
                k = f"d{ii}"
                ii += 1
                self.sem[k] = nc.alloc_semaphore("S_" + k)
                self.cnt[k] = 0
                self._dma_pools[q].append(k)
        self._dma_rr = {"sp": 0, "gp": 0, "actdma": 0}
        self.blk_idx = 0
        self._recs = {}
        self._last_wait = {}
        self._cmap = {}

    def setup_consts(self, values):
        vals = sorted(set(float(v) for v in values))
        if not vals:
            return
        self._ctile = self.nc.alloc_sbuf_tensor(
            "constcols", [128, len(vals)], F32).ap()
        for i, v in enumerate(vals):
            col = self._ctile[:, i:i + 1]
            self.op("pool", lambda e, c=col, vv=v: e.memset(c, vv))
            self._cmap[v] = col
            try:
                self.nc.const_aps.register(F32, v, col)
            except Exception:
                pass

    def ccol(self, v):
        return self._cmap[float(v)]

    def _rec(self, eng_name, fn):
        self._recs.setdefault(_SECTION[eng_name], []).append(fn)

    def _emit_waits(self, eng_name, waits):
        sec = _SECTION[eng_name]
        for ev in (waits or []):
            if ev is None:
                continue
            if ev.blk < self.blk_idx and not ev.dma:
                continue
            key = (sec, ev.key)
            if self._last_wait.get(key, -1) >= ev.tgt:
                continue
            self._last_wait[key] = ev.tgt
            sem, tgt = self.sem[ev.key], ev.tgt
            self._rec(eng_name, lambda e, s=sem, t=tgt: e.wait_ge(s, t))

    def op(self, eng_name, fn, wait=None, inc=True):
        self_serial = eng_name in ("dve", "act", "pool")
        wl = list(wait) if wait else []
        if self_serial and self.cnt[eng_name] > 0:
            wl.append(Ev(eng_name, self.cnt[eng_name], self.blk_idx))
        self._emit_waits(eng_name, wl)
        ev = None
        if inc or self_serial:
            self.cnt[eng_name] += 1
            sem, tgt = self.sem[eng_name], self.cnt[eng_name]
            ev = Ev(eng_name, tgt, self.blk_idx)
            self._rec(eng_name, lambda e, s=sem, f=fn: f(e).then_inc(s, 1))
        else:
            self._rec(eng_name, lambda e, f=fn: f(e))
        return ev

    def raw(self, eng_name, fn):
        self._rec(eng_name, fn)

    def dma_multi(self, q, transfers, wait=None, **kw):
        assert q in _QUEUES
        self._emit_waits(q, wait)
        pool = self._dma_pools[q]
        k = pool[self._dma_rr[q]]
        self._dma_rr[q] = (self._dma_rr[q] + 1) % len(pool)
        sem = self.sem[k]
        base = self.cnt[k]
        if base > 0:
            key = (_SECTION[q], k)
            if self._last_wait.get(key, -1) < base:
                self._last_wait[key] = base
                self._rec(q, lambda e, s=sem, t=base: e.wait_ge(s, t))
        n = len(transfers)
        for t in transfers:
            out, in_ = t[0], t[1]
            tkw = dict(kw)
            if len(t) > 2:
                tkw.update(t[2])
            self._rec(q, lambda e, o=out, i=in_, kww=tkw, s=sem:
                      e.dma_start(out=o, in_=i, **kww).then_inc(s, 16))
        self.cnt[k] = base + 16 * n
        return Ev(k, self.cnt[k], self.blk_idx, dma=True)

    def dma(self, q, out, in_, wait=None, **kw):
        return self.dma_multi(q, [(out, in_)], wait=wait, **kw)

    def dmaT_multi(self, transfers, wait=None):
        self._emit_waits("sp", wait)
        pool = self._dma_pools["sp"]
        k = pool[self._dma_rr["sp"]]
        self._dma_rr["sp"] = (self._dma_rr["sp"] + 1) % len(pool)
        sem = self.sem[k]
        base = self.cnt[k]
        if base > 0:
            key = (_SECTION["sp"], k)
            if self._last_wait.get(key, -1) < base:
                self._last_wait[key] = base
                self._rec("sp", lambda e, s=sem, t=base: e.wait_ge(s, t))
        for (out, in_) in transfers:
            self._rec("sp", lambda e, o=out, i=in_, s=sem:
                      e.dma_start_transpose(o, i).then_inc(s, 16))
        self.cnt[k] = base + 16 * len(transfers)
        return Ev(k, self.cnt[k], self.blk_idx, dma=True)

    def dmaT(self, out, in_, wait=None):
        return self.dmaT_multi([(out, in_)], wait=wait)

    def commit(self, name=None):
        if not self._recs:
            return
        recs, self._recs = self._recs, {}
        with self.nc.Block(name) as b:
            for sec, fns in recs.items():
                def body(eng, fns=fns):
                    for f in fns:
                        f(eng)
                getattr(b, sec)(body)
        self.blk_idx += 1
        self._last_wait = {}

    def pe(self, fn, wait=None, inc=False):
        return self.op("pe", fn, wait=wait, inc=inc)

    def dve(self, fn, wait=None, inc=False):
        return self.op("dve", fn, wait=wait, inc=inc)

    def act(self, fn, wait=None, inc=False):
        return self.op("act", fn, wait=wait, inc=inc)

    def pool(self, fn, wait=None, inc=False):
        return self.op("pool", fn, wait=wait, inc=inc)




E = 1024
T = 2048
NSEQ = 2
CH = 8
NTT = 4
NTC = 16
NEG = -1e30
EPS_LN = 1e-5
SHIFT0 = 12.0
SHIFT1 = 30.0
HID = 2730

CONST_SCALARS = [1.0 / E, -1.0, EPS_LN, 1e-6, -0.5, 1e-24, 8.0, 0.125,
                 -SHIFT0, -SHIFT1, 1e-30, 0.5, 1.0 / 4096.0, 1.0 / 256.0]


def host_consts():
    c = {}
    c["C_EYEF"] = np.eye(128, dtype=np.float32)
    p = np.arange(128)[:, None]
    f = np.arange(512)[None, :]
    m = np.zeros((4, 128, 512), np.float32)
    for j in range(4):
        m[j] = np.where(p <= f - 128 * j, 0.0, NEG)
    c["C_MASK0"] = m
    m1 = np.zeros((8, 128, 512), np.float32)
    for jc in range(8):
        keep = (f <= 128 * jc + p) if jc < 4 else (128 * (jc - 4) + p <= f)
        m1[jc] = np.where(keep, 0.0, NEG)
    c["C_MASK1"] = m1
    d = np.arange(128) % 64
    freq = 10000.0 ** (-2.0 * (d // 2) / 64)
    t = np.arange(T)
    ang = np.maximum(t[None, :] - 1, 0) * freq[:, None]
    Cc = np.cos(ang)
    Ss = np.sin(ang)
    sign = np.where(d % 2 == 0, -1.0, 1.0)[:, None]
    Cc[:, 0] = 1.0
    Ss[:, 0] = 0.0
    c["C_ROPE"] = np.stack([Cc, sign * Ss]).astype(np.float32)
    fi = d % 32
    inv_freq = 1.0 / 10000.0 ** (2.0 * fi / 64)
    base = (2.0 * fi + 0.4 * 64) / (1.4 * 64)
    fr = t[None, :] * inv_freq[:, None]
    scale = base[:, None] ** ((t - T // 2) / 256.0)[None, :]
    cos, sin = np.cos(fr), np.sin(fr)
    sgn = np.where(d < 32, -1.0, 1.0)[:, None]
    c["C_XPOS"] = np.stack([cos * scale, sgn * sin * scale,
                            cos / scale, sgn * sin / scale]).astype(np.float32)
    H = np.array([[1.0]])
    while H.shape[0] < 64:
        H = np.block([[H, H], [H, -H]])
    c["C_HAD"] = H.astype(np.float32)
    P32 = np.zeros((128, 128), np.float32)
    for p in range(128):
        blk, off = p // 64, p % 64
        P32[blk * 64 + (off + 32) % 64, p] = 1.0
    c["C_P32"] = P32
    return c


class K:
    def __init__(self, dbg=()):
        self.dbg = set(dbg)
        nc = bass.Bass("TRN2", target_bir_lowering=False)
        self.nc = nc
        self.P = Prog(nc)
        self.out_evs = []
        self.dbg_names = []
        self._io()
        self._sbuf()

    def _io(self):
        nc = self.nc
        ap = lambda n, s, dt=F32, kind="ExternalInput": nc.dram_tensor(
            n, s, dt, kind=kind).ap()
        i = self.i = {}
        i["x"] = ap("L0_attn_x", [NSEQ, T, E])
        for n, s in [("L0_attn_Wq", [E, E]), ("L0_attn_Wk", [E, E]),
                     ("L0_attn_Wv", [E, E]), ("L0_attn_Wo", [E, E]),
                     ("L0_attn_gamma", [E]), ("L0_attn_beta", [E]),
                     ("L0_ffn_router_w", [E, 8]), ("L0_ffn_router_b", [8]),
                     ("L0_ffn_expert_w", [8, E, E]), ("L0_ffn_expert_b", [8, E]),
                     ("L1_attn_w_qkv", [3 * E, E]), ("L1_attn_w_out", [E, E]),
                     ("L1_attn_sorf_signs", [16, 4, 3, 64]),
                     ("L1_attn_sorf_scales", [16, 4, 64]),
                     ("L1_ffn_wqkv", [2, E, 1536]), ("L1_ffn_q_scale", [2, 64]),
                     ("L1_ffn_k_scale", [2, 64]), ("L1_ffn_wo", [2, 512, E]),
                     ("L1_ffn_ln_g", [2, E]), ("L1_ffn_ln_b", [2, E]),
                     ("L1_ffn_w1", [2, E, 5460]), ("L1_ffn_b1", [2, 5460]),
                     ("L1_ffn_w2", [2, HID, E]), ("L1_ffn_b2", [2, E]),
                     ("C_EYEF", [128, 128]), ("C_MASK0", [4, 128, 512]),
                     ("C_MASK1", [8, 128, 512]), ("C_ROPE", [2, 128, T]),
                     ("C_XPOS", [4, 128, T]), ("C_HAD", [64, 64]),
                     ("C_P32", [128, 128])]:
            i[n] = ap(n, s)
        self.o_out = ap("OUT", [NSEQ, T, E], F32, kind="ExternalOutput")
        sc = lambda n, s, dt=BF16: nc.dram_tensor(n, s, dt, kind="Internal").ap()
        d = self.d = {}
        d["xb16"] = sc("xb16", [NSEQ, T, E])
        for n in ("wq_b", "wk_b", "wv_b", "wo_b", "wout1_b"):
            d[n] = sc(n, [E, E])
        d["wqkv1_b"] = sc("wqkv1_b", [3 * E, E])
        for n, s in [("wqT", [E, E]), ("wkT", [E, E]), ("wvT", [E, E]),
                     ("woT", [E, E]), ("wqkv1T", [E, 3 * E]),
                     ("wout1T", [E, E])]:
            d[n] = sc(n, s)
        d["wqkv_l"] = sc("wqkv_l", [2, E, 1536])
        d["wo_l"] = sc("wo_l", [2, 512, E])
        d["w1_l"] = sc("w1_l", [2, E, 5460])
        d["w2_l"] = sc("w2_l", [2, HID, E])
        d["ew"] = sc("ew", [8, E, E])
        d["disp"] = sc("disp", [8, T])
        d["gatec"] = sc("gatec", [1, T], F32)
        d["g0"] = sc("g0", [CH, 128, T])
        d["s1"] = sc("s1", [CH, 128, T])
        d["qs"] = sc("qs", [CH, 128, T])
        d["ks"] = sc("ks", [CH, 128, T])
        d["vs"] = sc("vs", [16, NTC, 128, 65])
        d["q1"] = sc("q1", [CH, 128, T])
        d["k1"] = sc("k1", [CH, 128, T])
        d["v1"] = sc("v1", [16, NTC, 128, 65])
        d["ybk"] = sc("ybk", [4, 128, T])
        d["x1T"] = sc("x1T", [NSEQ, T, E], F32)

    def dbg_dram(self, name, shape, dt=F32):
        a = self.nc.dram_tensor("DBG_" + name, shape, dt,
                                kind="ExternalOutput").ap()
        self.dbg_names.append(name)
        return a

    def _sbuf(self):
        nc = self.nc
        A = lambda n, s, dt=BF16, **kw: nc.alloc_sbuf_tensor(n, s, dt, **kw).ap()
        self.U0 = A("U0", [128, CH, T])
        self.U4 = A("U4", [128, 16384], is_transpose_dest=True)
        self.U5 = A("U5", [128, 16384], is_transpose_dest=True)
        self.U6 = A("U6", [128, 8192], is_transpose_dest=True)
        self.U7 = A("U7", [128, 4096])
        self.HA = A("HA", [128, T])
        self.HB = A("HB", [128, T])
        self.HC = A("HC", [128, T])
        self.HD = A("HD", [128, 2, 16, 65])
        self.HA2 = A("HA2", [128, 512])
        self.HB2 = A("HB2", [128, T])
        self.rf = A("rf", [1, 6144], F32)
        self.rb = A("rb", [1, 8192])
        self.eyef = A("eyef", [128, 128], F32)
        self.eyeb = A("eyeb", [128, 128])
        self.ones_r = A("ones_r", [1, 128])
        self.ones_rf = A("ones_rf", [1, 128], F32)
        self.ones_c = A("ones_c", [128, 1])
        self.ones512 = A("ones512", [1, 512])
        self.gam = A("gam", [128, 8], F32)
        self.rwt = A("rwt", [128, 8, 8])
        self.rbias = A("rbias", [8, 1], F32)
        self.ebias = A("ebias", [8, 1024])
        self.MT = A("MT", [128, 16, 256])
        self.lgam = A("lgam", [128, 2, 8], F32)
        self.lbet = A("lbet", [128, 2, 8], F32)
        self.b1c = A("b1c", [128, 2, 44], F32)
        self.b2c = A("b2c", [128, 2, 8], F32)
        self.qksc = A("qksc", [128, 2], F32)
        self.ltT8 = A("ltT8", [128, 8], F32)
        self.oh8 = A("oh8", [128, 8])
        self.t8 = A("t8", [8, 128])
        self.g1x = A("g1x", [1, 128], F32)
        self.mx8 = A("mx8", [128, 1], F32)
        self.nmx8 = A("nmx8", [128, 1], F32)
        self.et8 = A("et8", [128, 8], F32)
        self.sm8 = A("sm8", [128, 1], F32)
        self.gt8 = A("gt8", [128, 1], F32)
        self.mkb = A("mkb", [128, 512])
        self.gb = A("gb", [128, 512], F32)
        self.sb512 = A("sb512", [128, 512], F32)
        self.sb512b = A("sb512b", [128, 512], F32)
        self.nq = A("nq", [128, 16], F32)
        self.nk = A("nk", [128, 16], F32)
        self.colA = A("colA", [128, 1], F32)
        self.colB = A("colB", [128, 1], F32)
        self.colC = A("colC", [128, 1], F32)
        self.p32 = A("p32", [128, 128])
        self.qs2 = A("qs2", [128, 2], F32)
        self.ks2 = A("ks2", [128, 2], F32)
        self.fint = A("fint", [1, 4], F32)
        self.pb = [nc.alloc_psum_tensor(f"pb{j}", [128, 512], F32).ap()
                   for j in range(8)]
        self.u5f = self.U5.bitcast(F32)
        self.u4f = self.U4.bitcast(F32)

    def P_last(self, eng):
        return Ev(eng, self.P.cnt[eng], self.P.blk_idx)

    def prelude(self):
        P = self.P
        i, d = self.i, self.d
        P.setup_consts(CONST_SCALARS)
        for t_ in (self.ones_r, self.ones_rf, self.ones_c, self.ones512):
            P.pool(lambda e, t=t_: e.memset(t, 1.0))
        P.pool(lambda e: e.memset(self.HD[:, :, :, 64:65], 1.0))
        self.ev_xcast = P.dma("gp", d["xb16"], i["x"])
        ce = self.cast_evs = {}
        for src, dst in [("L0_attn_Wq", "wq_b"), ("L0_attn_Wk", "wk_b"),
                         ("L0_attn_Wv", "wv_b"), ("L0_attn_Wo", "wo_b"),
                         ("L1_attn_w_qkv", "wqkv1_b"),
                         ("L1_attn_w_out", "wout1_b"),
                         ("L0_ffn_expert_w", "ew"), ("L1_ffn_wqkv", "wqkv_l"),
                         ("L1_ffn_wo", "wo_l"), ("L1_ffn_w1", "w1_l"),
                         ("L1_ffn_w2", "w2_l")]:
            ce[dst] = P.dma("gp", d[dst], i[src])
        self.ev_eye = P.dma("sp", self.eyef, i["C_EYEF"])
        self.ev_m0 = P.dma(
            "gp", self.U7[:, 0:2048].rearrange("p (j f) -> p j f", f=512),
            i["C_MASK0"].rearrange("j p f -> p j f"))
        self.ev_eyeb = P.dma("gp", self.eyeb, i["C_EYEF"])
        self.ev_p32 = P.dma("gp", self.p32, i["C_P32"])
        slow = {"allow_slow_non_contiguous": True}
        self.ev_gam = P.dma("sp", self.gam,
                            i["L0_attn_gamma"].rearrange("(c p) -> p c", p=128),
                            **slow)
        self.gam16 = self.nc.alloc_sbuf_tensor("gam16", [128, 8], BF16).ap()
        self.bet16 = self.nc.alloc_sbuf_tensor("bet16", [128, 8], BF16).ap()
        self.ev_gam16 = P.dma("gp", self.gam16,
                              i["L0_attn_gamma"].rearrange("(c p) -> p c", p=128),
                              **slow)
        self.ev_bet16 = P.dma("gp", self.bet16,
                              i["L0_attn_beta"].rearrange("(c p) -> p c", p=128),
                              **slow)
        self.ev_rw = P.dma("gp", self.rwt,
                           i["L0_ffn_router_w"].rearrange("(c p) e -> p c e",
                                                          p=128))
        self.ev_rb = P.dma("sp", self.rbias,
                           i["L0_ffn_router_b"].unsqueeze(1))
        self.ev_eb = P.dma("gp", self.ebias, i["L0_ffn_expert_b"])
        for l in range(2):
            self.ev_lgam = P.dma("sp", self.lgam[:, l, :],
                                 i["L1_ffn_ln_g"][l].rearrange(
                                     "(c p) -> p c", p=128), **slow)
            self.ev_lbet = P.dma("sp", self.lbet[:, l, :],
                                 i["L1_ffn_ln_b"][l].rearrange(
                                     "(c p) -> p c", p=128), **slow)
            self.ev_b1a = P.dma("sp", self.b1c[:, l, 0:21],
                                i["L1_ffn_b1"][l][0:2688].rearrange(
                                    "(c p) -> p c", p=128), **slow)
            P.dma("sp", self.b1c[0:42, l, 21:22],
                  i["L1_ffn_b1"][l][2688:2730].rearrange(
                      "(c p) -> p c", p=42), **slow)
            P.dma("sp", self.b1c[:, l, 22:43],
                  i["L1_ffn_b1"][l][2730:5418].rearrange(
                      "(c p) -> p c", p=128), **slow)
            self.ev_b1b = P.dma("sp", self.b1c[0:42, l, 43:44],
                                i["L1_ffn_b1"][l][5418:5460].rearrange(
                                    "(c p) -> p c", p=42), **slow)
            self.ev_b2c = P.dma("sp", self.b2c[:, l, :],
                                i["L1_ffn_b2"][l].rearrange(
                                    "(c p) -> p c", p=128), **slow)
            self.ev_qks = P.dma("sp", self.qs2[0:64, l:l + 1],
                                i["L1_ffn_q_scale"][l].unsqueeze(1), **slow)
            P.dma("sp", self.qs2[64:128, l:l + 1],
                  i["L1_ffn_q_scale"][l].unsqueeze(1), **slow)
            P.dma("sp", self.ks2[0:64, l:l + 1],
                  i["L1_ffn_k_scale"][l].unsqueeze(1), **slow)
            self.ev_kks = P.dma("sp", self.ks2[64:128, l:l + 1],
                                i["L1_ffn_k_scale"][l].unsqueeze(1), **slow)
        P.commit("pre0")
        for (src, dst) in [("wq_b", "wqT"), ("wk_b", "wkT"), ("wv_b", "wvT"),
                           ("wo_b", "woT"), ("wout1_b", "wout1T"),
                           ("wqkv1_b", "wqkv1T")]:
            self._pretranspose(d[src], d[dst], ce[src])
        self._build_MT()

    def _pretranspose(self, src, dst, ev):
        P = self.P
        R, C = src.shape[0], src.shape[1]
        prev = [getattr(self, "_pt_prev0", None), getattr(self, "_pt_prev1", None)]
        for c in range(C // 128):
            buf = c % 2
            st = self.U6[:, buf * 4096:buf * 4096 + R]
            w = [ev] + ([prev[buf]] if prev[buf] else [])
            e1 = P.dmaT(st, src[:, c * 128:(c + 1) * 128], wait=w)
            prev[buf] = P.dma("sp", dst[c * 128:(c + 1) * 128, :], st, wait=[e1])
        self._pt_prev0, self._pt_prev1 = prev[0], prev[1]
        self.pret_ev = prev[(C // 128 - 1) % 2]

    def _build_MT(self):
        P = self.P
        i = self.i
        nc = self.nc
        had = nc.alloc_sbuf_tensor("had", [64, 64], BF16).ap()
        e_h = P.dma("gp", had, i["C_HAD"])
        sg = nc.alloc_sbuf_tensor("sg_", [64, 16, 4, 3], F32).ap()
        e_s = P.dma("sp", sg, i["L1_attn_sorf_signs"].rearrange(
            "h r k p -> p h r k"), allow_slow_non_contiguous=True)
        scl = nc.alloc_sbuf_tensor("scl", [64, 16, 4], F32).ap()
        e_c = P.dma("sp", scl, i["L1_attn_sorf_scales"].rearrange(
            "h r p -> p h r"), allow_slow_non_contiguous=True)
        eye64 = nc.alloc_sbuf_tensor("eye64", [64, 64], F32).ap()
        e_e = P.dma("sp", eye64, i["C_EYEF"][0:64, 0:64])
        tmp = nc.alloc_sbuf_tensor("mtmp", [64, 64], BF16).ap()
        pb = self.pb
        prev_m3 = None
        for h in range(16):
            for r in range(4):
                w0 = [e_e, e_c, e_h, e_s] + ([prev_m3] if prev_m3 else [])
                e0 = P.dve(lambda e, h=h, r=r: e.tensor_scalar(
                    tmp, eye64, scl[:, h:h + 1, r], 1.0 / 4096.0,
                    AL.mult, AL.mult), wait=w0)
                m1 = P.pe(lambda e: e.matmul(pb[2][0:64, 0:64], had, tmp,
                                             start=True, stop=True),
                          wait=[e0], inc=True)
                e1 = P.dve(lambda e, h=h, r=r: e.tensor_scalar_mul(
                    tmp, pb[2][0:64, 0:64], sg[:, h:h + 1, r, 2]), wait=[m1])
                m2 = P.pe(lambda e: e.matmul(pb[3][0:64, 0:64], had, tmp,
                                             start=True, stop=True),
                          wait=[e1], inc=True)
                e2 = P.dve(lambda e, h=h, r=r: e.tensor_scalar_mul(
                    tmp, pb[3][0:64, 0:64], sg[:, h:h + 1, r, 1]), wait=[m2])
                m3 = P.pe(lambda e: e.matmul(pb[2][0:64, 0:64], had, tmp,
                                             start=True, stop=True),
                          wait=[e2], inc=True)
                prev_m3 = m3
                P.dve(lambda e, h=h, r=r: e.tensor_scalar_mul(
                    self.MT[0:64, h, r * 64:(r + 1) * 64],
                    pb[2][0:64, 0:64], sg[:, h:h + 1, r, 0]), wait=[m3])
                self.ev_MT = P.dve(lambda e, h=h, r=r: e.tensor_copy(
                    self.MT[64:128, h, r * 64:(r + 1) * 64],
                    self.MT[0:64, h, r * 64:(r + 1) * 64]))

    def ln_stats(self, xF, wait_evs):
        P = self.P
        pbS, pbQ = self.pb[6], self.pb[7]
        sq = self.U5.rearrange("p (c f) -> p c f", f=512)
        r = lambda o: self.rf[0:1, o:o + 2048]
        rb = lambda o: self.rb[0:1, o:o + 2048]
        evS = evQ = None
        prev_q = None
        for tt in range(NTT):
            sqe = []
            for c in range(CH):
                w = list(wait_evs) if tt == 0 else [prev_q]
                sqe.append(P.pool(lambda e, c=c, tt=tt: e.tensor_tensor(
                    sq[:, c, :], xF[:, c, tt * 512:(tt + 1) * 512],
                    xF[:, c, tt * 512:(tt + 1) * 512], AL.mult), wait=w))
            mS = mQ = None
            for c in range(CH):
                st = (c == 0)
                wS = (list(wait_evs) if tt == 0 else []) + \
                    ([evS] if (st and evS) else [])
                mS = P.pe(lambda e, c=c, tt=tt, st=st: e.matmul(
                    pbS[0:1, :], self.ones_c,
                    xF[:, c, tt * 512:(tt + 1) * 512],
                    start=st, stop=(c == CH - 1)), wait=wS or None,
                    inc=(c == CH - 1))
                wQ = [sqe[c]] + ([evQ] if (st and evQ) else [])
                mQ = P.pe(lambda e, c=c, tt=tt, st=st: e.matmul(
                    pbQ[0:1, :], self.ones_c, sq[:, c, :],
                    start=st, stop=(c == CH - 1)), wait=wQ,
                    inc=(c == CH - 1))
            prev_q = mQ
            evS = P.dve(lambda e, tt=tt: e.tensor_copy(
                self.rf[0:1, tt * 512:(tt + 1) * 512], pbS[0:1, :]), wait=[mS])
            evQ = P.dve(lambda e, tt=tt: e.tensor_copy(
                self.rf[0:1, 2048 + tt * 512:2048 + (tt + 1) * 512],
                pbQ[0:1, :]), wait=[mQ])
        P.act(lambda e: e.activation(rb(4096), r(0), AF.Copy,
                                     bias=0.0, scale=1.0 / E), wait=[evS])
        ea = P.act(lambda e: e.activation(r(4096), r(0), AF.Copy,
                                          bias=0.0, scale=1.0 / E))
        e3 = P.pool(lambda e: e.tensor_tensor(r(0), r(4096), r(4096), AL.mult),
                    wait=[ea])
        e4 = P.dve(lambda e: e.scalar_tensor_tensor(
            r(2048), r(2048), 1.0 / E, r(0), AL.mult, AL.subtract),
            wait=[e3, evQ])
        e5 = P.act(lambda e: e.activation(r(0), r(2048), AF.Ln,
                                          bias=EPS_LN, scale=1.0), wait=[e4])
        e6 = P.act(lambda e: e.activation(r(2048), r(0), AF.Exp,
                                          bias=0.0, scale=-0.5), wait=[e5])
        e7 = P.dve(lambda e: e.scalar_tensor_tensor(
            rb(2048), r(4096), -1.0, r(2048), AL.mult, AL.mult), wait=[e6])
        P.dve(lambda e: e.tensor_copy(r(4096), r(2048)), wait=[e7])
        e9 = P.dve(lambda e: e.tensor_copy(rb(0), r(2048)))
        return e9

    def bcast2(self, rowA, rowB, pbA, pbB, wait=None):
        P = self.P
        mA = P.pe(lambda e: e.matmul(pbA, self.ones_r, rowA,
                                     start=True, stop=True), wait=wait,
                  inc=True)
        mB = P.pe(lambda e: e.matmul(pbB, self.ones_r, rowB,
                                     start=True, stop=True), inc=True)
        return mA, mB

    def s_stats_xn(self, s):
        P = self.P
        d = self.d
        xbF = self.U4.rearrange("p (c f) -> p c f", f=T)
        te = P.dmaT_multi(
            [(xbF[:, c, :], d["xb16"][s][:, c * 128:(c + 1) * 128])
             for c in range(CH)], wait=[self.ev_xcast])
        tes = [te]
        ev = self.ln_stats(xbF, [te])
        xn = self.U0
        pbA, pbB = self.pb[4], self.pb[5]
        ee = None
        prevA = prevB = None
        for tt in range(NTT):
            w = [ev] + ([prevA] if prevA else []) + ([prevB] if prevB else [])
            mA, mB = self.bcast2(
                self.rb[0:1, tt * 512:(tt + 1) * 512],
                self.rb[0:1, 2048 + tt * 512:2048 + (tt + 1) * 512],
                pbA, pbB, wait=w)
            eB = P.dve(lambda e: e.tensor_copy(self.sb512, pbB), wait=[mB])
            for c in range(CH):
                t1 = P.dve(lambda e, c=c, tt=tt: e.tensor_tensor(
                    self.sb512b, xbF[:, c, tt * 512:(tt + 1) * 512], pbA,
                    AL.mult), wait=[mA, eB])
                ee = P.dve(lambda e, c=c, tt=tt: e.tensor_tensor(
                    xn[:, c, tt * 512:(tt + 1) * 512], self.sb512b,
                    self.sb512, AL.add))
            prevA, prevB = ee, ee
        if "xn" in self.dbg:
            o = self.dbg_dram(f"xn{s}", [CH, 128, T])
            for c in range(CH):
                self.out_evs.append(P.dma("gp", o[c], xn[:, c, :], wait=[ee]))
        P.commit(f"xn{s}")
        return ee

    def s_moe(self, s, ev_xn):
        P = self.P
        d = self.d
        xn = self.U0
        pb = self.pb
        lt = self.u5f[0:8, 2048:4096]
        ev_lt = None
        for tt in range(NTT):
            m = None
            for ic in range(CH):
                m = P.pe(lambda e, ic=ic, tt=tt: e.matmul(
                    pb[0][0:8, :], self.rwt[:, ic, :],
                    xn[:, ic, tt * 512:(tt + 1) * 512],
                    start=(ic == 0), stop=(ic == CH - 1)),
                    wait=([ev_xn, self.ev_rw, ev_lt] if ic == 0 else None),
                    inc=(ic == CH - 1))
            ev_lt = P.dve(lambda e, tt=tt: e.tensor_scalar_add(
                lt[:, tt * 512:(tt + 1) * 512], pb[0][0:8, :], self.rbias),
                wait=[m, self.ev_rb])
        evd = []
        prev = [None, None, None]
        for tc in range(NTC):
            mt = P.pe(lambda e, tc=tc: e.transpose(
                pb[1][:, 0:8], lt[0:8, tc * 128:(tc + 1) * 128],
                self.eyef[0:8, 0:8]),
                wait=[ev_lt, self.ev_eye] + ([prev[0]] if prev[0] else []),
                inc=True)
            e1 = P.dve(lambda e: e.tensor_copy(self.ltT8, pb[1][:, 0:8]),
                       wait=[mt])
            prev[0] = e1
            P.dve(lambda e: e.reduce_max(self.mx8, self.ltT8, axis=AX))
            e3 = P.dve(lambda e: e.tensor_scalar_mul(self.nmx8, self.mx8, -1.0))
            e4 = P.act(lambda e: e.activation(self.et8, self.ltT8, AF.Exp,
                                              bias=self.nmx8, scale=1.0),
                       wait=[e3])
            P.dve(lambda e: e.reduce_sum(self.sm8, self.et8, axis=AX),
                  wait=[e4])
            e6 = P.dve(lambda e: e.reciprocal(self.gt8, self.sm8))
            e7 = P.dve(lambda e: e.tensor_single_scalar(
                self.oh8, self.ltT8, self.mx8, AL.is_ge))
            mo = P.pe(lambda e: e.transpose(
                pb[2].bitcast(BF16)[0:8, 0:128], self.oh8, self.eyeb),
                wait=[e7, self.ev_eyeb] + ([prev[1]] if prev[1] else []),
                inc=True)
            mg = P.pe(lambda e: e.transpose(
                pb[3][0:1, 0:128], self.gt8, self.eyef),
                wait=[e6] + ([prev[2]] if prev[2] else []), inc=True)
            e8 = P.dve(lambda e: e.tensor_copy(
                self.t8, pb[2].bitcast(BF16)[0:8, 0:128]),
                wait=[mo] + evd[-2:])
            prev[1] = e8
            e9 = P.dve(lambda e: e.tensor_copy(self.g1x, pb[3][0:1, 0:128]),
                       wait=[mg])
            prev[2] = e9
            evd.append(P.dma("sp", d["disp"][:, tc * 128:(tc + 1) * 128],
                             self.t8, wait=[e8]))
            evd.append(P.dma("sp", d["gatec"][0:1, tc * 128:(tc + 1) * 128],
                             self.g1x, wait=[e9]))
        P.commit(f"rt{s}")
        ev_gr = P.dma("sp", self.rf[0:1, 0:2048], d["gatec"], wait=evd)
        disp8 = self.U7[0:8, 2048:4096]
        ev_d8 = P.dma("sp", disp8, d["disp"], wait=evd)
        xm = self.U5.rearrange("p (c f) -> p c f", f=512)
        mkb8 = self.U5[:, 8192:12288].rearrange("p (e f) -> p e f", f=512)
        dtt = self.U5[0:1, 12288:16384].rearrange("o (e f) -> o e f", f=512)
        g0t = self.U5[:, 12800:13312]
        ew = self.U4.rearrange("p (b c f) -> p b c f", b=2, f=1024)
        ev_ew = [None, None]
        ev_free = [None, None]
        last_evac = None
        for tt in range(NTT):
            e_dt = P.dma("sp", dtt,
                         d["disp"][:, tt * 512:(tt + 1) * 512].unsqueeze(0),
                         wait=evd + ([last_evac] if last_evac else []))
            mg = P.pe(lambda e, tt=tt: e.matmul(
                pb[0], self.ones_rf, self.rf[0:1, tt * 512:(tt + 1) * 512],
                start=True, stop=True),
                wait=[ev_gr] + ([last_evac] if last_evac else []), inc=True)
            eg = P.dve(lambda e: e.tensor_copy(self.gb, pb[0]), wait=[mg])
            emk = []
            mkprev = None
            for ee in range(8):
                mk = P.pe(lambda e, ee=ee: e.matmul(
                    pb[1], self.ones_r, dtt[:, ee, :],
                    start=True, stop=True),
                    wait=[e_dt] + ([mkprev] if mkprev else []), inc=True)
                mkprev = P.dve(lambda e, ee=ee: e.tensor_copy(
                    mkb8[:, ee, :], pb[1]), wait=[mk])
                emk.append(mkprev)
            mm_done = None
            for ee in range(8):
                buf = ee % 2
                w = [self.cast_evs["ew"]] + \
                    ([ev_free[buf]] if ev_free[buf] else [])
                ev_ew[buf] = P.dma("gp", ew[:, buf, :, :],
                                   d["ew"][ee].rearrange("(c p) o -> p c o",
                                                         p=128), wait=w)
                exm = []
                for c in range(CH):
                    exm.append(P.pool(lambda e, c=c, tt=tt, ee=ee:
                                      e.tensor_tensor(
                        xm[:, c, :], xn[:, c, tt * 512:(tt + 1) * 512],
                        mkb8[:, ee, :], AL.mult),
                        wait=[emk[ee], ev_xn] + ([mm_done] if mm_done else [])))
                for oc in range(8):
                    for ic in range(CH):
                        st = (ee == 0 and ic == 0)
                        w = [ev_ew[buf], exm[ic]]
                        if st:
                            w.append(emk[7])
                        mm = P.pe(lambda e, oc=oc, ic=ic, st=st, buf=buf:
                                  e.matmul(
                            pb[oc], ew[:, buf, ic, oc * 128:(oc + 1) * 128],
                            xm[:, ic, :], start=st, stop=False),
                            wait=w, inc=(oc == 7 and ic == CH - 1))
                mm_done = mm
                ev_free[buf] = mm
            evE = []
            for oc in range(8):
                evE.append(P.pe(lambda e, oc=oc, tt=tt: e.matmul(
                    pb[oc], self.ebias[0:8, oc * 128:(oc + 1) * 128],
                    disp8[:, tt * 512:(tt + 1) * 512],
                    start=False, stop=True),
                    wait=[ev_d8, self.ev_eb, mm_done], inc=True))
            prev_e2 = getattr(self, "_moe_prev_e2", None)
            for oc in range(8):
                e1 = P.dve(lambda e, oc=oc: e.scalar_tensor_tensor(
                    self.sb512, pb[oc], 1.0, self.gb, AL.mult, AL.mult),
                    wait=[evE[oc], eg] + ([prev_e2] if prev_e2 else []))
                e2 = P.act(lambda e: e.activation(
                    self.mkb, self.sb512, AF.Silu, bias=0.0, scale=1.0),
                    wait=[e1] + ([last_evac] if last_evac else []))
                prev_e2 = e2
                last_evac = P.dma("sp", d["g0"][oc][:, tt * 512:(tt + 1) * 512],
                                  self.mkb, wait=[e2])
            self._moe_prev_e2 = prev_e2
        if "g0" in self.dbg:
            o = self.dbg_dram(f"g0_{s}", [CH, 128, T], BF16)
            self.out_evs.append(P.dma("gp", o, d["g0"], wait=[last_evac]))
        P.commit(f"moe{s}")
        return last_evac

    def s_proj(self, s, xsrc, ev_x, wq_ap, wk_ap, wv_ap, dq, dk, dv, vmode):
        P = self.P
        pb = self.pb
        wv = self.U6.rearrange("p (c f) -> p c f", f=1024)
        ev_wv = P.dma("sp", wv, wv_ap.rearrange("(c p) o -> p c o", p=128),
                      wait=[x for x in (getattr(self, "_pt_prev0", None),
                                        getattr(self, "_pt_prev1", None)) if x])
        vstage = self.HC[:, 0:1040].rearrange("p (h dd) -> p h dd", dd=65)
        em = P.pool(lambda e: e.memset(vstage[:, :, 64:65], 1.0))
        ev_vdma = None
        prev_ev = [None, None]
        for tc in range(NTC):
            evh = []
            for half in range(2):
                b = half
                m = None
                for ic in range(CH):
                    w = ([ev_wv, ev_x] if ic == 0 else None)
                    if ic == 0 and prev_ev[b] is not None:
                        w = w + [prev_ev[b]]
                    m = P.pe(lambda e, ic=ic, tc=tc, half=half, b=b: e.matmul(
                        pb[b], xsrc[:, ic, tc * 128:(tc + 1) * 128],
                        wv[:, ic, half * 512:(half + 1) * 512],
                        start=(ic == 0), stop=(ic == CH - 1)),
                        wait=w, inc=(ic == CH - 1))
                if vmode == "L0":
                    srcap = pb[b].rearrange("p (dd h) -> p dd h", h=16)
                    dstap = vstage[:, :, 32 * half:32 * half + 32].rearrange(
                        "p h dd -> p dd h")
                else:
                    srcap = pb[b].rearrange("p (h dd) -> p h dd", dd=64)
                    dstap = vstage[:, 8 * half:8 * half + 8, 0:64]
                ee = P.dve(lambda e, dst=dstap, sr=srcap: e.tensor_copy(
                    dst, sr), wait=[m, ev_vdma] if ev_vdma else [m])
                prev_ev[b] = ee
                evh.append(ee)
            ev_vdma = P.dma("sp", dv[:, tc, :, :].rearrange("h p dd -> p h dd"),
                            vstage, wait=evh + [em])
        qsl = self.U6[:, 0:4096].rearrange("p (b m c f) -> p b m c f", b=2, m=2, f=128)
        ev_sl = [None, None]
        prev_mm = [None, None]
        ev_qdma = ev_kdma = None
        prev_evq = [None] * NTT
        prev_evk = [None] * NTT
        for oc in range(CH):
            b = oc % 2
            w = [ev_vdma] + ([prev_mm[b]] if prev_mm[b] else [])
            ev_sl[b] = P.dma_multi("sp", [
                (qsl[:, b, 0, :, :], wq_ap(oc).rearrange("(c p) o -> p c o", p=128)),
                (qsl[:, b, 1, :, :], wk_ap(oc).rearrange("(c p) o -> p c o", p=128)),
            ], wait=w)
            evq = []
            evk = []
            mlast = None
            for tt in range(NTT):
                mq = mk = None
                for ic in range(CH):
                    w0 = None
                    if ic == 0:
                        w0 = [ev_sl[b], ev_x]
                        if oc == 0:
                            w0 += [x for x in prev_ev if x]
                        if prev_evq[tt]:
                            w0.append(prev_evq[tt])
                    mq = P.pe(lambda e, ic=ic, tt=tt, b=b: e.matmul(
                        pb[tt], qsl[:, b, 0, ic, :],
                        xsrc[:, ic, tt * 512:(tt + 1) * 512],
                        start=(ic == 0), stop=(ic == CH - 1)),
                        wait=w0, inc=(ic == CH - 1))
                for ic in range(CH):
                    w0 = [prev_evk[tt]] if (ic == 0 and prev_evk[tt]) else None
                    mk = P.pe(lambda e, ic=ic, tt=tt, b=b: e.matmul(
                        pb[4 + tt], qsl[:, b, 1, ic, :],
                        xsrc[:, ic, tt * 512:(tt + 1) * 512],
                        start=(ic == 0), stop=(ic == CH - 1)),
                        wait=w0, inc=(ic == CH - 1))
                wq = [mq] + ([ev_qdma] if (tt == 0 and ev_qdma) else [])
                prev_evq[tt] = P.dve(lambda e, tt=tt: e.tensor_copy(
                    self.HA[:, tt * 512:(tt + 1) * 512], pb[tt]), wait=wq)
                evq.append(prev_evq[tt])
                wk = [mk] + ([ev_kdma] if (tt == 0 and ev_kdma) else [])
                prev_evk[tt] = P.dve(lambda e, tt=tt: e.tensor_copy(
                    self.HB[:, tt * 512:(tt + 1) * 512], pb[4 + tt]), wait=wk)
                evk.append(prev_evk[tt])
                mlast = mk
            prev_mm[b] = mlast
            ev_qdma = P.dma("sp", dq[oc], self.HA, wait=evq)
            ev_kdma = P.dma("sp", dk[oc], self.HB, wait=evk)
        P.commit(f"proj{s}_{vmode}")
        return ev_qdma, ev_kdma, ev_vdma

    def s_attn0(self, s, ev_q, ev_k, ev_v, ev_g0, ev_xn_user):
        P = self.P
        d = self.d
        pb = self.pb
        ek = self.U4.rearrange("p (sc f) -> p sc f", f=512)
        qkvt = self.U4[:, 8192:12288].rearrange("p (c f) -> p c f", f=512)
        g0t = self.U4[:, 0:4096].rearrange("p (c f) -> p c f", f=512)
        xbt = self.U4[:, 4096:8192].rearrange("p (c f) -> p c f", f=512)
        m0 = self.U7[:, 0:2048].rearrange("p (j f) -> p j f", f=512)
        sgf = self.u5f[:, 0:4096].rearrange("p (c f) -> p c f", f=512)
        xTf = self.u5f[:, 4096:8192].rearrange("p (sub f) -> p sub f", f=1024)
        woT = self.U6.rearrange("p (c f) -> p c f", f=1024)
        rbs = lambda o, n: self.rb[0:1, o:o + n]
        rfs = lambda o, n: self.rf[0:1, o:o + n]
        ev_hd1 = P.pool(lambda e: e.memset(self.HD[:, :, :, 64:65], 1.0))
        self._ev_hd1 = ev_hd1
        self.ev_m0 = P.dma(
            "gp", self.U7[:, 0:2048].rearrange("p (j f) -> p j f", f=512),
            self.i["C_MASK0"].rearrange("j p f -> p j f"))
        ev_wo = P.dma("sp", woT, d["woT"].rearrange("(c p) o -> p c o", p=128),
                      wait=[self.pret_ev])
        ev_wog = None
        for half in range(2):
            mw = None
            for ic in range(CH):
                mw = P.pe(lambda e, ic=ic, half=half: e.matmul(
                    pb[6][0:1, :], self.gam16[:, ic:ic + 1],
                    woT[:, ic, half * 512:(half + 1) * 512],
                    start=(ic == 0), stop=(ic == CH - 1)),
                    wait=([ev_wo, self.ev_gam16, ev_wog] if ic == 0 else None),
                    inc=(ic == CH - 1))
            P.dve(lambda e, half=half: e.tensor_copy(
                rbs(6144 + half * 512, 512), pb[6][0:1, :]), wait=[mw])
            mb = None
            for ic in range(CH):
                mb = P.pe(lambda e, ic=ic, half=half: e.matmul(
                    pb[7][0:1, :], self.bet16[:, ic:ic + 1],
                    woT[:, ic, half * 512:(half + 1) * 512],
                    start=(ic == 0), stop=(ic == CH - 1)),
                    wait=([self.ev_bet16] if ic == 0 else None),
                    inc=(ic == CH - 1))
            ev_wog = P.dve(lambda e, half=half: e.tensor_copy(
                rbs(7168 + half * 512, 512), pb[7][0:1, :]), wait=[mb])
        prev_x1dma = None
        prev_sg = [None] * CH
        prev_tr = None
        for TT in range(NTT):
            nsc = 4 * TT + 4
            exp_ev = [None, None, None]
            prev_av = None
            prev_qkvt_user = getattr(self, "_wo_done", None)
            prev_dinv = None
            HAs = [self.HA[:, 0:512], self.HA2]
            HBs = [self.HB, self.HB2]
            HDs = [self.HD, self.HD]
            if not hasattr(self, "_att_set_user"):
                self._att_set_user = [None, None]

            def issue_ha(hp, extra, nsc=nsc, TT=TT):
                b = hp % 2
                wl = [ev_q, ev_k, ev_v] + extra
                u = self._att_set_user[b]
                if u:
                    wl.append(u)
                return P.dma_multi("sp", [
                    (HAs[b], d["qs"][hp][:, TT * 512:(TT + 1) * 512]),
                    (HBs[b][:, 0:nsc * 128], d["ks"][hp][:, 0:nsc * 128]),
                ], wait=wl)

            def issue_hv(hp, extra, nsc=nsc):
                return P.dma_multi("sp", [
                    (self.HD[:, 0, 0:nsc, 0:64],
                     d["vs"][2 * hp][0:nsc].rearrange(
                         "tc p dd -> p tc dd")[:, :, 0:64]),
                    (self.HD[:, 1, 0:nsc, 0:64],
                     d["vs"][2 * hp + 1][0:nsc].rearrange(
                         "tc p dd -> p tc dd")[:, :, 0:64]),
                ], wait=[ev_v, self._ev_hd1] + extra)

            pending = issue_ha(0, [prev_av] if prev_av else [])
            for half in range(2):
                for hpi in range(4):
                    hp = half * 4 + hpi
                    bset = hp % 2
                    ha = pending
                    hv = issue_hv(hp, [prev_av] if prev_av else [])
                    if hp + 1 < 8:
                        pending = issue_ha(hp + 1, [])
                    ek2 = [ek, self.U5.rearrange("p (sc f) -> p sc f",
                                                 f=512)]
                    all_eks = [[], []]
                    for hi in range(2):
                        hr = hi * 64
                        for sc in range(nsc):
                            bk = (hi * nsc + sc) % 3
                            diag = (sc >= 4 * TT)
                            w = [ha] + ([exp_ev[bk]] if exp_ev[bk] else [])
                            mm = P.pe(lambda e, hr=hr, sc=sc, bk=bk,
                                      diag=diag, bset=bset: e.matmul(
                                pb[bk],
                                HBs[bset][hr:hr + 64, sc * 128:(sc + 1) * 128],
                                HAs[bset][hr:hr + 64, 0:512],
                                start=True, stop=(not diag),
                                tile_position=(hr, 0)), wait=w,
                                inc=(not diag))
                            if diag:
                                j = sc - 4 * TT
                                mm = P.pe(lambda e, j=j, bk=bk: e.matmul(
                                    pb[bk], self.eyeb, m0[:, j, :],
                                    start=False, stop=True),
                                    wait=[self.ev_m0, self.ev_eyeb], inc=True)
                            ee = P.act(lambda e, sc=sc, bk=bk, hi=hi,
                                       e2=ek2[hi]: e.activation(
                                e2[:, sc, :], pb[bk], AF.Exp,
                                bias=-SHIFT0, scale=0.125),
                                wait=[mm] + ([prev_av] if (sc == 0 and hi == 0
                                                           and prev_av)
                                             else []))
                            exp_ev[bk] = ee
                            all_eks[hi].append(ee)
                    for hi in range(2):
                        h = 2 * hp + hi
                        hr = hi * 64
                        eks = all_eks[hi]
                        mv = None
                        for sc in range(nsc):
                            w = [eks[sc], hv]
                            if sc == 0 and prev_av:
                                w.append(prev_av)
                            if sc == 0 and prev_qkvt_user:
                                w.append(prev_qkvt_user)
                                prev_qkvt_user = None
                            mv = P.pe(lambda e, hi=hi, sc=sc, nsc=nsc,
                                      bset=bset: e.matmul(
                                pb[4][0:65, :], HDs[bset][:, hi, sc, :],
                                ek2[hi][:, sc, :], start=(sc == 0),
                                stop=(sc == nsc - 1)), wait=w,
                                inc=(sc == nsc - 1))
                        hc = h // 2
                        P.dve(lambda e, hr=hr, hc=hc: e.tensor_copy(
                            qkvt[hr:hr + 64, hc, :], pb[4][0:64, :]),
                            wait=[mv])
                        prev_av = P.dve(lambda e, h=h: e.tensor_copy(
                            rfs((h % 8) * 512, 512), pb[4][64:65, :]))
                    self._att_set_user[bset] = prev_av
                einv = None
                for piece in range(2):
                    eL = P.act(lambda e, piece=piece: e.activation(
                        rfs(4096, 2048), rfs(piece * 2048, 2048),
                        AF.Ln, bias=0.0, scale=1.0), wait=[prev_av, einv])
                    einv = P.act(lambda e, piece=piece: e.activation(
                        rbs(piece * 2048, 2048), rfs(4096, 2048),
                        AF.Exp, bias=0.0, scale=-1.0), wait=[eL])
                for cc in range(4):
                    c = half * 4 + cc
                    w = [einv] + ([prev_dinv] if prev_dinv else [])
                    mA = P.pe(lambda e, cc=cc: e.matmul(
                        pb[5][0:64, :], self.ones_r[0:1, 0:64],
                        rbs((2 * cc) * 512, 512),
                        start=True, stop=True), wait=w, inc=False)
                    mB = P.pe(lambda e, cc=cc: e.matmul(
                        pb[5][64:128, :], self.ones_r[0:1, 0:64],
                        rbs((2 * cc + 1) * 512, 512),
                        start=True, stop=True,
                        tile_position=(0, 64)), inc=True)
                    prev_dinv = P.dve(lambda e, c=c: e.scalar_tensor_tensor(
                        qkvt[:, c, :], qkvt[:, c, :], 1.0, pb[5],
                        AL.bypass, AL.mult), wait=[mB])
            ev_qt = prev_dinv
            sqt = self.U5.rearrange("p (c f) -> p c f", f=512)
            esq = P.pool(lambda e: e.tensor_tensor(
                self.U5[:, 0:4096], self.U4[:, 8192:12288],
                self.U4[:, 8192:12288], AL.mult),
                wait=[ev_qt] + [x for x in prev_sg if x])
            mS = mQ = None
            for c in range(CH):
                mS = P.pe(lambda e, c=c: e.matmul(
                    pb[6][0:1, :], self.ones_c, qkvt[:, c, :],
                    start=(c == 0), stop=(c == CH - 1)),
                    wait=[ev_qt] if c == 0 else None, inc=(c == CH - 1))
                mQ = P.pe(lambda e, c=c: e.matmul(
                    pb[7][0:1, :], self.ones_c, sqt[:, c, :],
                    start=(c == 0), stop=(c == CH - 1)),
                    wait=[esq] if c == 0 else None, inc=(c == CH - 1))
            eA = P.dve(lambda e: e.tensor_copy(rfs(4096, 512), pb[6][0:1, :]),
                       wait=[mS])
            eB = P.dve(lambda e: e.tensor_copy(rfs(4608, 512), pb[7][0:1, :]),
                       wait=[mQ])
            eC = P.act(lambda e: e.activation(rfs(5120, 512), rfs(4096, 512),
                                              AF.Copy, bias=0.0, scale=1.0 / E),
                       wait=[eA])
            eD = P.pool(lambda e: e.tensor_tensor(
                rfs(4096, 512), rfs(5120, 512), rfs(5120, 512), AL.mult),
                wait=[eC])
            eE = P.dve(lambda e: e.scalar_tensor_tensor(
                rfs(4608, 512), rfs(4608, 512), 1.0 / E, rfs(4096, 512),
                AL.mult, AL.subtract), wait=[eD, eB])
            eF = P.act(lambda e: e.activation(rfs(4096, 512), rfs(4608, 512),
                                              AF.Ln, bias=EPS_LN, scale=1.0),
                       wait=[eE])
            eG = P.act(lambda e: e.activation(rfs(4608, 512), rfs(4096, 512),
                                              AF.Exp, bias=0.0, scale=-0.5),
                       wait=[eF])
            eH = P.dve(lambda e: e.scalar_tensor_tensor(
                rbs(4096, 512), rfs(5120, 512), -1.0, rfs(4608, 512),
                AL.mult, AL.mult), wait=[eG])
            eI = P.dve(lambda e: e.tensor_copy(rbs(4608, 512), rfs(4608, 512)))
            mR = P.pe(lambda e: e.matmul(
                pb[5], self.ones_r, rbs(4608, 512), start=True, stop=True),
                wait=[eI, prev_dinv], inc=True)
            ev_af = None
            for c in range(CH):
                ev_af = P.dve(lambda e, c=c: e.scalar_tensor_tensor(
                    qkvt[:, c, :], qkvt[:, c, :], self.gam[:, c:c + 1],
                    pb[5], AL.mult, AL.mult), wait=[mR, self.ev_gam, esq])
            ev_wo_mms = []
            for c in range(CH):
                mw = None
                for ic in range(CH):
                    w = [ev_af] if ic == 0 else None
                    if ic == 0 and prev_sg[c]:
                        w = w + [prev_sg[c]]
                    mw = P.pe(lambda e, c=c, ic=ic: e.matmul(
                        pb[c], woT[:, ic, c * 128:(c + 1) * 128],
                        qkvt[:, ic, :], start=(ic == 0), stop=False), wait=w)
                P.pe(lambda e, c=c: e.matmul(
                    pb[c], rbs(6144 + c * 128, 128), rbs(4096, 512),
                    start=False, stop=False), wait=[eH] if c == 0 else None)
                mw = P.pe(lambda e, c=c: e.matmul(
                    pb[c], rbs(7168 + c * 128, 128), self.ones512,
                    start=False, stop=True), wait=[ev_wog] if c == 0 else None,
                    inc=True)
                ev_wo_mms.append(mw)
            eg0 = P.dma("sp", g0t, d["g0"].rearrange(
                "c p t -> p c t")[:, :, TT * 512:(TT + 1) * 512],
                wait=[ev_g0, exp_ev[0], exp_ev[1], exp_ev[2], prev_av])
            tes = [P.dmaT(xbt[:, c, :],
                          d["xb16"][s][TT * 512:(TT + 1) * 512,
                                       c * 128:(c + 1) * 128],
                          wait=[self.ev_xcast, prev_av, exp_ev[0],
                                exp_ev[1], exp_ev[2]])
                   for c in range(CH)]
            ex = P.dma("sp", xTf,
                       self.i["x"][s][TT * 512:(TT + 1) * 512, :].rearrange(
                           "(sub p) e -> p sub e", p=128),
                       wait=[prev_x1dma, prev_tr] if prev_x1dma else
                       ([prev_tr] if prev_tr else None))
            trs = []
            for c in range(CH):
                e1 = P.dve(lambda e, c=c: e.scalar_tensor_tensor(
                    sgf[:, c, :], pb[c], 1.0, g0t[:, c, :],
                    AL.bypass, AL.mult), wait=[ev_wo_mms[c], eg0])
                prev_sg[c] = e1
                P.pool(lambda e, c=c, TT=TT: e.tensor_tensor(
                    self.U0[:, c, TT * 512:(TT + 1) * 512], xbt[:, c, :],
                    sgf[:, c, :], AL.add), wait=[e1, tes[c], ev_xn_user])
                tr = None
                for sub in range(4):
                    mt = P.pe(lambda e, c=c, sub=sub: e.transpose(
                        pb[c][:, 0:128], sgf[:, c, sub * 128:(sub + 1) * 128],
                        self.eyef), wait=[e1, ex] + ([tr] if tr else []),
                        inc=True)
                    tr = P.dve(lambda e, c=c, sub=sub: e.tensor_tensor(
                        xTf[:, sub, c * 128:(c + 1) * 128],
                        xTf[:, sub, c * 128:(c + 1) * 128],
                        pb[c][:, 0:128], AL.add), wait=[mt])
                    trs.append(tr)
            prev_tr = trs[-1]
            prev_x1dma = P.dma(
                "sp", d["x1T"][s][TT * 512:(TT + 1) * 512, :].rearrange(
                    "(sub p) e -> p sub e", p=128), xTf, wait=trs)
            self._wo_done = prev_sg[7]
            P.commit(f"at{s}_{TT}")
        if "x1T" in self.dbg:
            o = self.dbg_dram(f"x1T{s}", [T, E])
            self.out_evs.append(P.dma("sp", o, d["x1T"][s],
                                      wait=[prev_x1dma]))
        return prev_x1dma

    def s_ln1_y(self, s, ev_x1):
        P = self.P
        ev = self.ln_stats(self.U0, [ev_x1])
        pbA, pbB = self.pb[4], self.pb[5]
        ee = None
        prev = None
        for tt in range(NTT):
            w = [ev] + ([prev] if prev else [])
            mA, mB = self.bcast2(
                self.rb[0:1, tt * 512:(tt + 1) * 512],
                self.rb[0:1, 2048 + tt * 512:2048 + (tt + 1) * 512],
                pbA, pbB, wait=w)
            eB = P.dve(lambda e: e.tensor_copy(self.sb512, pbB), wait=[mB])
            for c in range(CH):
                P.dve(lambda e, c=c, tt=tt: e.tensor_tensor(
                    self.sb512b, self.U0[:, c, tt * 512:(tt + 1) * 512],
                    pbA, AL.mult), wait=[mA, eB])
                ee = P.dve(lambda e, c=c, tt=tt: e.tensor_tensor(
                    self.U0[:, c, tt * 512:(tt + 1) * 512], self.sb512b,
                    self.sb512, AL.add))
            prev = ee
        if "y" in self.dbg:
            o = self.dbg_dram(f"y{s}", [CH, 128, T])
            for c in range(CH):
                self.out_evs.append(P.dma("gp", o[c], self.U0[:, c, :],
                                          wait=[ee]))
        P.commit(f"ln1_{s}")
        return ee


    def s_attn1(self, s, ev_q, ev_k, ev_v):
        P = self.P
        d = self.d
        pb = self.pb
        u5f = self.u5f
        ropeC = u5f[:, 0:2048]
        ropeS = u5f[:, 2048:4096]
        eq = self.U5[:, 8192:12288].rearrange("p (m f) -> p m f", f=T)
        ekT = self.U5[:, 12288:16384].rearrange("p (tc f) -> p tc f", f=256)
        outh_T = self.U4.rearrange("p (tc f) -> p tc f", f=1024)
        Ssb = self.HC[:, 0:130].rearrange("p (m f) -> p m f", f=65)
        ev_rope = P.dma("sp", u5f[:, 0:4096].rearrange(
            "p (j f) -> p j f", f=2048),
            self.i["C_ROPE"].rearrange("j p f -> p j f"))
        prev_eq = prev_ek = prev_S = prev_out = None
        prev_load = None
        for h in range(16):
            hc, hr = h // 2, (h % 2) * 64
            wl = [ev_q, ev_k, ev_v] + ([prev_load] if prev_load else [])
            ha = P.dma_multi("sp", [
                (self.HA[0:64, :], d["q1"][hc][hr:hr + 64, :]),
                (self.HA[64:128, :], d["k1"][hc][hr:hr + 64, :]),
                (self.HD[:, 0, :, :], d["v1"][h].rearrange("tc p dd -> p tc dd")),
            ], wait=wl)
            mask = []
            for q_ in range(16):
                mask += [2 * q_ + 1, 2 * q_]
            e1 = P.dve(lambda e: e.stream_shuffle(self.HB, self.HA, mask),
                       wait=[ha] + ([prev_out] if prev_out else []))
            e2 = P.pool(lambda e: e.tensor_tensor(self.HC, self.HA, ropeC,
                                                  AL.mult),
                        wait=[ha, ev_rope] + ([prev_S] if prev_S else []) +
                        ([prev_out] if prev_out else []))
            e3 = P.pool(lambda e: e.tensor_tensor(self.HB, self.HB, ropeS,
                                                  AL.mult), wait=[e1])
            e4 = P.pool(lambda e: e.tensor_tensor(self.HA, self.HC, self.HB,
                                                  AL.add))
            e5 = P.pool(lambda e: e.tensor_tensor(self.HB, self.HA, self.HA,
                                                  AL.mult), wait=[e4])
            for tc in range(NTC):
                mq = P.pe(lambda e, tc=tc: e.matmul(
                    pb[5][:, tc:tc + 1], self.HB[0:64, tc * 128:(tc + 1) * 128],
                    self.ones_c[0:64, :], start=True, stop=True),
                    wait=([e5] + ([prev_eq] if prev_eq else []))
                    if tc == 0 else None, inc=(tc == NTC - 1))
                mk = P.pe(lambda e, tc=tc: e.matmul(
                    pb[6][:, tc:tc + 1], self.HB[64:128, tc * 128:(tc + 1) * 128],
                    self.ones_c[64:128, :], start=True, stop=True,
                    tile_position=(64, 0)),
                    wait=None, inc=(tc == NTC - 1))
            enq = P.dve(lambda e: e.tensor_scalar(
                self.nq, pb[5][:, 0:16], -0.5, -5.545177444479562,
                AL.mult, AL.add), wait=[mq])
            efa = P.act(lambda e: e.activation(
                self.gb[:, 0:16], self.nq, AF.Exp, bias=0.0, scale=1.0),
                wait=[enq] + ([prev_out] if prev_out else []))
            enk = P.dve(lambda e: e.tensor_scalar_mul(self.nk, pb[6][:, 0:16],
                                                      -0.5), wait=[mk])
            peq = []
            for mc in range(2):
                for tt in range(NTT):
                    bk = (mc * NTT + tt) % 2
                    w = [e4, self.ev_MT] + ([peq[-2]] if len(peq) >= 2 else [])
                    m1 = P.pe(lambda e, mc=mc, tt=tt, bk=bk, h=h: e.matmul(
                        pb[bk], self.MT[0:64, h, mc * 128:(mc + 1) * 128],
                        self.HA[0:64, tt * 512:(tt + 1) * 512],
                        start=True, stop=True), wait=w, inc=True)
                    ee = P.act(lambda e, mc=mc, tt=tt, bk=bk: e.activation(
                        eq[:, mc, tt * 512:(tt + 1) * 512], pb[bk], AF.Exp,
                        bias=0.0, scale=1.0), wait=[m1, prev_eq])
                    peq.append(ee)
            prev_eq = peq[-1]
            pek = []
            for tc in range(NTC):
                bk = 2 + tc % 2
                w = [e4, self.ev_MT] + ([pek[-2]] if len(pek) >= 2 else [])
                m1 = P.pe(lambda e, tc=tc, bk=bk, h=h: e.matmul(
                    pb[bk][:, 0:256], self.HA[64:128, tc * 128:(tc + 1) * 128],
                    self.MT[64:128, h, :], start=True, stop=True,
                    tile_position=(64, 0)), wait=w, inc=True)
                ee = P.act(lambda e, tc=tc, bk=bk: e.activation(
                    ekT[:, tc, :], pb[bk][:, 0:256], AF.Exp,
                    bias=self.nk[:, tc:tc + 1], scale=1.0),
                    wait=[m1, enk, prev_ek])
                pek.append(ee)
            prev_ek = pek[-1]
            eS = []
            for mc in range(2):
                mS = None
                for tc in range(NTC):
                    w = [pek[tc], ha] + ([prev_S] if (tc == 0 and prev_S) else [])
                    if tc == 0 and mc == 1:
                        w.append(eS[0])
                    mS = P.pe(lambda e, mc=mc, tc=tc: e.matmul(
                        pb[4][:, 0:65], ekT[:, tc, mc * 128:(mc + 1) * 128],
                        self.HD[:, 0, tc, :], start=(tc == 0),
                        stop=(tc == NTC - 1)), wait=w if tc == 0 else [pek[tc]],
                        inc=(tc == NTC - 1))
                eS.append(P.dve(lambda e, mc=mc: e.tensor_copy(
                    Ssb[:, mc, :], pb[4][:, 0:65]), wait=[mS]))
            prev_S = eS[-1]
            for tc in range(NTC):
                m1 = None
                for mc in range(2):
                    w = [eS[mc], peq[-1]]
                    if mc == 0 and prev_out:
                        w.append(prev_out)
                    m1 = P.pe(lambda e, mc=mc, tc=tc: e.matmul(
                        pb[7][:, 0:65],
                        eq[:, mc, tc * 128:(tc + 1) * 128], Ssb[:, mc, :],
                        start=(mc == 0), stop=(mc == 1)), wait=w,
                        inc=(mc == 1))
                fa = P.act(lambda e, tc=tc: e.activation(
                    self.colA, self.nq[:, tc:tc + 1], AF.Exp,
                    bias=0.0, scale=1.0),
                    wait=[enq] + ([prev_out] if prev_out else []))
                u1 = P.dve(lambda e: e.tensor_scalar(
                    self.colB, pb[7][:, 64:65], self.colA, 1.0 / 256.0,
                    AL.mult, AL.mult), wait=[m1, fa])
                u2 = P.dve(lambda e: e.tensor_scalar_max(
                    self.colB, self.colB, 1e-6))
                u3 = P.dve(lambda e: e.reciprocal(self.colC, self.colB))
                u4 = P.dve(lambda e: e.tensor_scalar(
                    self.colB, self.colC, self.colA, 1.0 / 256.0,
                    AL.mult, AL.mult))
                prev_out = P.dve(lambda e, tc=tc, h=h: e.tensor_scalar_mul(
                    outh_T[:, tc, h * 64:(h + 1) * 64], pb[7][:, 0:64],
                    self.colB), wait=[u4])
            prev_load = prev_out
            P.commit(f"l1a{s}_{h}")
        woutT = self.U6.rearrange("p (c f) -> p c f", f=1024)
        ev_wT = P.dma("sp", woutT,
                      d["wout1T"].rearrange("(c p) o -> p c o", p=128),
                      wait=[self.pret_ev])
        outF = self.U5[:, 0:4096].rearrange("p (c f) -> p c f", f=512)
        prev_mm = None
        prev_dma = None
        ev_s1 = None
        last_tr = [None, None]
        for tt in range(NTT):
            etr = []
            for ic in range(4):
                tc = tt * 4 + ic
                for c in range(CH):
                    bk = c % 2
                    w = [x for x in (prev_out, prev_mm, last_tr[bk]) if x]
                    mt = P.pe(lambda e, tc=tc, c=c, bk=bk: e.transpose(
                        pb[bk].bitcast(BF16)[:, 0:128],
                        outh_T[:, tc, c * 128:(c + 1) * 128], self.eyeb),
                        wait=w or None, inc=True)
                    last_tr[bk] = P.dve(lambda e, tc=tc, c=c, ic=ic, bk=bk:
                                        e.tensor_copy(
                        outF[:, c, ic * 128:(ic + 1) * 128],
                        pb[bk].bitcast(BF16)[:, 0:128]), wait=[mt])
                    etr.append(last_tr[bk])
            evs1 = []
            if not hasattr(self, "_wout_ec"):
                self._wout_ec = [None] * 4
            for oc in range(CH):
                mw = None
                for ic in range(CH):
                    w = ([ev_wT] + etr) if ic == 0 else None
                    if ic == 0 and self._wout_ec[oc % 4]:
                        w = w + [self._wout_ec[oc % 4]]
                    mw = P.pe(lambda e, oc=oc, ic=ic: e.matmul(
                        pb[2 + oc % 4], woutT[:, ic, oc * 128:(oc + 1) * 128],
                        outF[:, ic, :], start=(ic == 0), stop=(ic == CH - 1)),
                        wait=w, inc=(ic == CH - 1))
                ec = P.dve(lambda e, oc=oc: e.tensor_copy(self.mkb, pb[2 + oc % 4]),
                           wait=[mw] + ([prev_dma] if prev_dma else []))
                self._wout_ec[oc % 4] = ec
                prev_dma = P.dma("sp", d["s1"][oc][:, tt * 512:(tt + 1) * 512],
                                 self.mkb, wait=[ec])
                evs1.append(prev_dma)
            prev_mm = Ev("pe", P.cnt["pe"], P.blk_idx)
            ev_s1 = evs1[-1]
            P.commit(f"l1aw{s}_{tt}")
        return ev_s1


    def s_ffn1(self, s, ev_y, ev_s1):
        P = self.P
        d = self.d
        pb = self.pb
        hs = self.U4.rearrange("p (c f) -> p c f", f=T)
        vT1 = self.U5[:, 4096:12416].rearrange("p (tc h dd) -> p tc h dd",
                                               h=8, dd=65)
        m1m = self.U7.rearrange("p (j f) -> p j f", f=512)
        ev_m1 = P.dma("gp", m1m, self.i["C_MASK1"].rearrange("j p f -> p j f"),
                      wait=[ev_y])
        hdf = self.HD.rearrange("p a b c -> p (a b c)")
        sqB = hdf[:, 0:2048]
        xpt = self.U5[:, 0:2048].rearrange("p (j f) -> p j f", f=512)
        rfs = lambda o, n: self.rf[0:1, o:o + n]
        rbs = lambda o, n: self.rb[0:1, o:o + n]
        prev_done = ev_s1
        for l in range(2):
            ehs = []
            for c in range(CH):
                w = [ev_y, ev_s1] if (l == 0 and c == 0) else \
                    ([prev_done] if c == 0 else None)
                if c < 4:
                    ehs.append(P.pool(lambda e, c=c: e.tensor_copy(
                        hs[:, c, :], self.U0[:, c, :]), wait=w))
                else:
                    P.pool(lambda e, c=c: e.memset(hs[:, c, 0:1], 0.0), wait=w)
                    ehs.append(P.pool(lambda e, c=c: e.tensor_copy(
                        hs[:, c, 1:T], self.U0[:, c, 0:T - 1])))
            ev_hs = ehs[-1]
            wqk = self.U6.rearrange("p (c f) -> p c f", f=1024)
            ev_wqk = P.dma("sp", wqk, d["wqkv_l"][l][:, 0:1024].rearrange(
                "(c p) o -> p c o", p=128), wait=[self.cast_evs["wqkv_l"],
                                                  prev_done])
            wv = self.U5[:, 0:4096].rearrange("p (c f) -> p c f", f=512)
            ev_wv = P.dma("sp", wv, d["wqkv_l"][l][:, 1024:1536].rearrange(
                "(c p) o -> p c o", p=128), wait=[self.cast_evs["wqkv_l"],
                                                  prev_done])
            em0 = P.pool(lambda e: e.memset(vT1[:, :, :, 64:65], 1.0))
            prev_ev = [None, None]
            for tc in range(NTC):
                b = tc % 2
                m = None
                for ic in range(CH):
                    w = ([ev_wv, ev_hs] + ([prev_ev[b]] if prev_ev[b] else [])) \
                        if ic == 0 else None
                    m = P.pe(lambda e, ic=ic, tc=tc, b=b: e.matmul(
                        pb[b], hs[:, ic, tc * 128:(tc + 1) * 128],
                        wv[:, ic, :], start=(ic == 0), stop=(ic == CH - 1)),
                        wait=w, inc=(ic == CH - 1))
                prev_ev[b] = P.dve(lambda e, tc=tc, b=b: e.tensor_copy(
                    vT1[:, tc, :, 0:64],
                    pb[b].rearrange("p (h dd) -> p h dd", dd=64)),
                    wait=[m, em0])
            ev_v1 = [x for x in prev_ev if x]
            ek1 = self.U5[:, 0:4096].rearrange("p (j f) -> p j f", f=512)
            prev_abuser = None
            prev_osb = None
            for hp in range(4):
                h0, h1 = 2 * hp, 2 * hp + 1
                evq = []
                for hi in range(2):
                    h = 2 * hp + hi
                    for tt in range(NTT):
                        bk = hi * 4 + tt
                        m = None
                        for ic in range(CH):
                            w = None
                            if ic == 0:
                                w = [ev_wqk, ev_hs] + list(ev_v1) + \
                                    ([prev_abuser] if prev_abuser else [])
                            P.pe(lambda e, h=h, ic=ic, tt=tt, bk=bk:
                                 e.matmul(
                                pb[bk][0:64, :],
                                wqk[:, ic, h * 64:h * 64 + 64],
                                hs[:, ic, tt * 512:(tt + 1) * 512],
                                start=(ic == 0), stop=(ic == CH - 1)),
                                wait=w)
                        for ic in range(CH):
                            m = P.pe(lambda e, h=h, ic=ic, tt=tt, bk=bk:
                                     e.matmul(
                                pb[bk][64:128, :],
                                wqk[:, ic, 512 + h * 64:512 + h * 64 + 64],
                                hs[:, ic, tt * 512:(tt + 1) * 512],
                                start=(ic == 0), stop=(ic == CH - 1),
                                tile_position=(0, 64)),
                                inc=(ic == CH - 1))
                        evq.append(P.dve(lambda e, hi=hi, tt=tt, bk=bk:
                                         e.tensor_copy(
                            self.HA[hi * 64:hi * 64 + 64,
                                    tt * 512:(tt + 1) * 512],
                            pb[bk][0:64, :]), wait=[m]))
                        evq.append(P.dve(lambda e, hi=hi, tt=tt, bk=bk:
                                         e.tensor_copy(
                            self.HB[hi * 64:hi * 64 + 64,
                                    tt * 512:(tt + 1) * 512],
                            pb[bk][64:128, :]), wait=[m]))
                eA = P.pool(lambda e: e.tensor_tensor(self.HC, self.HA,
                                                      self.HA, AL.mult),
                            wait=evq + ([prev_osb] if prev_osb else []))
                eB = P.pool(lambda e: e.tensor_tensor(sqB, self.HB, self.HB,
                                                      AL.mult), wait=evq)
                rb_slot = {(0, 0): 0, (0, 1): 2048, (1, 0): 4096, (1, 1): 6144}
                prev_r = None
                for (ti, hi) in [(0, 0), (0, 1), (1, 0), (1, 1)]:
                    sq = self.HC if ti == 0 else sqB
                    esq = eA if ti == 0 else eB
                    ecp = []
                    for tt in range(NTT):
                        mr = P.pe(lambda e, sq=sq, hi=hi, tt=tt: e.matmul(
                            pb[6][0:1, :], self.ones_c[hi * 64:hi * 64 + 64, :],
                            sq[hi * 64:hi * 64 + 64, tt * 512:(tt + 1) * 512],
                            start=True, stop=True,
                            tile_position=(hi * 64, 0)),
                            wait=[esq] + ([prev_r] if prev_r else []),
                            inc=True)
                        prev_r = P.dve(lambda e, tt=tt: e.tensor_copy(
                            rfs(4096 + tt * 512, 512), pb[6][0:1, :]),
                            wait=[mr])
                        ecp.append(prev_r)
                    e1 = P.dve(lambda e: e.tensor_scalar_max(
                        rfs(4096, 2048), rfs(4096, 2048), 1e-24))
                    e2 = P.act(lambda e: e.activation(
                        rfs(0, 2048), rfs(4096, 2048), AF.Ln,
                        bias=0.0, scale=1.0), wait=[e1])
                    e3 = P.act(lambda e: e.activation(
                        rfs(4096, 2048), rfs(0, 2048), AF.Exp,
                        bias=0.0, scale=-0.5), wait=[e2])
                    prev_r = P.dve(lambda e, sl=rb_slot[(ti, hi)]:
                                   e.tensor_copy(rbs(sl, 2048),
                                                 rfs(4096, 2048)),
                                   wait=[e3])
                ev_inv = prev_r
                prev_x = None
                for tt in range(NTT):
                    ext = P.dma("gp", xpt,
                                self.i["C_XPOS"][:, :, tt * 512:(tt + 1) * 512]
                                .rearrange("j p f -> p j f"),
                                wait=[prev_x] if prev_x else None)
                    mqa = P.pe(lambda e, tt=tt: e.matmul(
                        pb[2][0:64, :], self.ones_r[0:1, 0:64],
                        rbs(0 + tt * 512, 512), start=True, stop=True),
                        wait=[ev_inv], inc=True)
                    mqb = P.pe(lambda e, tt=tt: e.matmul(
                        pb[2][64:128, :], self.ones_r[0:1, 0:64],
                        rbs(2048 + tt * 512, 512), start=True, stop=True,
                        tile_position=(0, 64)), inc=True)
                    mka = P.pe(lambda e, tt=tt: e.matmul(
                        pb[3][0:64, :], self.ones_r[0:1, 0:64],
                        rbs(4096 + tt * 512, 512), start=True, stop=True),
                        inc=True)
                    mkb = P.pe(lambda e, tt=tt: e.matmul(
                        pb[3][64:128, :], self.ones_r[0:1, 0:64],
                        rbs(6144 + tt * 512, 512), start=True, stop=True,
                        tile_position=(0, 64)), inc=True)
                    en1 = P.dve(lambda e, tt=tt, l=l: e.scalar_tensor_tensor(
                        self.HA[:, tt * 512:(tt + 1) * 512],
                        self.HA[:, tt * 512:(tt + 1) * 512],
                        self.qs2[:, l:l + 1], pb[2], AL.mult, AL.mult),
                        wait=[mqa, mqb])
                    en2 = P.dve(lambda e, tt=tt, l=l: e.scalar_tensor_tensor(
                        self.HB[:, tt * 512:(tt + 1) * 512],
                        self.HB[:, tt * 512:(tt + 1) * 512],
                        self.ks2[:, l:l + 1], pb[3], AL.mult, AL.mult),
                        wait=[mka, mkb])
                    msa = P.pe(lambda e, tt=tt: e.matmul(
                        pb[4], self.p32, self.HA[:, tt * 512:(tt + 1) * 512],
                        start=True, stop=True),
                        wait=[en1, self.ev_p32], inc=True)
                    msb = P.pe(lambda e, tt=tt: e.matmul(
                        pb[5], self.p32, self.HB[:, tt * 512:(tt + 1) * 512],
                        start=True, stop=True), wait=[en2], inc=True)
                    t1 = P.dve(lambda e, tt=tt: e.tensor_tensor(
                        self.sb512, self.HA[:, tt * 512:(tt + 1) * 512],
                        xpt[:, 0, :], AL.mult), wait=[ext, en1])
                    t2 = P.dve(lambda e, tt=tt: e.scalar_tensor_tensor(
                        self.sb512b, pb[4], 1.0, xpt[:, 1, :],
                        AL.mult, AL.mult), wait=[msa])
                    ea1 = P.dve(lambda e, tt=tt: e.tensor_tensor(
                        self.HA[:, tt * 512:(tt + 1) * 512], self.sb512,
                        self.sb512b, AL.add))
                    t3 = P.dve(lambda e, tt=tt: e.tensor_tensor(
                        self.sb512, self.HB[:, tt * 512:(tt + 1) * 512],
                        xpt[:, 2, :], AL.mult), wait=[en2])
                    t4 = P.dve(lambda e, tt=tt: e.scalar_tensor_tensor(
                        self.sb512b, pb[5], 1.0, xpt[:, 3, :],
                        AL.mult, AL.mult), wait=[msb])
                    prev_x = P.dve(lambda e, tt=tt: e.tensor_tensor(
                        self.HB[:, tt * 512:(tt + 1) * 512], self.sb512,
                        self.sb512b, AL.add))
                exp_ev = [None, None]
                prev_av = None
                osb_ev = []
                for hi in range(2):
                    h = 2 * hp + hi
                    for wi in range(NTT):
                        jcs = list(range(4, 8)) if wi == 0 else list(range(8))
                        eks = {}
                        for jc in jcs:
                            kc = 4 * (wi - 1) + jc
                            bk = jc % 2
                            w = [prev_x] + ([exp_ev[bk]] if exp_ev[bk] else [])
                            P.pe(lambda e, hi=hi, jc=jc, kc=kc, wi=wi, bk=bk:
                                 e.matmul(
                                pb[bk],
                                self.HB[hi * 64:hi * 64 + 64,
                                        kc * 128:(kc + 1) * 128],
                                self.HA[hi * 64:hi * 64 + 64,
                                        wi * 512:(wi + 1) * 512],
                                start=True, stop=False,
                                tile_position=(hi * 64, 0)), wait=w)
                            mm = P.pe(lambda e, jc=jc, bk=bk: e.matmul(
                                pb[bk], self.eyeb, m1m[:, jc, :],
                                start=False, stop=True),
                                wait=[ev_m1, self.ev_eyeb], inc=True)
                            ee = P.act(lambda e, jc=jc, bk=bk: e.activation(
                                ek1[:, jc, :], pb[bk], AF.Exp,
                                bias=-SHIFT1, scale=8.0), wait=[mm])
                            exp_ev[bk] = ee
                            eks[jc] = ee
                        mv = None
                        first = True
                        for jc in jcs:
                            kc = 4 * (wi - 1) + jc
                            w = [eks[jc]] + ([prev_av] if (first and prev_av)
                                             else [])
                            if hi == 0:
                                mv = P.pe(lambda e, h=h, kc=kc, jc=jc,
                                          first=first: e.matmul(
                                    pb[4][0:65, :], vT1[:, kc, h, :],
                                    ek1[:, jc, :], start=first,
                                    stop=(jc == jcs[-1])), wait=w,
                                    inc=(jc == jcs[-1]))
                            else:
                                mv = P.pe(lambda e, h=h, kc=kc, jc=jc,
                                          first=first: e.matmul(
                                    pb[5][64:128, :], vT1[:, kc, h, 0:64],
                                    ek1[:, jc, :], start=first,
                                    stop=(jc == jcs[-1]),
                                    tile_position=(0, 64)), wait=w,
                                    inc=False)
                                mv = P.pe(lambda e, h=h, kc=kc, jc=jc,
                                          first=first: e.matmul(
                                    pb[6][0:1, :], vT1[:, kc, h, 64:65],
                                    ek1[:, jc, :], start=first,
                                    stop=(jc == jcs[-1])), wait=[eks[jc]],
                                    inc=(jc == jcs[-1]))
                            first = False
                        if hi == 0:
                            prev_av = P.dve(lambda e, wi=wi: e.tensor_copy(
                                rfs(wi * 512, 512), pb[4][64:65, :]),
                                wait=[mv])
                        else:
                            prev_av = P.dve(lambda e, wi=wi: e.tensor_copy(
                                rfs(2048 + wi * 512, 512), pb[6][0:1, :]),
                                wait=[mv])
                        if hi == 0:
                            prev_av = P.dve(lambda e, wi=wi: e.tensor_copy(
                                self.HC[0:64, wi * 512:(wi + 1) * 512],
                                pb[4][0:64, :]))
                        else:
                            prev_av = P.dve(lambda e, wi=wi: e.tensor_copy(
                                self.HC[64:128, wi * 512:(wi + 1) * 512],
                                pb[5][64:128, :]))
                        osb_ev.append(prev_av)
                eL = P.act(lambda e: e.activation(rfs(4096, 2048),
                                                  rfs(0, 2048), AF.Ln,
                                                  bias=0.0, scale=1.0),
                           wait=[prev_av])
                eE1 = P.act(lambda e: e.activation(rbs(0, 2048),
                                                   rfs(4096, 2048), AF.Exp,
                                                   bias=0.0, scale=-1.0),
                            wait=[eL])
                eL2 = P.act(lambda e: e.activation(rfs(4096, 2048),
                                                   rfs(2048, 2048), AF.Ln,
                                                   bias=0.0, scale=1.0))
                eE2 = P.act(lambda e: e.activation(rbs(2048, 2048),
                                                   rfs(4096, 2048), AF.Exp,
                                                   bias=0.0, scale=-1.0),
                            wait=[eL2])
                prev_bc = None
                eo = []
                for wi in range(NTT):
                    ma = P.pe(lambda e, wi=wi: e.matmul(
                        pb[7][0:64, :], self.ones_r[0:1, 0:64],
                        rbs(wi * 512, 512), start=True, stop=True),
                        wait=[eE1, eE2] + ([prev_bc] if prev_bc else []),
                        inc=True)
                    mb = P.pe(lambda e, wi=wi: e.matmul(
                        pb[7][64:128, :], self.ones_r[0:1, 0:64],
                        rbs(2048 + wi * 512, 512), start=True, stop=True,
                        tile_position=(0, 64)), inc=True)
                    ec = P.dve(lambda e: e.tensor_copy(self.sb512, pb[7]),
                               wait=[ma, mb] + ([eo[-1]] if eo else []))
                    prev_bc = ec
                    eo.append(P.pool(lambda e, wi=wi: e.tensor_tensor(
                        self.HC[:, wi * 512:(wi + 1) * 512],
                        self.HC[:, wi * 512:(wi + 1) * 512],
                        self.sb512, AL.mult), wait=[ec] + osb_ev))
                prev_osb = P.dma("sp", d["ybk"][hp], self.HC, wait=eo)
                prev_abuser = eo[-1]
                if hp == 3:
                    ev_opairs = prev_osb
            ob = self.U6.rearrange("p (c f) -> p c f", f=T)[:, 0:4, :]
            ev_ob = P.dma("sp", self.U6.rearrange("p (c f) -> p c f", f=T),
                          d["ybk"], wait=[prev_osb])
            wo_sb = self.U5[:, 0:4096].rearrange("p (c f) -> p c f", f=1024)
            ev_wos = P.dma("sp", wo_sb, d["wo_l"][l].rearrange(
                "(c p) o -> p c o", p=128), wait=[self.cast_evs["wo_l"],
                                                  prev_osb])
            ev_yup = None
            yup_bank = [None] * CH
            for tt in range(NTT):
                for oc in range(CH):
                    mw = None
                    for ic in range(4):
                        w = None
                        if ic == 0:
                            w = [ev_ob, ev_wos] + \
                                ([yup_bank[oc]] if yup_bank[oc] else [])
                        mw = P.pe(lambda e, oc=oc, ic=ic, tt=tt: e.matmul(
                            pb[oc], wo_sb[:, ic, oc * 128:(oc + 1) * 128],
                            ob[:, ic, tt * 512:(tt + 1) * 512],
                            start=(ic == 0), stop=(ic == 3)),
                            wait=w, inc=(ic == 3))
                    ev_yup = P.dve(lambda e, oc=oc, tt=tt: e.scalar_tensor_tensor(
                        self.U0[:, oc, tt * 512:(tt + 1) * 512], pb[oc], 1.0,
                        self.U0[:, oc, tt * 512:(tt + 1) * 512],
                        AL.mult, AL.add), wait=[mw])
                    yup_bank[oc] = ev_yup
            P.commit(f"f1a{s}_{l}")
            ehs = []
            for c in range(CH):
                w = [ev_yup] if c == 0 else None
                if c < 4:
                    ehs.append(P.pool(lambda e, c=c: e.tensor_copy(
                        hs[:, c, :], self.U0[:, c, :]), wait=w))
                else:
                    P.pool(lambda e, c=c: e.memset(hs[:, c, 0:1], 0.0), wait=w)
                    ehs.append(P.pool(lambda e, c=c: e.tensor_copy(
                        hs[:, c, 1:T], self.U0[:, c, 0:T - 1])))
            ev = self.ln_stats(hs, [ehs[-1]])
            pbA, pbB = self.pb[4], self.pb[5]
            eaf = None
            prev = None
            for tt in range(NTT):
                w = [ev] + ([prev] if prev else [])
                mA, mB = self.bcast2(
                    self.rb[0:1, tt * 512:(tt + 1) * 512],
                    self.rb[0:1, 2048 + tt * 512:2048 + (tt + 1) * 512],
                    pbA, pbB, wait=w)
                eBc = P.dve(lambda e: e.tensor_copy(self.sb512, pbB),
                            wait=[mB])
                for c in range(CH):
                    P.dve(lambda e, c=c, tt=tt: e.tensor_tensor(
                        self.sb512b, hs[:, c, tt * 512:(tt + 1) * 512], pbA,
                        AL.mult), wait=[mA, eBc])
                    t2 = P.dve(lambda e: e.tensor_tensor(
                        self.gb, self.sb512b, self.sb512, AL.add))
                    eaf = P.dve(lambda e, c=c, tt=tt, l=l: e.tensor_scalar(
                        hs[:, c, tt * 512:(tt + 1) * 512], self.gb,
                        self.lgam[:, l, c:c + 1], self.lbet[:, l, c:c + 1],
                        AL.mult, AL.add))
                prev = eaf
            P.commit(f"f1b{s}_{l}")
            h3 = self.U5[:, 0:11264].rearrange("p (k f) -> p k f", f=512)
            if True:
                pass
            wsl = self.U6[:, 0:4096].rearrange("p (b m c f) -> p b m c f",
                                               b=2, m=2, f=128)
            ev_w2done = None
            prev_sl = [None, None]
            for tt in range(NTT):
                evh3 = []
                for ocp in range(22):
                    wdt = 128 if ocp < 21 else 42
                    b = ocp % 2
                    acol = ocp * 128
                    gcol = 2730 + ocp * 128
                    w = [self.cast_evs["w1_l"]] + \
                        ([prev_sl[b]] if prev_sl[b] else []) + \
                        ([eaf] if (tt == 0 and ocp < 2) else [])
                    esl = P.dma_multi("sp", [
                        (wsl[:, b, 0, :, 0:wdt],
                         d["w1_l"][l][:, acol:acol + wdt].rearrange(
                             "(c p) o -> p c o", p=128)),
                        (wsl[:, b, 1, :, 0:wdt],
                         d["w1_l"][l][:, gcol:gcol + wdt].rearrange(
                             "(c p) o -> p c o", p=128)),
                    ], wait=w)
                    mA2 = mG = None
                    prev_eh3 = getattr(self, "_w1_prev_eh3", None)
                    prev_eg = getattr(self, "_w1_prev_eg", None)
                    w2ev = getattr(self, "_w2_evac", [None] * CH)
                    for ic in range(CH):
                        w0 = None
                        if ic == 0:
                            w0 = [esl] + ([prev_eh3] if prev_eh3 else []) + \
                                ([w2ev[0]] if (ocp < 2 and w2ev[0]) else [])
                        mA2 = P.pe(lambda e, b=b, ic=ic, tt=tt, wdt=wdt:
                                   e.matmul(
                            pb[0][0:wdt, :], wsl[:, b, 0, ic, 0:wdt],
                            hs[:, ic, tt * 512:(tt + 1) * 512],
                            start=(ic == 0), stop=(ic == CH - 1)),
                            wait=w0, inc=(ic == CH - 1))
                    for ic in range(CH):
                        w0 = None
                        if ic == 0:
                            w0 = ([prev_eg] if prev_eg else []) + \
                                ([w2ev[1]] if (ocp < 2 and w2ev[1]) else [])
                            w0 = w0 or None
                        mG = P.pe(lambda e, b=b, ic=ic, tt=tt, wdt=wdt:
                                  e.matmul(
                            pb[1][0:wdt, :], wsl[:, b, 1, ic, 0:wdt],
                            hs[:, ic, tt * 512:(tt + 1) * 512],
                            start=(ic == 0), stop=(ic == CH - 1)),
                            wait=w0, inc=(ic == CH - 1))
                    prev_sl[b] = mG
                    eg = P.act(lambda e, ocp=ocp, wdt=wdt, l=l: e.activation(
                        self.mkb[0:wdt, :], pb[1][0:wdt, :], AF.Gelu,
                        bias=self.b1c[0:wdt, l, 22 + ocp:23 + ocp],
                        scale=1.0), wait=[mG])
                    eh3 = P.dve(lambda e, ocp=ocp, wdt=wdt, l=l:
                                e.scalar_tensor_tensor(
                        h3[:, ocp, :][0:wdt, :], pb[0][0:wdt, :],
                        self.b1c[0:wdt, l, ocp:ocp + 1], self.mkb[0:wdt, :],
                        AL.add, AL.mult), wait=[mA2, eg])
                    self._w1_prev_eh3 = eh3
                    self._w1_prev_eg = eg
                    evh3.append(eh3)
                prev_w2sl = getattr(self, "_w2_prev_sl", [None] * 4)
                mws = []
                for kc in range(22):
                    wk = 128 if kc < 21 else 42
                    b2 = kc % 4
                    w = [self.cast_evs["w2_l"]] + \
                        ([prev_w2sl[b2]] if prev_w2sl[b2] else [])
                    es2 = P.dma("sp",
                                self.U6[0:wk, 4096 + b2 * 1024:
                                        4096 + b2 * 1024 + 1024],
                                d["w2_l"][l][kc * 128:kc * 128 + wk, :],
                                wait=w)
                    for oc in range(CH):
                        w0 = [es2, evh3[kc]] if oc == 0 else None
                        if kc == 0:
                            w0 = (w0 or []) + [evh3[-1]] + \
                                ([ev_w2done] if ev_w2done else [])
                        mw2 = P.pe(lambda e, kc=kc, oc=oc, b2=b2, wk=wk:
                                   e.matmul(
                            pb[oc],
                            self.U6[0:wk, 4096 + b2 * 1024 + oc * 128:
                                    4096 + b2 * 1024 + (oc + 1) * 128],
                            h3[0:wk, kc, :], start=(kc == 0),
                            stop=(kc == 21)), wait=w0,
                            inc=(kc == 21 or oc == 7))
                        if kc == 21:
                            mws.append(mw2)
                    prev_w2sl[b2] = mw2
                self._w2_prev_sl = prev_w2sl
                if not hasattr(self, "_w2_evac"):
                    self._w2_evac = [None] * CH
                for oc in range(CH):
                    ev_w2done = P.dve(lambda e, oc=oc, tt=tt, l=l:
                                      e.scalar_tensor_tensor(
                        self.U0[:, oc, tt * 512:(tt + 1) * 512], pb[oc],
                        self.b2c[:, l, oc:oc + 1],
                        self.U0[:, oc, tt * 512:(tt + 1) * 512],
                        AL.add, AL.add), wait=[mws[oc]])
                    self._w2_evac[oc] = ev_w2done
            prev_done = ev_w2done
            P.commit(f"f1c{s}_{l}")
        return prev_done

    def s_final(self, s, ev_ffn):
        P = self.P
        d = self.d
        pb = self.pb
        s1t = self.U4[:, 0:4096].rearrange("p (c f) -> p c f", f=512)
        sgf = self.u5f[:, 0:4096].rearrange("p (c f) -> p c f", f=512)
        xTf = self.u5f[:, 4096:8192].rearrange("p (sub f) -> p sub f", f=1024)
        ev_out = None
        prev_tr = None
        prev_sg = None
        prev_s1 = None
        for TT in range(NTT):
            es1 = P.dma("sp", s1t, d["s1"].rearrange(
                "c p t -> p c t")[:, :, TT * 512:(TT + 1) * 512],
                wait=[ev_ffn] + ([prev_s1] if prev_s1 else []))
            prev_s1 = es1
            ex = P.dma("sp", xTf,
                       d["x1T"][s][TT * 512:(TT + 1) * 512, :].rearrange(
                           "(sub p) e -> p sub e", p=128),
                       wait=([prev_tr] if prev_tr else []) +
                       ([ev_out] if ev_out else []))
            trs = []
            for c in range(CH):
                eg1 = P.act(lambda e, c=c, TT=TT: e.activation(
                    self.mkb, self.U0[:, c, TT * 512:(TT + 1) * 512],
                    AF.Silu, bias=0.0, scale=1.0),
                    wait=[ev_ffn] + ([prev_sg] if prev_sg else []))
                e1 = P.dve(lambda e, c=c: e.tensor_tensor(
                    sgf[:, c, :], s1t[:, c, :], self.mkb, AL.mult),
                    wait=[eg1, es1] + ([prev_tr] if prev_tr else []))
                prev_sg = e1
                tr = None
                for sub in range(4):
                    mt = P.pe(lambda e, c=c, sub=sub: e.transpose(
                        pb[c][:, 0:128], sgf[:, c, sub * 128:(sub + 1) * 128],
                        self.eyef), wait=[e1, ex] + ([tr] if tr else []),
                        inc=True)
                    tr = P.dve(lambda e, c=c, sub=sub: e.tensor_tensor(
                        xTf[:, sub, c * 128:(c + 1) * 128],
                        xTf[:, sub, c * 128:(c + 1) * 128],
                        pb[c][:, 0:128], AL.add), wait=[mt])
                    trs.append(tr)
            prev_tr = trs[-1]
            ev_out = P.dma(
                "sp", self.o_out[s][TT * 512:(TT + 1) * 512, :].rearrange(
                    "(sub p) e -> p sub e", p=128), xTf, wait=trs)
            self.out_evs.append(ev_out)
            P.commit(f"fc{s}_{TT}")
        return ev_out

    def finish(self):
        P = self.P
        P.op("pool", lambda e: e.memset(self.fint, 0.0), wait=self.out_evs)
        P.commit("fin")
        return self.nc


def build_l0(dbg=(), nseq=1):
    k = K(dbg=dbg)
    k.prelude()
    d = k.d
    for s in range(nseq):
        ev_xn = k.s_stats_xn(s)
        ev_g0 = k.s_moe(s, ev_xn)
        ev_q, ev_k, ev_v = k.s_proj(
            s, k.U0, ev_xn,
            lambda oc: d["wqT"][:, oc * 128:(oc + 1) * 128],
            lambda oc: d["wkT"][:, oc * 128:(oc + 1) * 128],
            d["wvT"], d["qs"], d["ks"], d["vs"], "L0")
        ev_x1 = k.s_attn0(s, ev_q, ev_k, ev_v, ev_g0, ev_xn)
        k.s_ln1_y(s, ev_x1)
    return k.finish(), k


def build_full(dbg=(), nseq=NSEQ):
    k = K(dbg=dbg)
    k.prelude()
    d = k.d
    for s in range(nseq):
        ev_xn = k.s_stats_xn(s)
        ev_g0 = k.s_moe(s, ev_xn)
        ev_q, ev_k, ev_v = k.s_proj(
            s, k.U0, ev_xn,
            lambda oc: d["wqT"][:, oc * 128:(oc + 1) * 128],
            lambda oc: d["wkT"][:, oc * 128:(oc + 1) * 128],
            d["wvT"], d["qs"], d["ks"], d["vs"], "L0")
        ev_x1 = k.s_attn0(s, ev_q, ev_k, ev_v, ev_g0, ev_xn)
        ev_y = k.s_ln1_y(s, ev_x1)
        ev_q1, ev_k1, ev_v1 = k.s_proj(
            s, k.U0, ev_y,
            lambda oc: d["wqkv1T"][:, oc * 128:(oc + 1) * 128],
            lambda oc: d["wqkv1T"][:, 1024 + oc * 128:1024 + (oc + 1) * 128],
            d["wqkv1T"][:, 2048:3072], d["q1"], d["k1"], d["v1"], "L1")
        ev_s1 = k.s_attn1(s, ev_q1, ev_k1, ev_v1)
        ev_ffn = k.s_ffn1(s, ev_y, ev_s1)
        k.s_final(s, ev_ffn)
    return k.finish(), k


_CACHE = {}


def kernel(**inputs):
    x = np.ascontiguousarray(inputs["L0_attn_x"])
    n_cores = 8
    if "nc" not in _CACHE:
        nc, k = build_full()
        _CACHE["nc"] = nc
    nc = _CACHE["nc"]
    consts = host_consts()
    base = {n: np.ascontiguousarray(v) for n, v in inputs.items()
            if n != "L0_attn_x"}
    base.update(consts)
    in_maps = []
    for c in range(n_cores):
        m = dict(base)
        m["L0_attn_x"] = x[2 * c:2 * c + 2]
        in_maps.append(m)
    res = run_bass_kernel_spmd(nc, in_maps, core_ids=list(range(n_cores)))
    out = np.concatenate([np.asarray(r["OUT"], dtype=np.float32)
                          for r in res.results], axis=0)
    return out
